# version 02
# baseline (speedup 1.0000x reference)
import numpy as np
import ml_dtypes
from contextlib import ExitStack

import concourse.bass as bass
import concourse.mybir as mybir
from concourse.bass_utils import run_bass_kernel_spmd

F32 = mybir.dt.float32
BF16 = mybir.dt.bfloat16
AF = mybir.ActivationFunctionType
ALU = mybir.AluOpType
AXX = mybir.AxisListType.X

D = 2048
NRT = 16
TT = 512
KT = 16
IC = 11
NE = 18
IS = 1408

PC_BY, PC_BX, PC_BIG, PC_BAG, PC_BOUT, PC_CB = 0, 16, 32, 48, 64, 80
PC_CW = 96
PC_AP = 160
PC_N = 176


class Ctr:
    def __init__(self, es, nc, name):
        self.h = es.enter_context(nc.semaphore(name))
        self.v = 0
        self.name = name


def _key(ap):
    if isinstance(ap, (str, tuple)):
        return ap
    t = getattr(ap, "tensor", None)
    return t.name if t is not None else str(ap)


class Q:
    def __init__(self, name, prog):
        self.name = name
        self.prog = prog
        self.waited = {}
        self.pending = []

    def wait(self, ctr, val):
        if val <= 0:
            return
        if self.waited.get(id(ctr), -1) >= val:
            return
        self.pending.append(lambda e, h=ctr.h, v=val: e.wait_ge(h, v))
        self.waited[id(ctr)] = val

    def raw(self, fn):
        self.pending.append(fn)

    def replay(self, eng):
        for fn in self.pending:
            fn(eng)
        self.pending = []


class Trk:
    def __init__(self):
        self.writer = {}
        self.readers = {}
        self.dma_sems = set()
        self.dma_q = {}

    def _pre(self, q, reads, writes):
        for ap in reads:
            w = self.writer.get(_key(ap))
            if w:
                q.wait(w[0], w[1])
        for ap in writes:
            k = _key(ap)
            w = self.writer.get(k)
            if w:
                q.wait(w[0], w[1])
            for (ctr, val) in self.readers.get(k, {}).values():
                q.wait(ctr, val)

    def _post(self, q, reads, writes, ctr, val):
        for ap in writes:
            k = _key(ap)
            self.writer[k] = (ctr, val)
            self.readers[k] = {}
        for ap in reads:
            self.readers.setdefault(_key(ap), {})[q.name] = (ctr, val)

    def op(self, q, fn, reads=(), writes=(), waits=()):
        self._pre(q, reads, writes)
        for (ctr, val) in waits:
            q.wait(ctr, val)
        q.prog.v += 1
        sh, sv = q.prog.h, q.prog.v
        q.raw(lambda e: fn(e).then_inc(sh, 1))
        self._post(q, reads, writes, q.prog, q.prog.v)

    def dma(self, q, sem, fn, reads=(), writes=(), waits=(), first=True):
        self.dma_sems.add(sem)
        self.dma_q[sem] = q
        if first and sem.v > 0:
            q.wait(sem, sem.v)
        self._pre(q, reads, writes)
        for (ctr, val) in waits:
            q.wait(ctr, val)
        sem.v += 16
        sh = sem.h
        q.raw(lambda e: fn(e).then_inc(sh, 16))
        self._post(q, reads, writes, sem, sem.v)


def build_program(nt_real=8, phases=("0", "A", "B", "C", "D0", "D", "E"), dbg=()):
    nwt = nt_real + 1
    rows = nwt * TT
    nc = bass.Bass("TRN2", target_bir_lowering=False)

    _dram = {}

    def inp(name, shape):
        if name not in _dram:
            _dram[name] = nc.dram_tensor(name, shape, F32, kind="ExternalInput").ap()
        return _dram[name]

    def scratch(name, shape, dt):
        if name not in _dram:
            kind = "ExternalOutput" if name in dbg else "Internal"
            _dram[name] = nc.dram_tensor(name, shape, dt, kind=kind).ap()
        return _dram[name]

    xs = inp("xs", [rows, D])
    flags = inp("flags", [128, 2])
    ppack = inp("ppack", [128, PC_N])

    es = ExitStack()
    pp = es.enter_context(nc.sbuf_tensor("pp", [128, PC_N], F32))
    fl = es.enter_context(nc.sbuf_tensor("fl", [128, 2], F32))
    sl = es.enter_context(nc.sbuf_tensor("sl", [128, 48], F32))
    nbias = es.enter_context(nc.sbuf_tensor("nbias", [128, 32], F32))
    cst = es.enter_context(nc.sbuf_tensor("cst", [128, 8], F32))
    ident = es.enter_context(nc.sbuf_tensor("ident", [128, 128], F32))
    ones_col_bf = es.enter_context(nc.sbuf_tensor("ones_col_bf", [128, 1], BF16))
    ones_row = es.enter_context(nc.sbuf_tensor("ones_row", [1, 128], F32))
    C_ONE, C_EPS, C_INVD, C_MONE, C_TINY = 0, 1, 2, 3, 4

    trk = Trk()
    PB = Ctr(es, nc, "PB")
    NENG = 5
    TEP = Ctr(es, nc, "TEP")
    VEP = Ctr(es, nc, "VEP")
    SEP = Ctr(es, nc, "SEP")
    GPP = Ctr(es, nc, "GPP")
    qT = Q("T", TEP)
    qV = Q("V", VEP)
    qS = Q("S", SEP)
    qG = Q("G", GPP)
    qY = Q("Y", None)
    ALLQ = [qT, qV, qS, qG, qY]

    def flush_phase():
        for sem, q in trk.dma_q.items():
            q.wait(sem, sem.v)
        for q in ALLQ:
            q.raw(lambda e: e.sem_inc(PB.h, 1))
        PB.v += NENG
        with nc.Block() as blk:
            @blk.tensor
            def _(e):
                qT.replay(e)

            @blk.vector
            def _(e):
                qV.replay(e)

            @blk.scalar
            def _(e):
                qS.replay(e)

            @blk.gpsimd
            def _(e):
                qG.replay(e)

            @blk.sync
            def _(e):
                qY.replay(e)

    def phase_begin():
        pbw = PB.v
        if pbw > 0:
            for q in ALLQ:
                q.raw(lambda e, v=pbw: e.wait_ge(PB.h, v))

    L0X = [Ctr(es, nc, f"L0X{i}") for i in range(2)]
    S0X = [Ctr(es, nc, f"S0X{i}") for i in range(2)]
    LPP = Ctr(es, nc, "LPP")
    LFL = Ctr(es, nc, "LFL")
    LWA = [Ctr(es, nc, f"LWA{i}") for i in range(2)]
    LXT = [Ctr(es, nc, f"LXT{i}") for i in range(3)]
    SYY = [Ctr(es, nc, f"SYY{i}") for i in range(2)]
    SCV = [Ctr(es, nc, f"SCV{i}") for i in range(2)]

    n0 = rows // 128

    if "0" in phases:
        xn1 = scratch("xn1", [rows, D], BF16)
        phase_begin()
        p0 = ExitStack()
        xld = [p0.enter_context(nc.sbuf_tensor(f"xld{i}", [128, D], F32)) for i in range(2)]
        sqs = [p0.enter_context(nc.sbuf_tensor(f"sqs{i}", [128, D], F32)) for i in range(2)]
        xnb = [p0.enter_context(nc.sbuf_tensor(f"xnb{i}", [128, D], BF16)) for i in range(2)]
        st0 = [p0.enter_context(nc.sbuf_tensor(f"st0{i}", [128, 8], F32)) for i in range(2)]

        trk.dma(qY, LPP, lambda e: e.dma_start(out=pp[:], in_=ppack[:]),
                reads=[ppack], writes=[pp])
        trk.dma(qY, LFL, lambda e: e.dma_start(out=fl[:], in_=flags[:]),
                reads=[flags], writes=[fl])
        trk.op(qV, lambda e: e.memset(ident[:], 0.0), writes=[ident])
        trk.op(qG, lambda e: e.affine_select(
            out=ident[:], in_=ident[:], compare_op=ALU.not_equal,
            fill=1.0, base=0, pattern=[[-1, 128]], channel_multiplier=1),
            reads=[ident], writes=[ident])
        trk.op(qV, lambda e: e.memset(ones_col_bf[:], 1.0), writes=[ones_col_bf])
        trk.op(qV, lambda e: e.memset(ones_row[:], 1.0), writes=[ones_row])
        for col, val in ((C_ONE, 1.0), (C_EPS, 1e-5), (C_INVD, 1.0 / D),
                         (C_MONE, -1.0), (C_TINY, 1e-20)):
            trk.op(qV, lambda e, col=col, val=val: e.memset(
                cst[:, col:col + 1], val), writes=[cst])
        trk.op(qV, lambda e: e.tensor_scalar(
            out=nbias[:, 0:16], in0=pp[:, PC_BIG:PC_BIG + 16],
            scalar1=-1.0, scalar2=None, op0=ALU.mult), reads=[pp], writes=[nbias])
        trk.op(qV, lambda e: e.tensor_scalar(
            out=nbias[:, 16:32], in0=pp[:, PC_BAG:PC_BAG + 16],
            scalar1=-1.0, scalar2=None, op0=ALU.mult), reads=[pp], writes=[nbias])
        trk.op(qS, lambda e: e.activation(
            out=sl[:, 16:32], in_=pp[:, PC_AP:PC_AP + 16], func=AF.Abs),
            reads=[pp], writes=[sl])
        trk.op(qS, lambda e: e.activation(
            out=sl[:, 16:32], in_=sl[:, 16:32], func=AF.Exp, scale=-1.0),
            reads=[sl], writes=[sl])
        trk.op(qS, lambda e: e.activation(
            out=sl[:, 16:32], in_=sl[:, 16:32], func=AF.Ln,
            bias=cst[:, C_ONE:C_ONE + 1]), reads=[sl, cst], writes=[sl])
        trk.op(qV, lambda e: e.tensor_scalar(
            out=sl[:, 0:16], in0=pp[:, PC_AP:PC_AP + 16],
            scalar1=0.0, scalar2=None, op0=ALU.max), reads=[pp, sl], writes=[sl])
        trk.op(qV, lambda e: e.tensor_add(
            out=sl[:, 32:48], in0=sl[:, 0:16], in1=sl[:, 16:32]),
            reads=[sl], writes=[sl])
        trk.op(qV, lambda e: e.tensor_scalar(
            out=sl[:, 0:16], in0=sl[:, 32:48],
            scalar1=-8.0, scalar2=None, op0=ALU.mult), reads=[sl], writes=[sl])
        trk.op(qV, lambda e: e.tensor_scalar(
            out=sl[:, 16:32], in0=sl[:, 32:48],
            scalar1=-16.0, scalar2=None, op0=ALU.mult), reads=[sl], writes=[sl])
        for i in range(n0):
            p = i % 2
            trk.dma(qY, L0X[p], lambda e, i=i, p=p: e.dma_start(
                out=xld[p][:], in_=xs[128 * i:128 * (i + 1), :]), writes=[xld[p]])
            trk.op(qS, lambda e, p=p: e.activation(
                out=sqs[p][:], in_=xld[p][:], func=AF.Square,
                accum_out=st0[p][:, 1:2]),
                reads=[xld[p]], writes=[sqs[p], st0[p]])
            trk.op(qV, lambda e, p=p: e.reduce_sum(
                out=st0[p][:, 0:1], in_=xld[p][:], axis=AXX),
                reads=[xld[p]], writes=[st0[p]])
            trk.op(qV, lambda e, p=p: e.tensor_scalar(
                out=st0[p][:, 2:3], in0=st0[p][:, 0:1],
                scalar1=1.0 / D, scalar2=None, op0=ALU.mult),
                reads=[st0[p]], writes=[st0[p]])
            trk.op(qV, lambda e, p=p: e.tensor_scalar(
                out=st0[p][:, 3:4], in0=st0[p][:, 1:2],
                scalar1=1.0 / D, scalar2=None, op0=ALU.mult),
                reads=[st0[p]], writes=[st0[p]])
            trk.op(qV, lambda e, p=p: e.tensor_tensor(
                out=st0[p][:, 4:5], in0=st0[p][:, 2:3], in1=st0[p][:, 2:3],
                op=ALU.mult), reads=[st0[p]], writes=[st0[p]])
            trk.op(qV, lambda e, p=p: e.tensor_sub(
                out=st0[p][:, 3:4], in0=st0[p][:, 3:4], in1=st0[p][:, 4:5]),
                reads=[st0[p]], writes=[st0[p]])
            trk.op(qV, lambda e, p=p: e.tensor_scalar(
                out=st0[p][:, 3:4], in0=st0[p][:, 3:4],
                scalar1=1e-5, scalar2=None, op0=ALU.add),
                reads=[st0[p]], writes=[st0[p]])
            trk.op(qS, lambda e, p=p: e.activation(
                out=st0[p][:, 4:5], in_=st0[p][:, 3:4], func=AF.Ln),
                reads=[st0[p]], writes=[st0[p]])
            trk.op(qS, lambda e, p=p: e.activation(
                out=st0[p][:, 5:6], in_=st0[p][:, 4:5], func=AF.Exp, scale=-0.5),
                reads=[st0[p]], writes=[st0[p]])
            trk.op(qV, lambda e, p=p: e.tensor_scalar(
                out=xnb[p][:], in0=xld[p][:],
                scalar1=st0[p][:, 2:3], scalar2=st0[p][:, 5:6],
                op0=ALU.subtract, op1=ALU.mult),
                reads=[xld[p], st0[p]], writes=[xnb[p]])
            trk.dma(qY, S0X[p], lambda e, i=i, p=p: e.dma_start(
                out=xn1[128 * i:128 * (i + 1), :], in_=xnb[p][:]),
                reads=[xnb[p]], writes=[("xn1", i)])
        qY.wait(S0X[0], S0X[0].v)
        qY.wait(S0X[1], S0X[1].v)
        flush_phase()
        p0.close()

    if "A" in phases:
        wy = inp("wy", [D, D])
        wx = inp("wx", [D, D])
        xn1 = scratch("xn1", [rows, D], BF16)
        convd = scratch("convd", [nwt, NRT, 128, TT], F32)
        yd = scratch("yd", [nt_real, NRT, 128, TT], F32)
        phase_begin()
        pa = ExitStack()
        wy_sb = pa.enter_context(nc.sbuf_tensor("wy_sb", [128, KT * D], BF16))
        wx_sb = pa.enter_context(nc.sbuf_tensor("wx_sb", [128, KT * D], BF16))
        xnt = [pa.enter_context(nc.sbuf_tensor(f"xnt{i}", [128, KT * TT], BF16))
               for i in range(3)]
        ybuf = [pa.enter_context(nc.sbuf_tensor(f"ybuf{i}", [128, TT], F32))
                for i in range(2)]
        xbx = [pa.enter_context(nc.sbuf_tensor(f"xbx{i}", [128, TT + 3], F32))
               for i in range(2)]
        cv1 = [pa.enter_context(nc.sbuf_tensor(f"cv1{i}", [128, TT], F32))
               for i in range(2)]
        cv2 = [pa.enter_context(nc.sbuf_tensor(f"cv2{i}", [128, TT], F32))
               for i in range(2)]
        carr = pa.enter_context(nc.sbuf_tensor("carr", [128, NRT * 3], F32))
        psY = [pa.enter_context(nc.psum_tensor(f"psY{i}", [128, TT], F32))
               for i in range(2)]
        psX = [pa.enter_context(nc.psum_tensor(f"psX{i}", [128, TT], F32))
               for i in range(2)]

        trk.dma(qG, LWA[0], lambda e: e.dma_start(
            out=wy_sb[:].rearrange("p (kt m) -> p kt m", kt=KT),
            in_=wy.rearrange("(kt p) m -> p kt m", p=128)),
            reads=[wy], writes=[wy_sb])
        trk.dma(qG, LWA[1], lambda e: e.dma_start(
            out=wx_sb[:].rearrange("p (kt m) -> p kt m", kt=KT),
            in_=wx.rearrange("(kt p) m -> p kt m", p=128)),
            reads=[wx], writes=[wx_sb])
        trk.op(qV, lambda e: e.memset(carr[:], 0.0), writes=[carr])

        def mmgroup(ps, wsb, rt, src, n=KT, w=TT):
            def fn(e):
                mm = None
                for kt in range(n):
                    mm = e.matmul(ps[:],
                                  wsb[:, kt * D + rt * 128:kt * D + rt * 128 + 128],
                                  src[:, kt * w:(kt + 1) * w],
                                  start=(kt == 0), stop=(kt == n - 1))
                return mm
            return fn

        for t in range(nwt):
            p3 = t % 3
            for rt in range(NRT):
                trk.dma(qY, LXT[p3], lambda e, t=t, rt=rt, p3=p3: e.dma_start_transpose(
                    out=xnt[p3][:, rt * TT:(rt + 1) * TT],
                    in_=xn1[TT * t:TT * (t + 1), 128 * rt:128 * (rt + 1)]),
                    reads=[("xn1", j) for j in range(4 * t, 4 * t + 4)],
                    writes=[xnt[p3]], first=(rt == 0))
            for rt in range(NRT):
                b = rt % 2
                if t >= 1:
                    trk.op(qT, mmgroup(psY[b], wy_sb, rt, xnt[p3]),
                           reads=[wy_sb, xnt[p3]], writes=[psY[b]])
                    trk.op(qS, lambda e, rt=rt, b=b: e.activation(
                        out=ybuf[b][:], in_=psY[b][:], func=AF.Gelu,
                        bias=pp[:, PC_BY + rt:PC_BY + rt + 1]),
                        reads=[psY[b], pp], writes=[ybuf[b]])
                    trk.dma(qY, SYY[b], lambda e, t=t, rt=rt, b=b: e.dma_start(
                        out=yd[t - 1, rt], in_=ybuf[b][:]), reads=[ybuf[b]],
                        writes=[("yd", t - 1, rt)])
                trk.op(qT, mmgroup(psX[b], wx_sb, rt, xnt[p3]),
                       reads=[wx_sb, xnt[p3]], writes=[psX[b]])
                trk.op(qV, lambda e, rt=rt, b=b: e.tensor_copy(
                    out=xbx[b][:, 0:3], in_=carr[:, 3 * rt:3 * rt + 3]),
                    reads=[carr], writes=[xbx[b]])
                trk.op(qS, lambda e, rt=rt, b=b: e.activation(
                    out=xbx[b][:, 3:TT + 3], in_=psX[b][:], func=AF.Identity,
                    bias=pp[:, PC_BX + rt:PC_BX + rt + 1]),
                    reads=[psX[b], pp], writes=[xbx[b]])
                cwc = PC_CW + 4 * rt
                trk.op(qV, lambda e, b=b, cwc=cwc: e.tensor_scalar(
                    out=cv1[b][:], in0=xbx[b][:, 0:TT],
                    scalar1=pp[:, cwc:cwc + 1], scalar2=None, op0=ALU.mult),
                    reads=[xbx[b], pp], writes=[cv1[b]])
                trk.op(qV, lambda e, b=b, cwc=cwc: e.scalar_tensor_tensor(
                    out=cv2[b][:], in0=xbx[b][:, 1:TT + 1],
                    scalar=pp[:, cwc + 1:cwc + 2], in1=cv1[b][:],
                    op0=ALU.mult, op1=ALU.add),
                    reads=[xbx[b], pp, cv1[b]], writes=[cv2[b]])
                trk.op(qV, lambda e, b=b, cwc=cwc: e.scalar_tensor_tensor(
                    out=cv1[b][:], in0=xbx[b][:, 2:TT + 2],
                    scalar=pp[:, cwc + 2:cwc + 3], in1=cv2[b][:],
                    op0=ALU.mult, op1=ALU.add),
                    reads=[xbx[b], pp, cv2[b]], writes=[cv1[b]])
                trk.op(qV, lambda e, b=b, cwc=cwc: e.scalar_tensor_tensor(
                    out=cv2[b][:], in0=xbx[b][:, 3:TT + 3],
                    scalar=pp[:, cwc + 3:cwc + 4], in1=cv1[b][:],
                    op0=ALU.mult, op1=ALU.add),
                    reads=[xbx[b], pp, cv1[b]], writes=[cv2[b]])
                trk.op(qV, lambda e, b=b, rt=rt: e.tensor_scalar(
                    out=cv2[b][:], in0=cv2[b][:],
                    scalar1=pp[:, PC_CB + rt:PC_CB + rt + 1],
                    scalar2=None, op0=ALU.add),
                    reads=[cv2[b], pp], writes=[cv2[b]])
                trk.op(qV, lambda e, b=b, rt=rt: e.tensor_copy(
                    out=carr[:, 3 * rt:3 * rt + 3], in_=xbx[b][:, TT:TT + 3]),
                    reads=[xbx[b]], writes=[carr])
                trk.dma(qG, SCV[b], lambda e, t=t, rt=rt, b=b: e.dma_start(
                    out=convd[t, rt], in_=cv2[b][:]), reads=[cv2[b]],
                    writes=[("convd", t, rt)])
        qY.wait(SYY[0], SYY[0].v)
        qY.wait(SYY[1], SYY[1].v)
        qG.wait(SCV[0], SCV[0].v)
        qG.wait(SCV[1], SCV[1].v)
        flush_phase()
        pa.close()

    LCB = [Ctr(es, nc, f"LCB{i}") for i in range(2)]
    LCF = [Ctr(es, nc, f"LCF{i}") for i in range(2)]
    SHH = [Ctr(es, nc, f"SHH{i}") for i in range(2)]
    LWB = [Ctr(es, nc, f"LWB{i}") for i in range(2)]

    if "B" in phases:
        wig = inp("wig", [D, D])
        wag = inp("wag", [D, D])
        convd = scratch("convd", [nwt, NRT, 128, TT], F32)
        hd = scratch("hd", [nt_real, NRT, 128, TT], F32)
        phase_begin()
        pb_ = ExitStack()
        wig_sb = pb_.enter_context(nc.sbuf_tensor("wig_sb", [128, KT * D], BF16))
        wag_sb = pb_.enter_context(nc.sbuf_tensor("wag_sb", [128, KT * D], BF16))
        cbf = [pb_.enter_context(nc.sbuf_tensor(f"cbf{i}", [128, KT * TT], BF16))
               for i in range(2)]
        cf32 = [pb_.enter_context(nc.sbuf_tensor(f"cf32{i}", [128, TT], F32))
                for i in range(2)]
        g1 = [pb_.enter_context(nc.sbuf_tensor(f"g1_{i}", [128, TT], F32))
              for i in range(2)]
        g2 = [pb_.enter_context(nc.sbuf_tensor(f"g2_{i}", [128, TT], F32))
              for i in range(2)]
        abuf = [pb_.enter_context(nc.sbuf_tensor(f"abuf{i}", [128, TT], F32))
                for i in range(2)]
        m2b = [pb_.enter_context(nc.sbuf_tensor(f"m2b{i}", [128, TT], F32))
               for i in range(2)]
        nx1 = [pb_.enter_context(nc.sbuf_tensor(f"nx1_{i}", [128, TT], F32))
               for i in range(2)]
        nxb = [pb_.enter_context(nc.sbuf_tensor(f"nxb{i}", [128, TT], F32))
               for i in range(2)]
        hbf = [pb_.enter_context(nc.sbuf_tensor(f"hbf{i}", [128, TT], F32))
               for i in range(2)]
        hcar = pb_.enter_context(nc.sbuf_tensor("hcar", [128, NRT], F32))
        bsc = pb_.enter_context(nc.sbuf_tensor("bsc", [128, 4], F32))
        psA = [pb_.enter_context(nc.psum_tensor(f"psA{i}", [128, TT], F32))
               for i in range(2)]
        psB = [pb_.enter_context(nc.psum_tensor(f"psB{i}", [128, TT], F32))
               for i in range(2)]

        trk.dma(qG, LWB[0], lambda e: e.dma_start(
            out=wig_sb[:].rearrange("p (kt m) -> p kt m", kt=KT),
            in_=wig.rearrange("(kt p) m -> p kt m", p=128)),
            reads=[wig], writes=[wig_sb])
        trk.dma(qG, LWB[1], lambda e: e.dma_start(
            out=wag_sb[:].rearrange("p (kt m) -> p kt m", kt=KT),
            in_=wag.rearrange("(kt p) m -> p kt m", p=128)),
            reads=[wag], writes=[wag_sb])

        def mmg(ps, wsb, rt, src):
            def fn(e):
                mm = None
                for kt in range(KT):
                    mm = e.matmul(ps[:],
                                  wsb[:, kt * D + rt * 128:kt * D + rt * 128 + 128],
                                  src[:, kt * TT:(kt + 1) * TT],
                                  start=(kt == 0), stop=(kt == KT - 1))
                return mm
            return fn

        for t in range(nwt):
            p2 = t % 2
            trk.dma(qG, LCB[p2], lambda e, t=t, p2=p2: e.dma_start(
                out=cbf[p2][:].rearrange("p (rt n) -> p rt n", rt=NRT),
                in_=convd[t].rearrange("rt p n -> p rt n")),
                reads=[("convd", t, r_) for r_ in range(NRT)], writes=[cbf[p2]])
            for rt in range(NRT):
                b = rt % 2
                trk.dma(qY, LCF[b], lambda e, t=t, rt=rt, b=b: e.dma_start(
                    out=cf32[b][:], in_=convd[t, rt]),
                    reads=[("convd", t, rt)], writes=[cf32[b]])
                trk.op(qT, mmg(psA[b], wag_sb, rt, cbf[p2]),
                       reads=[wag_sb, cbf[p2]], writes=[psA[b]])
                trk.op(qS, lambda e, rt=rt, b=b: e.activation(
                    out=g1[b][:], in_=psA[b][:], func=AF.Exp, scale=-1.0,
                    bias=nbias[:, 16 + rt:16 + rt + 1]),
                    reads=[psA[b], nbias], writes=[g1[b]])
                trk.op(qV, lambda e, b=b: e.tensor_scalar(
                    out=g1[b][:], in0=g1[b][:], scalar1=1.0, scalar2=None,
                    op0=ALU.add), reads=[g1[b]], writes=[g1[b]])
                trk.op(qV, lambda e, b=b: e.reciprocal(
                    out=g1[b][:], in_=g1[b][:]), reads=[g1[b]], writes=[g1[b]])
                trk.op(qS, lambda e, rt=rt, b=b: e.activation(
                    out=abuf[b][:], in_=g1[b][:], func=AF.Exp,
                    scale=sl[:, rt:rt + 1]),
                    reads=[g1[b], sl], writes=[abuf[b]])
                trk.op(qS, lambda e, rt=rt, b=b: e.activation(
                    out=m2b[b][:], in_=g1[b][:], func=AF.Exp,
                    scale=sl[:, 16 + rt:16 + rt + 1]),
                    reads=[g1[b], sl], writes=[m2b[b]])
                trk.op(qV, lambda e, b=b: e.tensor_scalar(
                    out=m2b[b][:], in0=m2b[b][:], scalar1=-1.0, scalar2=1.0,
                    op0=ALU.mult, op1=ALU.add), reads=[m2b[b]], writes=[m2b[b]])
                trk.op(qS, lambda e, b=b: e.activation(
                    out=m2b[b][:], in_=m2b[b][:], func=AF.Ln),
                    reads=[m2b[b]], writes=[m2b[b]])
                trk.op(qS, lambda e, b=b: e.activation(
                    out=m2b[b][:], in_=m2b[b][:], func=AF.Exp, scale=0.5),
                    reads=[m2b[b]], writes=[m2b[b]])
                trk.op(qT, mmg(psB[b], wig_sb, rt, cbf[p2]),
                       reads=[wig_sb, cbf[p2]], writes=[psB[b]])
                trk.op(qS, lambda e, rt=rt, b=b: e.activation(
                    out=g2[b][:], in_=psB[b][:], func=AF.Exp, scale=-1.0,
                    bias=nbias[:, rt:rt + 1]),
                    reads=[psB[b], nbias], writes=[g2[b]])
                trk.op(qV, lambda e, b=b: e.tensor_scalar(
                    out=g2[b][:], in0=g2[b][:], scalar1=1.0, scalar2=None,
                    op0=ALU.add), reads=[g2[b]], writes=[g2[b]])
                trk.op(qV, lambda e, b=b: e.reciprocal(
                    out=g2[b][:], in_=g2[b][:]), reads=[g2[b]], writes=[g2[b]])
                trk.op(qV, lambda e, b=b: e.tensor_mul(
                    out=nx1[b][:], in0=g2[b][:], in1=cf32[b][:]),
                    reads=[g2[b], cf32[b]], writes=[nx1[b]])
                trk.op(qV, lambda e, b=b: e.tensor_mul(
                    out=nxb[b][:], in0=nx1[b][:], in1=m2b[b][:]),
                    reads=[nx1[b], m2b[b]], writes=[nxb[b]])
                if t == 1:
                    trk.op(qV, lambda e, b=b: e.tensor_scalar(
                        out=abuf[b][:, 0:1], in0=abuf[b][:, 0:1],
                        scalar1=fl[:, 0:1], scalar2=None, op0=ALU.mult),
                        reads=[abuf[b], fl], writes=[abuf[b]])
                    trk.op(qV, lambda e, b=b: e.tensor_scalar(
                        out=bsc[:, 0:1], in0=nx1[b][:, 0:1],
                        scalar1=fl[:, 1:2], scalar2=None, op0=ALU.mult),
                        reads=[nx1[b], fl], writes=[bsc])
                    trk.op(qV, lambda e, b=b: e.scalar_tensor_tensor(
                        out=nxb[b][:, 0:1], in0=nxb[b][:, 0:1],
                        scalar=fl[:, 0:1], in1=bsc[:, 0:1],
                        op0=ALU.mult, op1=ALU.add),
                        reads=[nxb[b], fl, bsc], writes=[nxb[b]])
                if t == 0:
                    trk.op(qV, lambda e, b=b: e.tensor_tensor_scan(
                        out=hbf[b][:], data0=abuf[b][:], data1=nxb[b][:],
                        initial=0.0, op0=ALU.mult, op1=ALU.add),
                        reads=[abuf[b], nxb[b]], writes=[hbf[b]])
                else:
                    trk.op(qV, lambda e, b=b, rt=rt: e.tensor_tensor_scan(
                        out=hbf[b][:], data0=abuf[b][:], data1=nxb[b][:],
                        initial=hcar[:, rt:rt + 1], op0=ALU.mult, op1=ALU.add),
                        reads=[abuf[b], nxb[b], hcar], writes=[hbf[b]])
                trk.op(qV, lambda e, b=b, rt=rt: e.tensor_copy(
                    out=hcar[:, rt:rt + 1], in_=hbf[b][:, TT - 1:TT]),
                    reads=[hbf[b]], writes=[hcar])
                if t >= 1:
                    trk.dma(qY, SHH[b], lambda e, t=t, rt=rt, b=b: e.dma_start(
                        out=hd[t - 1, rt], in_=hbf[b][:]), reads=[hbf[b]],
                        writes=[("hd", t - 1, rt)])
        qY.wait(SHH[0], SHH[0].v)
        qY.wait(SHH[1], SHH[1].v)
        flush_phase()
        pb_.close()

    LXC = [Ctr(es, nc, f"LXC{i}") for i in range(2)]
    LHH = [Ctr(es, nc, f"LHH{i}") for i in range(2)]
    LYY = [Ctr(es, nc, f"LYY{i}") for i in range(2)]
    SAC = [Ctr(es, nc, f"SAC{i}") for i in range(2)]
    SX2 = [Ctr(es, nc, f"SX2{i}") for i in range(2)]
    LWC = Ctr(es, nc, "LWC")

    if "C" in phases:
        wo = inp("wo", [D, D])
        yd = scratch("yd", [nt_real, NRT, 128, TT], F32)
        hd = scratch("hd", [nt_real, NRT, 128, TT], F32)
        accd = scratch("accd", [nt_real, NRT, 128, TT], F32)
        xn2d = scratch("xn2d", [nt_real, NRT, 128, TT], BF16)
        phase_begin()
        pc_ = ExitStack()
        wo_sb = pc_.enter_context(nc.sbuf_tensor("wo_sb", [128, KT * D], BF16))
        xc = [pc_.enter_context(nc.sbuf_tensor(f"xc{i}", [128, D], F32))
              for i in range(2)]
        xT = pc_.enter_context(nc.sbuf_tensor("xT", [128, NRT * TT], F32))
        hld = [pc_.enter_context(nc.sbuf_tensor(f"hld{i}", [128, TT], F32))
               for i in range(2)]
        yld = [pc_.enter_context(nc.sbuf_tensor(f"yld{i}", [128, TT], F32))
               for i in range(2)]
        hybf = pc_.enter_context(nc.sbuf_tensor("hybf", [128, NRT * TT], BF16))
        x1b = [pc_.enter_context(nc.sbuf_tensor(f"x1b{i}", [128, TT], F32))
               for i in range(2)]
        x1bf = pc_.enter_context(nc.sbuf_tensor("x1bf", [128, NRT * TT], BF16))
        x1sq = [pc_.enter_context(nc.sbuf_tensor(f"x1sq{i}", [128, TT], BF16))
                for i in range(2)]
        t2b = [pc_.enter_context(nc.sbuf_tensor(f"t2b{i}", [128, TT], F32))
               for i in range(2)]
        xn2b = [pc_.enter_context(nc.sbuf_tensor(f"xn2b{i}", [128, TT], BF16))
                for i in range(2)]
        stt = pc_.enter_context(nc.sbuf_tensor("stt", [1, 4 * TT], F32))
        ptp = [pc_.enter_context(nc.psum_tensor(f"ptp{i}", [128, TT], F32))
               for i in range(2)]
        psO = [pc_.enter_context(nc.psum_tensor(f"psO{i}", [128, TT], F32))
               for i in range(2)]
        psS1 = pc_.enter_context(nc.psum_tensor("psS1", [1, TT], F32))
        psS2 = pc_.enter_context(nc.psum_tensor("psS2", [1, TT], F32))
        psM = pc_.enter_context(nc.psum_tensor("psM", [128, TT], F32))
        psR = pc_.enter_context(nc.psum_tensor("psR", [128, TT], F32))

        trk.dma(qG, LWC, lambda e: e.dma_start(
            out=wo_sb[:].rearrange("p (kt m) -> p kt m", kt=KT),
            in_=wo.rearrange("(kt p) m -> p kt m", p=128)),
            reads=[wo], writes=[wo_sb])
        xTv = xT[:].rearrange("p (rt c2 n) -> p rt c2 n", rt=NRT, c2=4)

        for t in range(nt_real):
            for c4 in range(4):
                pc2 = c4 % 2
                r0 = TT * (t + 1) + 128 * c4
                trk.dma(qY, LXC[pc2], lambda e, r0=r0, pc2=pc2: e.dma_start(
                    out=xc[pc2][:], in_=xs[r0:r0 + 128, :]), writes=[xc[pc2]])
                for g4 in range(4):
                    pb2 = g4 % 2
                    def tpg(e, pc2=pc2, g4=g4, pb2=pb2):
                        mm = None
                        for j in range(4):
                            rt = g4 * 4 + j
                            mm = e.transpose(ptp[pb2][:, j * 128:(j + 1) * 128],
                                             xc[pc2][:, rt * 128:(rt + 1) * 128],
                                             ident[:])
                        return mm
                    trk.op(qT, tpg, reads=[xc[pc2], ident], writes=[ptp[pb2]])
                    trk.op(qS, lambda e, c4=c4, g4=g4, pb2=pb2: e.activation(
                        out=xTv[:, g4 * 4:(g4 + 1) * 4, c4, :], in_=ptp[pb2][:],
                        func=AF.Identity),
                        reads=[ptp[pb2]], writes=[xT])
            for rt in range(NRT):
                b = rt % 2
                trk.dma(qY, LHH[b], lambda e, t=t, rt=rt, b=b: e.dma_start(
                    out=hld[b][:], in_=hd[t, rt]), reads=[("hd", t, rt)],
                    writes=[hld[b]])
                trk.dma(qY, LYY[b], lambda e, t=t, rt=rt, b=b: e.dma_start(
                    out=yld[b][:], in_=yd[t, rt]), reads=[("yd", t, rt)],
                    writes=[yld[b]])
                trk.op(qV, lambda e, rt=rt, b=b: e.tensor_mul(
                    out=hybf[:, rt * TT:(rt + 1) * TT], in0=hld[b][:],
                    in1=yld[b][:]), reads=[hld[b], yld[b]], writes=[hybf])
            for rt in range(NRT):
                b = rt % 2
                def og(e, rt=rt, b=b):
                    mm = None
                    for kt in range(KT):
                        mm = e.matmul(psO[b][:],
                                      wo_sb[:, kt * D + rt * 128:kt * D + rt * 128 + 128],
                                      hybf[:, kt * TT:(kt + 1) * TT],
                                      start=(kt == 0), stop=(kt == KT - 1))
                    return mm
                trk.op(qT, og, reads=[wo_sb, hybf], writes=[psO[b]])
                trk.op(qS, lambda e, rt=rt, b=b: e.activation(
                    out=x1b[b][:], in_=psO[b][:], func=AF.Identity,
                    bias=pp[:, PC_BOUT + rt:PC_BOUT + rt + 1]),
                    reads=[psO[b], pp], writes=[x1b[b]])
                trk.op(qV, lambda e, rt=rt, b=b: e.tensor_add(
                    out=x1b[b][:], in0=x1b[b][:],
                    in1=xT[:, rt * TT:(rt + 1) * TT]),
                    reads=[x1b[b], xT], writes=[x1b[b]])
                trk.dma(qY, SAC[b], lambda e, t=t, rt=rt, b=b: e.dma_start(
                    out=accd[t, rt], in_=x1b[b][:]), reads=[x1b[b]],
                    writes=[("accd", t, rt)])
                trk.op(qS, lambda e, rt=rt, b=b: e.activation(
                    out=x1bf[:, rt * TT:(rt + 1) * TT], in_=x1b[b][:],
                    func=AF.Identity), reads=[x1b[b]], writes=[x1bf])
                trk.op(qS, lambda e, rt=rt, b=b: e.activation(
                    out=x1sq[b][:], in_=x1b[b][:], func=AF.Square),
                    reads=[x1b[b]], writes=[x1sq[b]])
                trk.op(qT, lambda e, rt=rt: e.matmul(
                    psS1[:], ones_col_bf[:], x1bf[:, rt * TT:(rt + 1) * TT],
                    start=(rt == 0), stop=(rt == NRT - 1)),
                    reads=[ones_col_bf, x1bf], writes=[psS1])
                trk.op(qT, lambda e, rt=rt, b=b: e.matmul(
                    psS2[:], ones_col_bf[:], x1sq[b][:],
                    start=(rt == 0), stop=(rt == NRT - 1)),
                    reads=[ones_col_bf, x1sq[b]], writes=[psS2])
            trk.op(qV, lambda e: e.tensor_scalar(
                out=stt[0:1, 0:TT], in0=psS1[:], scalar1=1.0 / D, scalar2=None,
                op0=ALU.mult), reads=[psS1], writes=[stt])
            trk.op(qV, lambda e: e.tensor_scalar(
                out=stt[0:1, TT:2 * TT], in0=psS2[:], scalar1=1.0 / D,
                scalar2=None, op0=ALU.mult), reads=[psS2], writes=[stt])
            trk.op(qV, lambda e: e.tensor_mul(
                out=stt[0:1, 3 * TT:4 * TT], in0=stt[0:1, 0:TT],
                in1=stt[0:1, 0:TT]), reads=[stt], writes=[stt])
            trk.op(qV, lambda e: e.tensor_sub(
                out=stt[0:1, TT:2 * TT], in0=stt[0:1, TT:2 * TT],
                in1=stt[0:1, 3 * TT:4 * TT]), reads=[stt], writes=[stt])
            trk.op(qV, lambda e: e.tensor_scalar(
                out=stt[0:1, TT:2 * TT], in0=stt[0:1, TT:2 * TT],
                scalar1=1e-5, scalar2=None, op0=ALU.add),
                reads=[stt], writes=[stt])
            trk.op(qS, lambda e: e.activation(
                out=stt[0:1, 3 * TT:4 * TT], in_=stt[0:1, TT:2 * TT], func=AF.Ln),
                reads=[stt], writes=[stt])
            trk.op(qS, lambda e: e.activation(
                out=stt[0:1, 2 * TT:3 * TT], in_=stt[0:1, 3 * TT:4 * TT],
                func=AF.Exp, scale=-0.5), reads=[stt], writes=[stt])
            trk.op(qT, lambda e: e.matmul(
                psM[:], ones_row[:], stt[0:1, 0:TT], start=True, stop=True),
                reads=[ones_row, stt], writes=[psM])
            trk.op(qT, lambda e: e.matmul(
                psR[:], ones_row[:], stt[0:1, 2 * TT:3 * TT], start=True,
                stop=True), reads=[ones_row, stt], writes=[psR])
            for rt in range(NRT):
                b = rt % 2
                trk.op(qV, lambda e, rt=rt, b=b: e.tensor_sub(
                    out=t2b[b][:], in0=x1bf[:, rt * TT:(rt + 1) * TT],
                    in1=psM[:]), reads=[x1bf, psM], writes=[t2b[b]])
                trk.op(qV, lambda e, rt=rt, b=b: e.tensor_mul(
                    out=xn2b[b][:], in0=t2b[b][:], in1=psR[:]),
                    reads=[t2b[b], psR], writes=[xn2b[b]])
                trk.dma(qY, SX2[b], lambda e, t=t, rt=rt, b=b: e.dma_start(
                    out=xn2d[t, rt], in_=xn2b[b][:]), reads=[xn2b[b]],
                    writes=[("xn2d", t, rt)])
        qY.wait(SAC[0], SAC[0].v)
        qY.wait(SAC[1], SAC[1].v)
        qY.wait(SX2[0], SX2[0].v)
        qY.wait(SX2[1], SX2[1].v)
        flush_phase()
        pc_.close()

    LX20 = [Ctr(es, nc, f"LX20{i}") for i in range(2)]
    LGW = Ctr(es, nc, "LGW")
    SCB = Ctr(es, nc, "SCB")
    SCB1 = Ctr(es, nc, "SCB1")

    if "D0" in phases:
        gwT = inp("gwT", [D, 16])
        xn2d = scratch("xn2d", [nt_real, NRT, 128, TT], BF16)
        combd = scratch("combd", [NE, nt_real * TT], F32)
        phase_begin()
        pd0 = ExitStack()
        gw_sb = pd0.enter_context(nc.sbuf_tensor("gw_sb", [128, KT * 16], BF16))
        x2t = [pd0.enter_context(nc.sbuf_tensor(f"x2t{i}", [128, KT * TT], BF16))
               for i in range(2)]
        scf = pd0.enter_context(nc.sbuf_tensor("scf", [16, TT], F32))
        sct = pd0.enter_context(nc.sbuf_tensor("sct", [128, 64], F32))
        esc = pd0.enter_context(nc.sbuf_tensor("esc", [128, 16], F32))
        prb = pd0.enter_context(nc.sbuf_tensor("prb", [128, 16], F32))
        wrk = pd0.enter_context(nc.sbuf_tensor("wrk", [128, 16], F32))
        msc = pd0.enter_context(nc.sbuf_tensor("msc", [128, 16], F32))
        cmb = pd0.enter_context(nc.sbuf_tensor("cmb", [128, 16], F32))
        b0 = pd0.enter_context(nc.sbuf_tensor("b0", [128, 1], F32))
        b1 = pd0.enter_context(nc.sbuf_tensor("b1", [128, 1], F32))
        b2 = pd0.enter_context(nc.sbuf_tensor("b2", [128, 1], F32))
        ccrow = pd0.enter_context(nc.sbuf_tensor("ccrow", [16, 128], F32))
        onr = pd0.enter_context(nc.sbuf_tensor("onr", [1, nt_real * TT], F32))
        psGt = pd0.enter_context(nc.psum_tensor("psGt", [16, TT], F32))
        ptg = pd0.enter_context(nc.psum_tensor("ptg", [128, 64], F32))
        pcc = pd0.enter_context(nc.psum_tensor("pcc", [16, 128], F32))

        trk.dma(qG, LGW, lambda e: e.dma_start(
            out=gw_sb[:].rearrange("p (kt m) -> p kt m", kt=KT),
            in_=gwT.rearrange("(kt p) m -> p kt m", p=128)),
            reads=[gwT], writes=[gw_sb])
        trk.op(qV, lambda e: e.memset(onr[:], 1.0), writes=[onr])
        trk.dma(qY, SCB1, lambda e: e.dma_start(
            out=combd[16:17, :], in_=onr[:]), reads=[onr], writes=[("combd", 16)])
        trk.dma(qY, SCB1, lambda e: e.dma_start(
            out=combd[17:18, :], in_=onr[:]), reads=[onr], writes=[("combd", 17)],
            first=False)

        for t in range(nt_real):
            p2 = t % 2
            trk.dma(qY, LX20[p2], lambda e, t=t, p2=p2: e.dma_start(
                out=x2t[p2][:].rearrange("p (rt n) -> p rt n", rt=NRT),
                in_=xn2d[t].rearrange("rt p n -> p rt n")),
                reads=[("xn2d", t, r_) for r_ in range(NRT)], writes=[x2t[p2]])
            def gg(e, p2=p2):
                mm = None
                for kt in range(KT):
                    mm = e.matmul(psGt[:], gw_sb[:, kt * 16:(kt + 1) * 16],
                                  x2t[p2][:, kt * TT:(kt + 1) * TT],
                                  start=(kt == 0), stop=(kt == KT - 1))
                return mm
            trk.op(qT, gg, reads=[gw_sb, x2t[p2]], writes=[psGt])
            trk.op(qS, lambda e: e.activation(
                out=scf[:], in_=psGt[:], func=AF.Identity),
                reads=[psGt], writes=[scf])
            def tg(e):
                mm = None
                for c4 in range(4):
                    mm = e.transpose(ptg[:, c4 * 16:(c4 + 1) * 16],
                                     scf[0:16, c4 * 128:(c4 + 1) * 128],
                                     ident[0:16, 0:16])
                return mm
            trk.op(qT, tg, reads=[scf, ident], writes=[ptg])
            trk.op(qS, lambda e: e.activation(
                out=sct[:], in_=ptg[:], func=AF.Identity),
                reads=[ptg], writes=[sct])
            for c4 in range(4):
                ch = sct[:, c4 * 16:(c4 + 1) * 16]
                trk.op(qV, lambda e, ch=ch: e.reduce_max(
                    out=b0[:], in_=ch, axis=AXX, negate=True),
                    reads=[sct], writes=[b0])
                trk.op(qS, lambda e, ch=ch: e.activation(
                    out=esc[:], in_=ch, func=AF.Exp, bias=b0[:]),
                    reads=[sct, b0], writes=[esc])
                trk.op(qV, lambda e: e.reduce_sum(
                    out=b1[:], in_=esc[:], axis=AXX), reads=[esc], writes=[b1])
                trk.op(qV, lambda e: e.reciprocal(out=b2[:], in_=b1[:]),
                       reads=[b1], writes=[b2])
                trk.op(qV, lambda e: e.tensor_scalar(
                    out=prb[:], in0=esc[:], scalar1=b2[:], scalar2=None,
                    op0=ALU.mult), reads=[esc, b2], writes=[prb])
                trk.op(qV, lambda e: e.tensor_copy(out=wrk[:], in_=prb[:]),
                       reads=[prb], writes=[wrk])
                for it in range(5):
                    trk.op(qV, lambda e: e.reduce_max(
                        out=b0[:], in_=wrk[:], axis=AXX),
                        reads=[wrk], writes=[b0])
                    trk.op(qV, lambda e: e.tensor_scalar(
                        out=msc[:], in0=wrk[:], scalar1=b0[:], scalar2=2.0,
                        op0=ALU.is_ge, op1=ALU.mult),
                        reads=[wrk, b0], writes=[msc])
                    trk.op(qV, lambda e: e.tensor_sub(
                        out=wrk[:], in0=wrk[:], in1=msc[:]),
                        reads=[wrk, msc], writes=[wrk])
                trk.op(qV, lambda e: e.reduce_max(
                    out=b0[:], in_=wrk[:], axis=AXX), reads=[wrk], writes=[b0])
                trk.op(qV, lambda e: e.tensor_scalar(
                    out=msc[:], in0=prb[:], scalar1=b0[:], scalar2=None,
                    op0=ALU.is_ge), reads=[prb, b0], writes=[msc])
                trk.op(qV, lambda e: e.tensor_mul(
                    out=wrk[:], in0=prb[:], in1=msc[:]),
                    reads=[prb, msc], writes=[wrk])
                trk.op(qV, lambda e: e.reduce_sum(
                    out=b1[:], in_=wrk[:], axis=AXX), reads=[wrk], writes=[b1])
                trk.op(qV, lambda e: e.tensor_scalar(
                    out=b1[:], in0=b1[:], scalar1=1e-20, scalar2=None,
                    op0=ALU.add), reads=[b1], writes=[b1])
                trk.op(qV, lambda e: e.reciprocal(out=b2[:], in_=b1[:]),
                       reads=[b1], writes=[b2])
                trk.op(qV, lambda e: e.tensor_scalar(
                    out=cmb[:], in0=wrk[:], scalar1=b2[:], scalar2=None,
                    op0=ALU.mult), reads=[wrk, b2], writes=[cmb])
                trk.op(qT, lambda e: e.transpose(pcc[:], cmb[:], ident[:]),
                       reads=[cmb, ident], writes=[pcc])
                trk.op(qS, lambda e: e.activation(
                    out=ccrow[:], in_=pcc[:], func=AF.Identity),
                    reads=[pcc], writes=[ccrow])
                c0 = t * TT + c4 * 128
                trk.dma(qY, SCB, lambda e, c0=c0: e.dma_start(
                    out=combd[0:16, c0:c0 + 128], in_=ccrow[:]),
                    reads=[ccrow], writes=[("combd", t, c4)])
        qY.wait(SCB, SCB.v)
        flush_phase()
        pd0.close()

    LW13 = Ctr(es, nc, "LW13")
    LW2D = Ctr(es, nc, "LW2D")
    LX2D = [Ctr(es, nc, f"LX2D{i}") for i in range(2)]
    LCR = [Ctr(es, nc, f"LCR{i}") for i in range(2)]
    SACCD = [Ctr(es, nc, f"SACCD{i}") for i in range(NRT)]

    if "D" in phases:
        w13p = inp("w13p", [NE, D, 2816])
        w2p = inp("w2p", [NE, IS, D])
        accd = scratch("accd", [nt_real, NRT, 128, TT], F32)
        xn2d = scratch("xn2d", [nt_real, NRT, 128, TT], BF16)
        combd = scratch("combd", [NE, nt_real * TT], F32)
        phase_begin()
        pd_ = ExitStack()
        w13_sb = pd_.enter_context(nc.sbuf_tensor("w13_sb", [128, KT * 2816], BF16))
        w2_sb = pd_.enter_context(nc.sbuf_tensor("w2_sb", [128, IC * D], BF16))
        x2d = [pd_.enter_context(nc.sbuf_tensor(f"x2d{i}", [128, KT * TT], BF16))
               for i in range(2)]
        s_sb = pd_.enter_context(nc.sbuf_tensor("s_sb", [128, IC * TT], BF16))
        sg = [pd_.enter_context(nc.sbuf_tensor(f"sg{i}", [128, TT], F32))
              for i in range(2)]
        su = [pd_.enter_context(nc.sbuf_tensor(f"su{i}", [128, TT], F32))
              for i in range(2)]
        ob = [pd_.enter_context(nc.sbuf_tensor(f"ob{i}", [128, TT], F32))
              for i in range(2)]
        crow = [pd_.enter_context(nc.sbuf_tensor(f"crow{i}", [1, TT], F32))
                for i in range(2)]
        psG = [pd_.enter_context(nc.psum_tensor(f"psG{i}", [128, TT], F32))
               for i in range(2)]
        psU = [pd_.enter_context(nc.psum_tensor(f"psU{i}", [128, TT], F32))
               for i in range(2)]
        psC = pd_.enter_context(nc.psum_tensor("psC", [128, TT], F32))
        psO2 = [pd_.enter_context(nc.psum_tensor(f"psO2{i}", [128, TT], F32))
                for i in range(2)]

        for s in range(NE):
            for ch in range(4):
                trk.dma(qG, LW13, lambda e, s=s, ch=ch: e.dma_start(
                    out=w13_sb[:].rearrange("p (kt m) -> p kt m", kt=KT)[:, 4 * ch:4 * (ch + 1), :],
                    in_=w13p[s].rearrange("(kt p) m -> p kt m", p=128)[:, 4 * ch:4 * (ch + 1), :]),
                    reads=[w13p], writes=[w13_sb], first=(ch == 0))
            for ch in range(2):
                kk = (6 * ch, min(IC, 6 * (ch + 1)))
                trk.dma(qG, LW2D, lambda e, s=s, kk=kk: e.dma_start(
                    out=w2_sb[:].rearrange("p (kt m) -> p kt m", kt=IC)[:, kk[0]:kk[1], :],
                    in_=w2p[s].rearrange("(kt p) m -> p kt m", p=128)[:, kk[0]:kk[1], :]),
                    reads=[w2p], writes=[w2_sb], first=(ch == 0))
            for t in range(nt_real):
                p2 = t % 2
                trk.dma(qY, LX2D[p2], lambda e, t=t, p2=p2: e.dma_start(
                    out=x2d[p2][:].rearrange("p (rt n) -> p rt n", rt=NRT),
                    in_=xn2d[t].rearrange("rt p n -> p rt n")),
                    reads=[("xn2d", t, r_) for r_ in range(NRT)],
                    writes=[x2d[p2]])
                if s < 16:
                    crdeps = [("combd", t, c4_) for c4_ in range(4)]
                else:
                    crdeps = [("combd", s)]
                trk.dma(qY, LCR[p2], lambda e, s=s, t=t, p2=p2: e.dma_start(
                    out=crow[p2][:], in_=combd[s:s + 1, t * TT:(t + 1) * TT]),
                    reads=crdeps, writes=[crow[p2]])
                trk.op(qT, lambda e, p2=p2: e.matmul(
                    psC[:], ones_row[:], crow[p2][:], start=True, stop=True),
                    reads=[ones_row, crow[p2]], writes=[psC])
                for c in range(IC):
                    b = c % 2

                    def g1g(e, p2=p2, c=c, b=b):
                        mm = None
                        for kt in range(KT):
                            mm = e.matmul(
                                psG[b][:],
                                w13_sb[:, kt * 2816 + c * 128:kt * 2816 + c * 128 + 128],
                                x2d[p2][:, kt * TT:(kt + 1) * TT],
                                start=(kt == 0), stop=(kt == KT - 1))
                        return mm

                    def u1g(e, p2=p2, c=c, b=b):
                        mm = None
                        for kt in range(KT):
                            mm = e.matmul(
                                psU[b][:],
                                w13_sb[:, kt * 2816 + IS + c * 128:kt * 2816 + IS + c * 128 + 128],
                                x2d[p2][:, kt * TT:(kt + 1) * TT],
                                start=(kt == 0), stop=(kt == KT - 1))
                        return mm
                    trk.op(qT, g1g, reads=[w13_sb, x2d[p2]], writes=[psG[b]])
                    trk.op(qT, u1g, reads=[w13_sb, x2d[p2]], writes=[psU[b]])
                    trk.op(qS, lambda e, b=b: e.activation(
                        out=sg[b][:], in_=psG[b][:], func=AF.Silu),
                        reads=[psG[b]], writes=[sg[b]])
                    trk.op(qV, lambda e, b=b: e.tensor_mul(
                        out=su[b][:], in0=sg[b][:], in1=psU[b][:]),
                        reads=[sg[b], psU[b]], writes=[su[b]])
                    trk.op(qV, lambda e, b=b, c=c: e.tensor_mul(
                        out=s_sb[:, c * TT:(c + 1) * TT], in0=su[b][:],
                        in1=psC[:]), reads=[su[b], psC], writes=[s_sb])
                for rt in range(NRT):
                    b = rt % 2

                    def o2g(e, rt=rt, b=b):
                        mm = None
                        for kt in range(IC):
                            mm = e.matmul(
                                psO2[b][:],
                                w2_sb[:, kt * D + rt * 128:kt * D + rt * 128 + 128],
                                s_sb[:, kt * TT:(kt + 1) * TT],
                                start=(kt == 0), stop=(kt == IC - 1))
                        return mm
                    trk.op(qT, o2g, reads=[w2_sb, s_sb], writes=[psO2[b]])
                    trk.op(qV, lambda e, b=b: e.tensor_copy(
                        out=ob[b][:], in_=psO2[b][:]),
                        reads=[psO2[b]], writes=[ob[b]])
                    trk.dma(qG, SACCD[rt], lambda e, t=t, rt=rt, b=b: e.dma_start(
                        out=accd[t, rt], in_=ob[b][:], accum_op=ALU.add),
                        reads=[ob[b]], writes=[("accd", t, rt)])
        flush_phase()
        pd_.close()

    LAE = [Ctr(es, nc, f"LAE{i}") for i in range(2)]
    SOUT = [Ctr(es, nc, f"SOUT{i}") for i in range(2)]

    if "E" in phases:
        accd = scratch("accd", [nt_real, NRT, 128, TT], F32)
        out = nc.dram_tensor("out", [nt_real * TT, D], F32, kind="ExternalOutput").ap()
        phase_begin()
        pe_ = ExitStack()
        aee = [pe_.enter_context(nc.sbuf_tensor(f"aee{i}", [128, NRT * TT], F32))
               for i in range(2)]
        oy = [pe_.enter_context(nc.sbuf_tensor(f"oy{i}", [128, D], F32))
              for i in range(2)]
        psE = [pe_.enter_context(nc.psum_tensor(f"psE{i}", [128, TT], F32))
               for i in range(2)]

        for t in range(nt_real):
            p2 = t % 2
            trk.dma(qY, LAE[p2], lambda e, t=t, p2=p2: e.dma_start(
                out=aee[p2][:].rearrange("p (rt n) -> p rt n", rt=NRT),
                in_=accd[t].rearrange("rt p n -> p rt n")),
                reads=[("accd", t, r_) for r_ in range(NRT)], writes=[aee[p2]])
            for c4 in range(4):
                pc2 = c4 % 2
                for g4 in range(4):

                    def tpe(e, p2=p2, c4=c4, g4=g4, pc2=pc2):
                        mm = None
                        for j in range(4):
                            rt = g4 * 4 + j
                            mm = e.transpose(
                                psE[pc2][:, j * 128:(j + 1) * 128],
                                aee[p2][:, rt * TT + c4 * 128:rt * TT + c4 * 128 + 128],
                                ident[:])
                        return mm
                    trk.op(qT, tpe, reads=[aee[p2], ident], writes=[psE[pc2]])
                    trk.op(qS, lambda e, g4=g4, pc2=pc2: e.activation(
                        out=oy[pc2][:, g4 * TT:(g4 + 1) * TT], in_=psE[pc2][:],
                        func=AF.Identity), reads=[psE[pc2]], writes=[oy[pc2]])
                r0 = t * TT + c4 * 128
                trk.dma(qY, SOUT[pc2], lambda e, r0=r0, pc2=pc2: e.dma_start(
                    out=out[r0:r0 + 128, :], in_=oy[pc2][:]),
                    reads=[oy[pc2]], writes=[("out", r0)])
        flush_phase()
        pe_.close()

    for s in sorted(trk.dma_sems, key=lambda c: c.name):
        if s.v > 0:
            qG.wait(s, s.v)
    with nc.Block() as blk:
        @blk.gpsimd
        def _(e):
            qG.replay(e)

    es.close()
    return nc


def _pack_inputs(inputs, nt_real=8, n_cores=8, names=None):
    f32 = np.float32
    x = np.ascontiguousarray(np.asarray(inputs["L0_attn_x"]).reshape(32768, D))
    chunk = 4096
    warm = TT
    in_maps = []

    ppk = np.zeros((128, PC_N), f32)

    def col16(v):
        return np.ascontiguousarray(np.asarray(v).reshape(16, 128).T)

    ppk[:, PC_BY:PC_BY + 16] = col16(inputs["L0_attn_b_y"])
    ppk[:, PC_BX:PC_BX + 16] = col16(inputs["L0_attn_b_x"])
    ppk[:, PC_BIG:PC_BIG + 16] = col16(inputs["L0_attn_b_ig"])
    ppk[:, PC_BAG:PC_BAG + 16] = col16(inputs["L0_attn_b_ag"])
    ppk[:, PC_BOUT:PC_BOUT + 16] = col16(inputs["L0_attn_b_out"])
    ppk[:, PC_CB:PC_CB + 16] = col16(inputs["L0_attn_conv_b"])
    cw = np.asarray(inputs["L0_attn_conv_w"]).reshape(4, 16, 128).transpose(2, 1, 0)
    ppk[:, PC_CW:PC_CW + 64] = np.ascontiguousarray(cw.reshape(128, 64))
    ppk[:, PC_AP:PC_AP + 16] = col16(inputs["L0_attn_a_param"])

    gwT_ = np.ascontiguousarray(np.asarray(inputs["L0_ffn_gate_w"]).T)

    w13 = np.asarray(inputs["L0_ffn_w13"])
    sw13T = np.ascontiguousarray(np.asarray(inputs["L0_ffn_sw13"]).T)
    w2 = np.asarray(inputs["L0_ffn_w2"])
    sw2T = np.ascontiguousarray(np.asarray(inputs["L0_ffn_sw2"]).T)
    w13p_ = np.empty((NE, D, 2816), f32)
    w13p_[0:16] = w13.transpose(0, 2, 1)
    w13p_[16] = np.concatenate([sw13T[:, 0:IS], sw13T[:, 2 * IS:3 * IS]], axis=1)
    w13p_[17] = np.concatenate([sw13T[:, IS:2 * IS], sw13T[:, 3 * IS:4 * IS]], axis=1)
    w2p_ = np.empty((NE, IS, D), f32)
    w2p_[0:16] = w2.transpose(0, 2, 1)
    w2p_[16] = sw2T[0:IS]
    w2p_[17] = sw2T[IS:2 * IS]

    common = {
        "ppack": ppk, "gwT": gwT_, "w13p": w13p_, "w2p": w2p_,
        "wy": np.asarray(inputs["L0_attn_w_y"]),
        "wx": np.asarray(inputs["L0_attn_w_x"]),
        "wig": np.asarray(inputs["L0_attn_w_ig"]),
        "wag": np.asarray(inputs["L0_attn_w_ag"]),
        "wo": np.asarray(inputs["L0_attn_w_out"]),
    }
    if names is not None:
        common = {k: v for k, v in common.items() if k in names}
    for c in range(n_cores):
        lo = chunk * c
        if c % 4 == 0:
            sh = np.concatenate([np.zeros((warm, D), f32), x[lo:lo + nt_real * TT]],
                                axis=0)
            flag = 0.0
        else:
            sh = np.ascontiguousarray(x[lo - warm:lo + nt_real * TT])
            flag = 1.0
        m = dict(common)
        m["xs"] = sh
        m["flags"] = np.tile(np.array([[flag, 1.0 - flag]], f32), (128, 1))
        in_maps.append(m)
    return in_maps


def kernel(**inputs):
    n_cores = 8
    nc = build_program()
    in_maps = _pack_inputs(inputs, nt_real=8, n_cores=n_cores)
    res = run_bass_kernel_spmd(nc, in_maps, core_ids=list(range(n_cores)))
    out = np.concatenate([res.results[c]["out"] for c in range(n_cores)], axis=0)
    return out.reshape(2, 16384, D)


# version 05
# speedup vs baseline: 1.0366x; 1.0366x over previous
import numpy as np
import ml_dtypes
from contextlib import ExitStack

import concourse.bass as bass
import concourse.mybir as mybir
from concourse.bass_utils import run_bass_kernel_spmd

F32 = mybir.dt.float32
BF16 = mybir.dt.bfloat16
AF = mybir.ActivationFunctionType
ALU = mybir.AluOpType
AXX = mybir.AxisListType.X

D = 2048
NRT = 16
TT = 512
KT = 16
IC = 11
NE = 18
IS = 1408

PC_BY, PC_BX, PC_BIG, PC_BAG, PC_BOUT, PC_CB = 0, 16, 32, 48, 64, 80
PC_CW = 96
PC_AP = 160
PC_N = 176


class Ctr:
    def __init__(self, es, nc, name):
        self.h = es.enter_context(nc.semaphore(name))
        self.v = 0
        self.name = name


def _key(ap):
    if isinstance(ap, (str, tuple)):
        return ap
    t = getattr(ap, "tensor", None)
    return t.name if t is not None else str(ap)


class Q:
    def __init__(self, name, prog):
        self.name = name
        self.prog = prog
        self.waited = {}
        self.pending = []

    def wait(self, ctr, val):
        if val <= 0:
            return
        if self.waited.get(id(ctr), -1) >= val:
            return
        self.pending.append(lambda e, h=ctr.h, v=val: e.wait_ge(h, v))
        self.waited[id(ctr)] = val

    def raw(self, fn):
        self.pending.append(fn)

    def replay(self, eng):
        for fn in self.pending:
            fn(eng)
        self.pending = []


class Trk:
    def __init__(self):
        self.writer = {}
        self.readers = {}
        self.dma_sems = set()
        self.dma_q = {}

    def _pre(self, q, reads, writes):
        for ap in reads:
            w = self.writer.get(_key(ap))
            if w:
                q.wait(w[0], w[1])
        for ap in writes:
            k = _key(ap)
            w = self.writer.get(k)
            if w:
                q.wait(w[0], w[1])
            for (ctr, val) in self.readers.get(k, {}).values():
                q.wait(ctr, val)

    def _post(self, q, reads, writes, ctr, val):
        for ap in writes:
            k = _key(ap)
            self.writer[k] = (ctr, val)
            self.readers[k] = {}
        for ap in reads:
            self.readers.setdefault(_key(ap), {})[q.name] = (ctr, val)

    def op(self, q, fn, reads=(), writes=(), waits=()):
        self._pre(q, reads, writes)
        for (ctr, val) in waits:
            q.wait(ctr, val)
        q.prog.v += 1
        sh, sv = q.prog.h, q.prog.v
        q.raw(lambda e: fn(e).then_inc(sh, 1))
        self._post(q, reads, writes, q.prog, q.prog.v)

    def dma(self, q, sem, fn, reads=(), writes=(), waits=(), first=True,
            no_drain=False):
        self.dma_sems.add(sem)
        if not no_drain:
            self.dma_q[sem] = q
        if first and sem.v > 0:
            q.wait(sem, sem.v)
        self._pre(q, reads, writes)
        for (ctr, val) in waits:
            q.wait(ctr, val)
        sem.v += 16
        sh = sem.h
        q.raw(lambda e: fn(e).then_inc(sh, 16))
        self._post(q, reads, writes, sem, sem.v)


def build_program(nt_real=8, phases=("0", "A", "B", "C", "D0", "D", "E"), dbg=()):
    nwt = nt_real + 1
    rows = nwt * TT
    nc = bass.Bass("TRN2", target_bir_lowering=False)

    _dram = {}

    def inp(name, shape):
        if name not in _dram:
            _dram[name] = nc.dram_tensor(name, shape, F32, kind="ExternalInput").ap()
        return _dram[name]

    def scratch(name, shape, dt):
        if name not in _dram:
            kind = "ExternalOutput" if name in dbg else "Internal"
            _dram[name] = nc.dram_tensor(name, shape, dt, kind=kind).ap()
        return _dram[name]

    xs = inp("xs", [rows, D])
    flags = inp("flags", [128, 2])
    ppack = inp("ppack", [128, PC_N])

    es = ExitStack()
    pp = es.enter_context(nc.sbuf_tensor("pp", [128, PC_N], F32))
    fl = es.enter_context(nc.sbuf_tensor("fl", [128, 2], F32))
    sl = es.enter_context(nc.sbuf_tensor("sl", [128, 48], F32))
    nbias = es.enter_context(nc.sbuf_tensor("nbias", [128, 32], F32))
    cst = es.enter_context(nc.sbuf_tensor("cst", [128, 8], F32))
    ident = es.enter_context(nc.sbuf_tensor("ident", [128, 128], F32))
    ones_col_bf = es.enter_context(nc.sbuf_tensor("ones_col_bf", [128, 1], BF16))
    ones_row = es.enter_context(nc.sbuf_tensor("ones_row", [1, 128], F32))
    C_ONE, C_EPS, C_INVD, C_MONE, C_TINY = 0, 1, 2, 3, 4

    trk = Trk()
    PB = Ctr(es, nc, "PB")
    NENG = 5
    TEP = Ctr(es, nc, "TEP")
    VEP = Ctr(es, nc, "VEP")
    SEP = Ctr(es, nc, "SEP")
    GPP = Ctr(es, nc, "GPP")
    qT = Q("T", TEP)
    qV = Q("V", VEP)
    qS = Q("S", SEP)
    qG = Q("G", GPP)
    qY = Q("Y", None)
    ALLQ = [qT, qV, qS, qG, qY]

    def flush_phase():
        for sem, q in trk.dma_q.items():
            q.wait(sem, sem.v)
        for q in ALLQ:
            q.raw(lambda e: e.sem_inc(PB.h, 1))
        PB.v += NENG
        with nc.Block() as blk:
            @blk.tensor
            def _(e):
                qT.replay(e)

            @blk.vector
            def _(e):
                qV.replay(e)

            @blk.scalar
            def _(e):
                qS.replay(e)

            @blk.gpsimd
            def _(e):
                qG.replay(e)

            @blk.sync
            def _(e):
                qY.replay(e)

    def phase_begin():
        pbw = PB.v
        if pbw > 0:
            for q in ALLQ:
                q.raw(lambda e, v=pbw: e.wait_ge(PB.h, v))

    L0X = [Ctr(es, nc, f"L0X{i}") for i in range(2)]
    S0X = [Ctr(es, nc, f"S0X{i}") for i in range(2)]
    LPP = Ctr(es, nc, "LPP")
    LFL = Ctr(es, nc, "LFL")
    LWA = [Ctr(es, nc, f"LWA{i}") for i in range(2)]
    LXT = [Ctr(es, nc, f"LXT{i}") for i in range(3)]
    SYY = [Ctr(es, nc, f"SYY{i}") for i in range(2)]
    SCV = [Ctr(es, nc, f"SCV{i}") for i in range(2)]

    n0 = rows // 128

    if "0" in phases:
        xn1 = scratch("xn1", [rows, D], BF16)
        phase_begin()
        p0 = ExitStack()
        xld = [p0.enter_context(nc.sbuf_tensor(f"xld{i}", [128, D], F32)) for i in range(2)]
        sqs = [p0.enter_context(nc.sbuf_tensor(f"sqs{i}", [128, D], F32)) for i in range(2)]
        xnb = [p0.enter_context(nc.sbuf_tensor(f"xnb{i}", [128, D], BF16)) for i in range(2)]
        st0 = [p0.enter_context(nc.sbuf_tensor(f"st0{i}", [128, 8], F32)) for i in range(2)]

        trk.dma(qY, LPP, lambda e: e.dma_start(out=pp[:], in_=ppack[:]),
                reads=[ppack], writes=[pp])
        trk.dma(qY, LFL, lambda e: e.dma_start(out=fl[:], in_=flags[:]),
                reads=[flags], writes=[fl])
        trk.op(qV, lambda e: e.memset(ident[:], 0.0), writes=[ident])
        trk.op(qG, lambda e: e.affine_select(
            out=ident[:], in_=ident[:], compare_op=ALU.not_equal,
            fill=1.0, base=0, pattern=[[-1, 128]], channel_multiplier=1),
            reads=[ident], writes=[ident])
        trk.op(qV, lambda e: e.memset(ones_col_bf[:], 1.0), writes=[ones_col_bf])
        trk.op(qV, lambda e: e.memset(ones_row[:], 1.0), writes=[ones_row])
        for col, val in ((C_ONE, 1.0), (C_EPS, 1e-5), (C_INVD, 1.0 / D),
                         (C_MONE, -1.0), (C_TINY, 1e-20)):
            trk.op(qV, lambda e, col=col, val=val: e.memset(
                cst[:, col:col + 1], val), writes=[cst])
        trk.op(qV, lambda e: e.tensor_scalar(
            out=nbias[:, 0:16], in0=pp[:, PC_BIG:PC_BIG + 16],
            scalar1=-1.0, scalar2=None, op0=ALU.mult), reads=[pp], writes=[nbias])
        trk.op(qV, lambda e: e.tensor_scalar(
            out=nbias[:, 16:32], in0=pp[:, PC_BAG:PC_BAG + 16],
            scalar1=-1.0, scalar2=None, op0=ALU.mult), reads=[pp], writes=[nbias])
        trk.op(qS, lambda e: e.activation(
            out=sl[:, 16:32], in_=pp[:, PC_AP:PC_AP + 16], func=AF.Abs),
            reads=[pp], writes=[sl])
        trk.op(qS, lambda e: e.activation(
            out=sl[:, 16:32], in_=sl[:, 16:32], func=AF.Exp, scale=-1.0),
            reads=[sl], writes=[sl])
        trk.op(qS, lambda e: e.activation(
            out=sl[:, 16:32], in_=sl[:, 16:32], func=AF.Ln,
            bias=cst[:, C_ONE:C_ONE + 1]), reads=[sl, cst], writes=[sl])
        trk.op(qV, lambda e: e.tensor_scalar(
            out=sl[:, 0:16], in0=pp[:, PC_AP:PC_AP + 16],
            scalar1=0.0, scalar2=None, op0=ALU.max), reads=[pp, sl], writes=[sl])
        trk.op(qV, lambda e: e.tensor_add(
            out=sl[:, 32:48], in0=sl[:, 0:16], in1=sl[:, 16:32]),
            reads=[sl], writes=[sl])
        trk.op(qV, lambda e: e.tensor_scalar(
            out=sl[:, 0:16], in0=sl[:, 32:48],
            scalar1=-8.0, scalar2=None, op0=ALU.mult), reads=[sl], writes=[sl])
        trk.op(qV, lambda e: e.tensor_scalar(
            out=sl[:, 16:32], in0=sl[:, 32:48],
            scalar1=-16.0, scalar2=None, op0=ALU.mult), reads=[sl], writes=[sl])
        for i in range(n0):
            p = i % 2
            trk.dma(qY, L0X[p], lambda e, i=i, p=p: e.dma_start(
                out=xld[p][:], in_=xs[128 * i:128 * (i + 1), :]), writes=[xld[p]])
            trk.op(qS, lambda e, p=p: e.activation(
                out=sqs[p][:], in_=xld[p][:], func=AF.Square,
                accum_out=st0[p][:, 1:2]),
                reads=[xld[p]], writes=[sqs[p], st0[p]])
            trk.op(qV, lambda e, p=p: e.reduce_sum(
                out=st0[p][:, 0:1], in_=xld[p][:], axis=AXX),
                reads=[xld[p]], writes=[st0[p]])
            trk.op(qV, lambda e, p=p: e.tensor_scalar(
                out=st0[p][:, 2:3], in0=st0[p][:, 0:1],
                scalar1=1.0 / D, scalar2=None, op0=ALU.mult),
                reads=[st0[p]], writes=[st0[p]])
            trk.op(qV, lambda e, p=p: e.tensor_scalar(
                out=st0[p][:, 3:4], in0=st0[p][:, 1:2],
                scalar1=1.0 / D, scalar2=None, op0=ALU.mult),
                reads=[st0[p]], writes=[st0[p]])
            trk.op(qV, lambda e, p=p: e.tensor_tensor(
                out=st0[p][:, 4:5], in0=st0[p][:, 2:3], in1=st0[p][:, 2:3],
                op=ALU.mult), reads=[st0[p]], writes=[st0[p]])
            trk.op(qV, lambda e, p=p: e.tensor_sub(
                out=st0[p][:, 3:4], in0=st0[p][:, 3:4], in1=st0[p][:, 4:5]),
                reads=[st0[p]], writes=[st0[p]])
            trk.op(qV, lambda e, p=p: e.tensor_scalar(
                out=st0[p][:, 3:4], in0=st0[p][:, 3:4],
                scalar1=1e-5, scalar2=None, op0=ALU.add),
                reads=[st0[p]], writes=[st0[p]])
            trk.op(qS, lambda e, p=p: e.activation(
                out=st0[p][:, 4:5], in_=st0[p][:, 3:4], func=AF.Ln),
                reads=[st0[p]], writes=[st0[p]])
            trk.op(qS, lambda e, p=p: e.activation(
                out=st0[p][:, 5:6], in_=st0[p][:, 4:5], func=AF.Exp, scale=-0.5),
                reads=[st0[p]], writes=[st0[p]])
            trk.op(qV, lambda e, p=p: e.tensor_scalar(
                out=xnb[p][:], in0=xld[p][:],
                scalar1=st0[p][:, 2:3], scalar2=st0[p][:, 5:6],
                op0=ALU.subtract, op1=ALU.mult),
                reads=[xld[p], st0[p]], writes=[xnb[p]])
            trk.dma(qY, S0X[p], lambda e, i=i, p=p: e.dma_start(
                out=xn1[128 * i:128 * (i + 1), :], in_=xnb[p][:]),
                reads=[xnb[p]], writes=[("xn1", i)])
        qY.wait(S0X[0], S0X[0].v)
        qY.wait(S0X[1], S0X[1].v)
        flush_phase()
        p0.close()

    if "A" in phases:
        wy = inp("wy", [D, D])
        wx = inp("wx", [D, D])
        xn1 = scratch("xn1", [rows, D], BF16)
        convd = scratch("convd", [nwt, NRT, 128, TT], F32)
        yd = scratch("yd", [nt_real, NRT, 128, TT], F32)
        phase_begin()
        pa = ExitStack()
        wy_sb = pa.enter_context(nc.sbuf_tensor("wy_sb", [128, KT * D], BF16))
        wx_sb = pa.enter_context(nc.sbuf_tensor("wx_sb", [128, KT * D], BF16))
        xnt = [pa.enter_context(nc.sbuf_tensor(f"xnt{i}", [128, KT * TT], BF16))
               for i in range(3)]
        ybuf = [pa.enter_context(nc.sbuf_tensor(f"ybuf{i}", [128, TT], F32))
                for i in range(2)]
        xbx = [pa.enter_context(nc.sbuf_tensor(f"xbx{i}", [128, TT + 3], F32))
               for i in range(2)]
        cv1 = [pa.enter_context(nc.sbuf_tensor(f"cv1{i}", [128, TT], F32))
               for i in range(2)]
        cv2 = [pa.enter_context(nc.sbuf_tensor(f"cv2{i}", [128, TT], F32))
               for i in range(2)]
        carr = pa.enter_context(nc.sbuf_tensor("carr", [128, NRT * 3], F32))
        psY = [pa.enter_context(nc.psum_tensor(f"psY{i}", [128, TT], F32))
               for i in range(2)]
        psX = [pa.enter_context(nc.psum_tensor(f"psX{i}", [128, TT], F32))
               for i in range(2)]

        trk.dma(qG, LWA[0], lambda e: e.dma_start(
            out=wy_sb[:].rearrange("p (kt m) -> p kt m", kt=KT),
            in_=wy.rearrange("(kt p) m -> p kt m", p=128)),
            reads=[wy], writes=[wy_sb])
        trk.dma(qG, LWA[1], lambda e: e.dma_start(
            out=wx_sb[:].rearrange("p (kt m) -> p kt m", kt=KT),
            in_=wx.rearrange("(kt p) m -> p kt m", p=128)),
            reads=[wx], writes=[wx_sb])
        trk.op(qV, lambda e: e.memset(carr[:], 0.0), writes=[carr])

        def mmgroup(ps, wsb, rt, src, n=KT, w=TT):
            def fn(e):
                mm = None
                for kt in range(n):
                    mm = e.matmul(ps[:],
                                  wsb[:, kt * D + rt * 128:kt * D + rt * 128 + 128],
                                  src[:, kt * w:(kt + 1) * w],
                                  start=(kt == 0), stop=(kt == n - 1))
                return mm
            return fn

        for t in range(nwt):
            p3 = t % 3
            for rt in range(NRT):
                trk.dma(qY, LXT[p3], lambda e, t=t, rt=rt, p3=p3: e.dma_start_transpose(
                    out=xnt[p3][:, rt * TT:(rt + 1) * TT],
                    in_=xn1[TT * t:TT * (t + 1), 128 * rt:128 * (rt + 1)]),
                    reads=[("xn1", j) for j in range(4 * t, 4 * t + 4)],
                    writes=[xnt[p3]], first=(rt == 0))
            for rt in range(NRT):
                b = rt % 2
                if t >= 1:
                    trk.op(qT, mmgroup(psY[b], wy_sb, rt, xnt[p3]),
                           reads=[wy_sb, xnt[p3]], writes=[psY[b]])
                    trk.op(qS, lambda e, rt=rt, b=b: e.activation(
                        out=ybuf[b][:], in_=psY[b][:], func=AF.Gelu,
                        bias=pp[:, PC_BY + rt:PC_BY + rt + 1]),
                        reads=[psY[b], pp], writes=[ybuf[b]])
                    trk.dma(qY, SYY[b], lambda e, t=t, rt=rt, b=b: e.dma_start(
                        out=yd[t - 1, rt], in_=ybuf[b][:]), reads=[ybuf[b]],
                        writes=[("yd", t - 1, rt)])
                trk.op(qT, mmgroup(psX[b], wx_sb, rt, xnt[p3]),
                       reads=[wx_sb, xnt[p3]], writes=[psX[b]])
                trk.op(qV, lambda e, rt=rt, b=b: e.tensor_copy(
                    out=xbx[b][:, 0:3], in_=carr[:, 3 * rt:3 * rt + 3]),
                    reads=[carr], writes=[xbx[b]])
                trk.op(qS, lambda e, rt=rt, b=b: e.activation(
                    out=xbx[b][:, 3:TT + 3], in_=psX[b][:], func=AF.Identity,
                    bias=pp[:, PC_BX + rt:PC_BX + rt + 1]),
                    reads=[psX[b], pp], writes=[xbx[b]])
                cwc = PC_CW + 4 * rt
                trk.op(qV, lambda e, b=b, cwc=cwc: e.tensor_scalar(
                    out=cv1[b][:], in0=xbx[b][:, 0:TT],
                    scalar1=pp[:, cwc:cwc + 1], scalar2=None, op0=ALU.mult),
                    reads=[xbx[b], pp], writes=[cv1[b]])
                trk.op(qV, lambda e, b=b, cwc=cwc: e.scalar_tensor_tensor(
                    out=cv2[b][:], in0=xbx[b][:, 1:TT + 1],
                    scalar=pp[:, cwc + 1:cwc + 2], in1=cv1[b][:],
                    op0=ALU.mult, op1=ALU.add),
                    reads=[xbx[b], pp, cv1[b]], writes=[cv2[b]])
                trk.op(qV, lambda e, b=b, cwc=cwc: e.scalar_tensor_tensor(
                    out=cv1[b][:], in0=xbx[b][:, 2:TT + 2],
                    scalar=pp[:, cwc + 2:cwc + 3], in1=cv2[b][:],
                    op0=ALU.mult, op1=ALU.add),
                    reads=[xbx[b], pp, cv2[b]], writes=[cv1[b]])
                trk.op(qV, lambda e, b=b, cwc=cwc: e.scalar_tensor_tensor(
                    out=cv2[b][:], in0=xbx[b][:, 3:TT + 3],
                    scalar=pp[:, cwc + 3:cwc + 4], in1=cv1[b][:],
                    op0=ALU.mult, op1=ALU.add),
                    reads=[xbx[b], pp, cv1[b]], writes=[cv2[b]])
                trk.op(qV, lambda e, b=b, rt=rt: e.tensor_scalar(
                    out=cv2[b][:], in0=cv2[b][:],
                    scalar1=pp[:, PC_CB + rt:PC_CB + rt + 1],
                    scalar2=None, op0=ALU.add),
                    reads=[cv2[b], pp], writes=[cv2[b]])
                trk.op(qV, lambda e, b=b, rt=rt: e.tensor_copy(
                    out=carr[:, 3 * rt:3 * rt + 3], in_=xbx[b][:, TT:TT + 3]),
                    reads=[xbx[b]], writes=[carr])
                trk.dma(qG, SCV[b], lambda e, t=t, rt=rt, b=b: e.dma_start(
                    out=convd[t, rt], in_=cv2[b][:]), reads=[cv2[b]],
                    writes=[("convd", t, rt)])
        qY.wait(SYY[0], SYY[0].v)
        qY.wait(SYY[1], SYY[1].v)
        qG.wait(SCV[0], SCV[0].v)
        qG.wait(SCV[1], SCV[1].v)
        flush_phase()
        pa.close()

    LCB = [Ctr(es, nc, f"LCB{i}") for i in range(2)]
    LCF = [Ctr(es, nc, f"LCF{i}") for i in range(2)]
    SHH = [Ctr(es, nc, f"SHH{i}") for i in range(2)]
    LWB = [Ctr(es, nc, f"LWB{i}") for i in range(2)]

    if "B" in phases:
        wig = inp("wig", [D, D])
        wag = inp("wag", [D, D])
        convd = scratch("convd", [nwt, NRT, 128, TT], F32)
        hd = scratch("hd", [nt_real, NRT, 128, TT], F32)
        phase_begin()
        pb_ = ExitStack()
        wig_sb = pb_.enter_context(nc.sbuf_tensor("wig_sb", [128, KT * D], BF16))
        wag_sb = pb_.enter_context(nc.sbuf_tensor("wag_sb", [128, KT * D], BF16))
        cbf = [pb_.enter_context(nc.sbuf_tensor(f"cbf{i}", [128, KT * TT], BF16))
               for i in range(2)]
        cf32 = [pb_.enter_context(nc.sbuf_tensor(f"cf32{i}", [128, TT], F32))
                for i in range(2)]
        g1 = [pb_.enter_context(nc.sbuf_tensor(f"g1_{i}", [128, TT], F32))
              for i in range(2)]
        g2 = [pb_.enter_context(nc.sbuf_tensor(f"g2_{i}", [128, TT], F32))
              for i in range(2)]
        abuf = [pb_.enter_context(nc.sbuf_tensor(f"abuf{i}", [128, TT], F32))
                for i in range(2)]
        m2b = [pb_.enter_context(nc.sbuf_tensor(f"m2b{i}", [128, TT], F32))
               for i in range(2)]
        nx1 = [pb_.enter_context(nc.sbuf_tensor(f"nx1_{i}", [128, TT], F32))
               for i in range(2)]
        nxb = [pb_.enter_context(nc.sbuf_tensor(f"nxb{i}", [128, TT], F32))
               for i in range(2)]
        hbf = [pb_.enter_context(nc.sbuf_tensor(f"hbf{i}", [128, TT], F32))
               for i in range(2)]
        hcar = pb_.enter_context(nc.sbuf_tensor("hcar", [128, NRT], F32))
        bsc = pb_.enter_context(nc.sbuf_tensor("bsc", [128, 4], F32))
        psA = [pb_.enter_context(nc.psum_tensor(f"psA{i}", [128, TT], F32))
               for i in range(2)]
        psB = [pb_.enter_context(nc.psum_tensor(f"psB{i}", [128, TT], F32))
               for i in range(2)]

        trk.dma(qG, LWB[0], lambda e: e.dma_start(
            out=wig_sb[:].rearrange("p (kt m) -> p kt m", kt=KT),
            in_=wig.rearrange("(kt p) m -> p kt m", p=128)),
            reads=[wig], writes=[wig_sb])
        trk.dma(qG, LWB[1], lambda e: e.dma_start(
            out=wag_sb[:].rearrange("p (kt m) -> p kt m", kt=KT),
            in_=wag.rearrange("(kt p) m -> p kt m", p=128)),
            reads=[wag], writes=[wag_sb])

        def mmg(ps, wsb, rt, src):
            def fn(e):
                mm = None
                for kt in range(KT):
                    mm = e.matmul(ps[:],
                                  wsb[:, kt * D + rt * 128:kt * D + rt * 128 + 128],
                                  src[:, kt * TT:(kt + 1) * TT],
                                  start=(kt == 0), stop=(kt == KT - 1))
                return mm
            return fn

        for t in range(nwt):
            p2 = t % 2
            trk.dma(qG, LCB[p2], lambda e, t=t, p2=p2: e.dma_start(
                out=cbf[p2][:].rearrange("p (rt n) -> p rt n", rt=NRT),
                in_=convd[t].rearrange("rt p n -> p rt n")),
                reads=[("convd", t, r_) for r_ in range(NRT)], writes=[cbf[p2]])
            for rt in range(NRT):
                b = rt % 2
                trk.dma(qY, LCF[b], lambda e, t=t, rt=rt, b=b: e.dma_start(
                    out=cf32[b][:], in_=convd[t, rt]),
                    reads=[("convd", t, rt)], writes=[cf32[b]])
                trk.op(qT, mmg(psA[b], wag_sb, rt, cbf[p2]),
                       reads=[wag_sb, cbf[p2]], writes=[psA[b]])
                trk.op(qS, lambda e, rt=rt, b=b: e.activation(
                    out=g1[b][:], in_=psA[b][:], func=AF.Sigmoid,
                    bias=pp[:, PC_BAG + rt:PC_BAG + rt + 1]),
                    reads=[psA[b], pp], writes=[g1[b]])
                trk.op(qS, lambda e, rt=rt, b=b: e.activation(
                    out=abuf[b][:], in_=g1[b][:], func=AF.Exp,
                    scale=sl[:, rt:rt + 1]),
                    reads=[g1[b], sl], writes=[abuf[b]])
                trk.op(qS, lambda e, rt=rt, b=b: e.activation(
                    out=m2b[b][:], in_=g1[b][:], func=AF.Exp,
                    scale=sl[:, 16 + rt:16 + rt + 1]),
                    reads=[g1[b], sl], writes=[m2b[b]])
                trk.op(qV, lambda e, b=b: e.tensor_scalar(
                    out=m2b[b][:], in0=m2b[b][:], scalar1=-1.0, scalar2=1.0,
                    op0=ALU.mult, op1=ALU.add), reads=[m2b[b]], writes=[m2b[b]])
                trk.op(qS, lambda e, b=b: e.activation(
                    out=m2b[b][:], in_=m2b[b][:], func=AF.Sqrt),
                    reads=[m2b[b]], writes=[m2b[b]])
                trk.op(qT, mmg(psB[b], wig_sb, rt, cbf[p2]),
                       reads=[wig_sb, cbf[p2]], writes=[psB[b]])
                trk.op(qS, lambda e, rt=rt, b=b: e.activation(
                    out=g2[b][:], in_=psB[b][:], func=AF.Sigmoid,
                    bias=pp[:, PC_BIG + rt:PC_BIG + rt + 1]),
                    reads=[psB[b], pp], writes=[g2[b]])
                trk.op(qV, lambda e, b=b: e.tensor_mul(
                    out=nx1[b][:], in0=g2[b][:], in1=cf32[b][:]),
                    reads=[g2[b], cf32[b]], writes=[nx1[b]])
                trk.op(qV, lambda e, b=b: e.tensor_mul(
                    out=nxb[b][:], in0=nx1[b][:], in1=m2b[b][:]),
                    reads=[nx1[b], m2b[b]], writes=[nxb[b]])
                if t == 1:
                    trk.op(qV, lambda e, b=b: e.tensor_scalar(
                        out=abuf[b][:, 0:1], in0=abuf[b][:, 0:1],
                        scalar1=fl[:, 0:1], scalar2=None, op0=ALU.mult),
                        reads=[abuf[b], fl], writes=[abuf[b]])
                    trk.op(qV, lambda e, b=b: e.tensor_scalar(
                        out=bsc[:, 0:1], in0=nx1[b][:, 0:1],
                        scalar1=fl[:, 1:2], scalar2=None, op0=ALU.mult),
                        reads=[nx1[b], fl], writes=[bsc])
                    trk.op(qV, lambda e, b=b: e.scalar_tensor_tensor(
                        out=nxb[b][:, 0:1], in0=nxb[b][:, 0:1],
                        scalar=fl[:, 0:1], in1=bsc[:, 0:1],
                        op0=ALU.mult, op1=ALU.add),
                        reads=[nxb[b], fl, bsc], writes=[nxb[b]])
                if t == 0:
                    trk.op(qV, lambda e, b=b: e.tensor_tensor_scan(
                        out=hbf[b][:], data0=abuf[b][:], data1=nxb[b][:],
                        initial=0.0, op0=ALU.mult, op1=ALU.add),
                        reads=[abuf[b], nxb[b]], writes=[hbf[b]])
                else:
                    trk.op(qV, lambda e, b=b, rt=rt: e.tensor_tensor_scan(
                        out=hbf[b][:], data0=abuf[b][:], data1=nxb[b][:],
                        initial=hcar[:, rt:rt + 1], op0=ALU.mult, op1=ALU.add),
                        reads=[abuf[b], nxb[b], hcar], writes=[hbf[b]])
                trk.op(qG, lambda e, b=b, rt=rt: e.tensor_copy(
                    out=hcar[:, rt:rt + 1], in_=hbf[b][:, TT - 1:TT]),
                    reads=[hbf[b]], writes=[hcar])
                if t >= 1:
                    trk.dma(qY, SHH[b], lambda e, t=t, rt=rt, b=b: e.dma_start(
                        out=hd[t - 1, rt], in_=hbf[b][:]), reads=[hbf[b]],
                        writes=[("hd", t - 1, rt)])
        qY.wait(SHH[0], SHH[0].v)
        qY.wait(SHH[1], SHH[1].v)
        flush_phase()
        pb_.close()

    LXC = [Ctr(es, nc, f"LXC{i}") for i in range(2)]
    LHH = [Ctr(es, nc, f"LHH{i}") for i in range(2)]
    LYY = [Ctr(es, nc, f"LYY{i}") for i in range(2)]
    SAC = [Ctr(es, nc, f"SAC{i}") for i in range(2)]
    SX2 = [Ctr(es, nc, f"SX2{i}") for i in range(2)]
    LWC = Ctr(es, nc, "LWC")

    if "C" in phases:
        wo = inp("wo", [D, D])
        yd = scratch("yd", [nt_real, NRT, 128, TT], F32)
        hd = scratch("hd", [nt_real, NRT, 128, TT], F32)
        accd = scratch("accd", [nt_real, NRT, 128, TT], F32)
        xn2d = scratch("xn2d", [nt_real, NRT, 128, TT], BF16)
        phase_begin()
        pc_ = ExitStack()
        wo_sb = pc_.enter_context(nc.sbuf_tensor("wo_sb", [128, KT * D], BF16))
        xc = [pc_.enter_context(nc.sbuf_tensor(f"xc{i}", [128, D], F32))
              for i in range(6)]
        hld = [pc_.enter_context(nc.sbuf_tensor(f"hld{i}", [128, TT], F32))
               for i in range(2)]
        yld = [pc_.enter_context(nc.sbuf_tensor(f"yld{i}", [128, TT], F32))
               for i in range(2)]
        hybf = [pc_.enter_context(nc.sbuf_tensor(f"hybf{i}", [128, NRT * TT], BF16))
                for i in range(2)]
        x1b = [pc_.enter_context(nc.sbuf_tensor(f"x1b{i}", [128, TT], F32))
               for i in range(2)]
        x1bf = [pc_.enter_context(nc.sbuf_tensor(f"x1bf{i}", [128, NRT * TT], BF16))
                for i in range(2)]
        x1sq = [pc_.enter_context(nc.sbuf_tensor(f"x1sq{i}", [128, TT], BF16))
                for i in range(2)]
        t2b = [pc_.enter_context(nc.sbuf_tensor(f"t2b{i}", [128, TT], F32))
               for i in range(2)]
        xn2b = [pc_.enter_context(nc.sbuf_tensor(f"xn2b{i}", [128, TT], BF16))
                for i in range(2)]
        stt = pc_.enter_context(nc.sbuf_tensor("stt", [1, 4 * TT], F32))
        psO = [pc_.enter_context(nc.psum_tensor(f"psO{i}", [128, TT], F32))
               for i in range(2)]
        psS1 = pc_.enter_context(nc.psum_tensor("psS1", [1, TT], F32))
        psS2 = pc_.enter_context(nc.psum_tensor("psS2", [1, TT], F32))
        psM = pc_.enter_context(nc.psum_tensor("psM", [128, TT], F32))
        psR = pc_.enter_context(nc.psum_tensor("psR", [128, TT], F32))

        trk.dma(qG, LWC, lambda e: e.dma_start(
            out=wo_sb[:].rearrange("p (kt m) -> p kt m", kt=KT),
            in_=wo.rearrange("(kt p) m -> p kt m", p=128)),
            reads=[wo], writes=[wo_sb])
        for t in range(nt_real):
            for c4 in range(4):
                xi = (4 * t + c4) % 6
                r0 = TT * (t + 1) + 128 * c4
                trk.dma(qY, LXC[xi % 2], lambda e, r0=r0, xi=xi: e.dma_start(
                    out=xc[xi][:], in_=xs[r0:r0 + 128, :]), writes=[xc[xi]])
            for rt in range(NRT):
                b = rt % 2
                trk.dma(qY, LHH[b], lambda e, t=t, rt=rt, b=b: e.dma_start(
                    out=hld[b][:], in_=hd[t, rt]), reads=[("hd", t, rt)],
                    writes=[hld[b]])
                trk.dma(qY, LYY[b], lambda e, t=t, rt=rt, b=b: e.dma_start(
                    out=yld[b][:], in_=yd[t, rt]), reads=[("yd", t, rt)],
                    writes=[yld[b]])
                trk.op(qV, lambda e, rt=rt, b=b, hp=t % 2: e.tensor_mul(
                    out=hybf[hp][:, rt * TT:(rt + 1) * TT], in0=hld[b][:],
                    in1=yld[b][:]), reads=[hld[b], yld[b]], writes=[hybf[t % 2]])
            for rt in range(NRT):
                b = rt % 2
                xis = [(4 * t + c4) % 6 for c4 in range(4)]
                def og(e, rt=rt, b=b, hp=t % 2, xis=tuple(xis)):
                    mm = None
                    for c4 in range(4):
                        mm = e.matmul(psO[b][:, c4 * 128:(c4 + 1) * 128],
                                      xc[xis[c4]][:, rt * 128:(rt + 1) * 128],
                                      ident[:], start=(c4 == 0), stop=False,
                                      is_transpose=True)
                    for kt in range(KT):
                        mm = e.matmul(psO[b][:],
                                      wo_sb[:, kt * D + rt * 128:kt * D + rt * 128 + 128],
                                      hybf[hp][:, kt * TT:(kt + 1) * TT],
                                      start=False, stop=(kt == KT - 1))
                    return mm
                trk.op(qT, og,
                       reads=[wo_sb, hybf[t % 2], ident] + [xc[x_] for x_ in xis],
                       writes=[psO[b]])
                trk.op(qS, lambda e, rt=rt, b=b: e.activation(
                    out=x1b[b][:], in_=psO[b][:], func=AF.Identity,
                    bias=pp[:, PC_BOUT + rt:PC_BOUT + rt + 1]),
                    reads=[psO[b], pp], writes=[x1b[b]])
                trk.dma(qY, SAC[b], lambda e, t=t, rt=rt, b=b: e.dma_start(
                    out=accd[t, rt], in_=x1b[b][:]), reads=[x1b[b]],
                    writes=[("accd", t, rt)])
                trk.op(qS, lambda e, rt=rt, b=b, hp=t % 2: e.activation(
                    out=x1bf[hp][:, rt * TT:(rt + 1) * TT], in_=x1b[b][:],
                    func=AF.Identity), reads=[x1b[b]], writes=[x1bf[t % 2]])
                trk.op(qS, lambda e, rt=rt, b=b: e.activation(
                    out=x1sq[b][:], in_=x1b[b][:], func=AF.Square),
                    reads=[x1b[b]], writes=[x1sq[b]])
                trk.op(qT, lambda e, rt=rt, hp=t % 2: e.matmul(
                    psS1[:], ones_col_bf[:], x1bf[hp][:, rt * TT:(rt + 1) * TT],
                    start=(rt == 0), stop=(rt == NRT - 1)),
                    reads=[ones_col_bf, x1bf[t % 2]], writes=[psS1])
                trk.op(qT, lambda e, rt=rt, b=b: e.matmul(
                    psS2[:], ones_col_bf[:], x1sq[b][:],
                    start=(rt == 0), stop=(rt == NRT - 1)),
                    reads=[ones_col_bf, x1sq[b]], writes=[psS2])
            trk.op(qV, lambda e: e.tensor_scalar(
                out=stt[0:1, 0:TT], in0=psS1[:], scalar1=1.0 / D, scalar2=None,
                op0=ALU.mult), reads=[psS1], writes=[stt])
            trk.op(qV, lambda e: e.tensor_scalar(
                out=stt[0:1, TT:2 * TT], in0=psS2[:], scalar1=1.0 / D,
                scalar2=None, op0=ALU.mult), reads=[psS2], writes=[stt])
            trk.op(qV, lambda e: e.tensor_mul(
                out=stt[0:1, 3 * TT:4 * TT], in0=stt[0:1, 0:TT],
                in1=stt[0:1, 0:TT]), reads=[stt], writes=[stt])
            trk.op(qV, lambda e: e.tensor_sub(
                out=stt[0:1, TT:2 * TT], in0=stt[0:1, TT:2 * TT],
                in1=stt[0:1, 3 * TT:4 * TT]), reads=[stt], writes=[stt])
            trk.op(qV, lambda e: e.tensor_scalar(
                out=stt[0:1, TT:2 * TT], in0=stt[0:1, TT:2 * TT],
                scalar1=1e-5, scalar2=None, op0=ALU.add),
                reads=[stt], writes=[stt])
            trk.op(qS, lambda e: e.activation(
                out=stt[0:1, 3 * TT:4 * TT], in_=stt[0:1, TT:2 * TT], func=AF.Ln),
                reads=[stt], writes=[stt])
            trk.op(qS, lambda e: e.activation(
                out=stt[0:1, 2 * TT:3 * TT], in_=stt[0:1, 3 * TT:4 * TT],
                func=AF.Exp, scale=-0.5), reads=[stt], writes=[stt])
            trk.op(qT, lambda e: e.matmul(
                psM[:], ones_row[:], stt[0:1, 0:TT], start=True, stop=True),
                reads=[ones_row, stt], writes=[psM])
            trk.op(qT, lambda e: e.matmul(
                psR[:], ones_row[:], stt[0:1, 2 * TT:3 * TT], start=True,
                stop=True), reads=[ones_row, stt], writes=[psR])
            for rt in range(NRT):
                b = rt % 2
                trk.op(qV, lambda e, rt=rt, b=b, hp=t % 2: e.tensor_sub(
                    out=t2b[b][:], in0=x1bf[hp][:, rt * TT:(rt + 1) * TT],
                    in1=psM[:]), reads=[x1bf[t % 2], psM], writes=[t2b[b]])
                trk.op(qV, lambda e, rt=rt, b=b: e.tensor_mul(
                    out=xn2b[b][:], in0=t2b[b][:], in1=psR[:]),
                    reads=[t2b[b], psR], writes=[xn2b[b]])
                trk.dma(qY, SX2[b], lambda e, t=t, rt=rt, b=b: e.dma_start(
                    out=xn2d[t, rt], in_=xn2b[b][:]), reads=[xn2b[b]],
                    writes=[("xn2d", t, rt)])
        qY.wait(SAC[0], SAC[0].v)
        qY.wait(SAC[1], SAC[1].v)
        qY.wait(SX2[0], SX2[0].v)
        qY.wait(SX2[1], SX2[1].v)
        flush_phase()
        pc_.close()

    LX20 = [Ctr(es, nc, f"LX20{i}") for i in range(2)]
    LGW = Ctr(es, nc, "LGW")
    SCB = Ctr(es, nc, "SCB")
    SCB1 = Ctr(es, nc, "SCB1")

    if "D0" in phases:
        gwT = inp("gwT", [D, 16])
        xn2d = scratch("xn2d", [nt_real, NRT, 128, TT], BF16)
        combd = scratch("combd", [NE, nt_real * TT], F32)
        phase_begin()
        pd0 = ExitStack()
        gw_sb = pd0.enter_context(nc.sbuf_tensor("gw_sb", [128, KT * 16], BF16))
        x2t = [pd0.enter_context(nc.sbuf_tensor(f"x2t{i}", [128, KT * TT], BF16))
               for i in range(2)]
        scf = pd0.enter_context(nc.sbuf_tensor("scf", [16, TT], F32))
        sct = pd0.enter_context(nc.sbuf_tensor("sct", [128, 64], F32))
        esc = pd0.enter_context(nc.sbuf_tensor("esc", [128, 16], F32))
        prb = pd0.enter_context(nc.sbuf_tensor("prb", [128, 16], F32))
        wrk = pd0.enter_context(nc.sbuf_tensor("wrk", [128, 16], F32))
        msc = pd0.enter_context(nc.sbuf_tensor("msc", [128, 16], F32))
        cmb = pd0.enter_context(nc.sbuf_tensor("cmb", [128, 16], F32))
        b0 = pd0.enter_context(nc.sbuf_tensor("b0", [128, 1], F32))
        b1 = pd0.enter_context(nc.sbuf_tensor("b1", [128, 1], F32))
        b2 = pd0.enter_context(nc.sbuf_tensor("b2", [128, 1], F32))
        ccrow = pd0.enter_context(nc.sbuf_tensor("ccrow", [16, 128], F32))
        onr = pd0.enter_context(nc.sbuf_tensor("onr", [1, nt_real * TT], F32))
        psGt = pd0.enter_context(nc.psum_tensor("psGt", [16, TT], F32))
        ptg = pd0.enter_context(nc.psum_tensor("ptg", [128, 64], F32))
        pcc = pd0.enter_context(nc.psum_tensor("pcc", [16, 128], F32))

        trk.dma(qG, LGW, lambda e: e.dma_start(
            out=gw_sb[:].rearrange("p (kt m) -> p kt m", kt=KT),
            in_=gwT.rearrange("(kt p) m -> p kt m", p=128)),
            reads=[gwT], writes=[gw_sb])
        trk.op(qV, lambda e: e.memset(onr[:], 1.0), writes=[onr])
        trk.dma(qY, SCB1, lambda e: e.dma_start(
            out=combd[16:17, :], in_=onr[:]), reads=[onr], writes=[("combd", 16)])
        trk.dma(qY, SCB1, lambda e: e.dma_start(
            out=combd[17:18, :], in_=onr[:]), reads=[onr], writes=[("combd", 17)],
            first=False)

        for t in range(nt_real):
            p2 = t % 2
            trk.dma(qY, LX20[p2], lambda e, t=t, p2=p2: e.dma_start(
                out=x2t[p2][:].rearrange("p (rt n) -> p rt n", rt=NRT),
                in_=xn2d[t].rearrange("rt p n -> p rt n")),
                reads=[("xn2d", t, r_) for r_ in range(NRT)], writes=[x2t[p2]])
            def gg(e, p2=p2):
                mm = None
                for kt in range(KT):
                    mm = e.matmul(psGt[:], gw_sb[:, kt * 16:(kt + 1) * 16],
                                  x2t[p2][:, kt * TT:(kt + 1) * TT],
                                  start=(kt == 0), stop=(kt == KT - 1))
                return mm
            trk.op(qT, gg, reads=[gw_sb, x2t[p2]], writes=[psGt])
            trk.op(qS, lambda e: e.activation(
                out=scf[:], in_=psGt[:], func=AF.Identity),
                reads=[psGt], writes=[scf])
            def tg(e):
                mm = None
                for c4 in range(4):
                    mm = e.transpose(ptg[:, c4 * 16:(c4 + 1) * 16],
                                     scf[0:16, c4 * 128:(c4 + 1) * 128],
                                     ident[0:16, 0:16])
                return mm
            trk.op(qT, tg, reads=[scf, ident], writes=[ptg])
            trk.op(qS, lambda e: e.activation(
                out=sct[:], in_=ptg[:], func=AF.Identity),
                reads=[ptg], writes=[sct])
            for c4 in range(4):
                ch = sct[:, c4 * 16:(c4 + 1) * 16]
                trk.op(qV, lambda e, ch=ch: e.reduce_max(
                    out=b0[:], in_=ch, axis=AXX, negate=True),
                    reads=[sct], writes=[b0])
                trk.op(qS, lambda e, ch=ch: e.activation(
                    out=esc[:], in_=ch, func=AF.Exp, bias=b0[:]),
                    reads=[sct, b0], writes=[esc])
                trk.op(qV, lambda e: e.reduce_sum(
                    out=b1[:], in_=esc[:], axis=AXX), reads=[esc], writes=[b1])
                trk.op(qV, lambda e: e.reciprocal(out=b2[:], in_=b1[:]),
                       reads=[b1], writes=[b2])
                trk.op(qV, lambda e: e.tensor_scalar(
                    out=prb[:], in0=esc[:], scalar1=b2[:], scalar2=None,
                    op0=ALU.mult), reads=[esc, b2], writes=[prb])
                trk.op(qV, lambda e: e.tensor_copy(out=wrk[:], in_=prb[:]),
                       reads=[prb], writes=[wrk])
                for it in range(5):
                    trk.op(qV, lambda e: e.reduce_max(
                        out=b0[:], in_=wrk[:], axis=AXX),
                        reads=[wrk], writes=[b0])
                    trk.op(qV, lambda e: e.tensor_scalar(
                        out=msc[:], in0=wrk[:], scalar1=b0[:], scalar2=2.0,
                        op0=ALU.is_ge, op1=ALU.mult),
                        reads=[wrk, b0], writes=[msc])
                    trk.op(qV, lambda e: e.tensor_sub(
                        out=wrk[:], in0=wrk[:], in1=msc[:]),
                        reads=[wrk, msc], writes=[wrk])
                trk.op(qV, lambda e: e.reduce_max(
                    out=b0[:], in_=wrk[:], axis=AXX), reads=[wrk], writes=[b0])
                trk.op(qV, lambda e: e.tensor_scalar(
                    out=msc[:], in0=prb[:], scalar1=b0[:], scalar2=None,
                    op0=ALU.is_ge), reads=[prb, b0], writes=[msc])
                trk.op(qV, lambda e: e.tensor_mul(
                    out=wrk[:], in0=prb[:], in1=msc[:]),
                    reads=[prb, msc], writes=[wrk])
                trk.op(qV, lambda e: e.reduce_sum(
                    out=b1[:], in_=wrk[:], axis=AXX), reads=[wrk], writes=[b1])
                trk.op(qV, lambda e: e.tensor_scalar(
                    out=b1[:], in0=b1[:], scalar1=1e-20, scalar2=None,
                    op0=ALU.add), reads=[b1], writes=[b1])
                trk.op(qV, lambda e: e.reciprocal(out=b2[:], in_=b1[:]),
                       reads=[b1], writes=[b2])
                trk.op(qV, lambda e: e.tensor_scalar(
                    out=cmb[:], in0=wrk[:], scalar1=b2[:], scalar2=None,
                    op0=ALU.mult), reads=[wrk, b2], writes=[cmb])
                trk.op(qT, lambda e: e.transpose(pcc[:], cmb[:], ident[:]),
                       reads=[cmb, ident], writes=[pcc])
                trk.op(qS, lambda e: e.activation(
                    out=ccrow[:], in_=pcc[:], func=AF.Identity),
                    reads=[pcc], writes=[ccrow])
                c0 = t * TT + c4 * 128
                trk.dma(qY, SCB, lambda e, c0=c0: e.dma_start(
                    out=combd[0:16, c0:c0 + 128], in_=ccrow[:]),
                    reads=[ccrow], writes=[("combd", t, c4)])
        qY.wait(SCB, SCB.v)
        flush_phase()
        pd0.close()

    LW13 = Ctr(es, nc, "LW13")
    LW13H = Ctr(es, nc, "LW13H")
    LW2D = Ctr(es, nc, "LW2D")
    LX2D = [Ctr(es, nc, f"LX2D{i}") for i in range(2)]
    LCR = [Ctr(es, nc, f"LCR{i}") for i in range(2)]
    SACCD = [Ctr(es, nc, f"SACCD{i}") for i in range(NRT)]

    if "D" in phases:
        w13p = inp("w13p", [NE, D, 2816])
        w2p = inp("w2p", [NE, IS, D])
        accd = scratch("accd", [nt_real, NRT, 128, TT], F32)
        xn2d = scratch("xn2d", [nt_real, NRT, 128, TT], BF16)
        combd = scratch("combd", [NE, nt_real * TT], F32)
        phase_begin()
        pd_ = ExitStack()
        w13_sb = pd_.enter_context(nc.sbuf_tensor("w13_sb", [128, KT * 2816], BF16))
        w2_sb = pd_.enter_context(nc.sbuf_tensor("w2_sb", [128, IC * D], BF16))
        x2d = [pd_.enter_context(nc.sbuf_tensor(f"x2d{i}", [128, KT * TT], BF16))
               for i in range(2)]
        s_sb = pd_.enter_context(nc.sbuf_tensor("s_sb", [128, IC * TT], BF16))
        sg = [pd_.enter_context(nc.sbuf_tensor(f"sg{i}", [128, TT], F32))
              for i in range(2)]
        ob = [pd_.enter_context(nc.sbuf_tensor(f"ob{i}", [128, TT], F32))
              for i in range(2)]
        cps = pd_.enter_context(nc.sbuf_tensor("cps", [128, TT], F32))
        crow = [pd_.enter_context(nc.sbuf_tensor(f"crow{i}", [1, TT], F32))
                for i in range(2)]
        psG = [pd_.enter_context(nc.psum_tensor(f"psG{i}", [128, TT], F32))
               for i in range(2)]
        psU = [pd_.enter_context(nc.psum_tensor(f"psU{i}", [128, TT], F32))
               for i in range(2)]
        psC = pd_.enter_context(nc.psum_tensor("psC", [128, TT], F32))
        psO2 = [pd_.enter_context(nc.psum_tensor(f"psO2{i}", [128, TT], F32))
                for i in range(2)]

        LOCH = 6

        def load_w13(s, first=False, half=None):
            wv = w13_sb[:].rearrange("p (kt m) -> p kt m", kt=KT)
            sv = w13p[s].rearrange("(kt p) m -> p kt m", p=128)
            if half in (None, "lo"):
                trk.dma(qG, LW13, lambda e: e.dma_start(
                    out=wv[:, :, 0:LOCH * 128], in_=sv[:, :, 0:LOCH * 128]),
                    reads=[w13p], writes=[("w13L",)])
                trk.dma(qG, LW13, lambda e: e.dma_start(
                    out=wv[:, :, IS:IS + LOCH * 128],
                    in_=sv[:, :, IS:IS + LOCH * 128]),
                    reads=[w13p], writes=[("w13L",)], first=False)
            if half in (None, "hi"):
                trk.dma(qG, LW13H, lambda e: e.dma_start(
                    out=wv[:, :, LOCH * 128:IS], in_=sv[:, :, LOCH * 128:IS]),
                    reads=[w13p], writes=[("w13H",)])
                trk.dma(qG, LW13H, lambda e: e.dma_start(
                    out=wv[:, :, IS + LOCH * 128:2 * IS],
                    in_=sv[:, :, IS + LOCH * 128:2 * IS]),
                    reads=[w13p], writes=[("w13H",)], first=False)

        for s in range(NE):
            for t in range(nt_real):
                p2 = t % 2
                trk.dma(qY, LX2D[p2], lambda e, t=t, p2=p2: e.dma_start(
                    out=x2d[p2][:].rearrange("p (rt n) -> p rt n", rt=NRT),
                    in_=xn2d[t].rearrange("rt p n -> p rt n")),
                    reads=[("xn2d", t, r_) for r_ in range(NRT)],
                    writes=[x2d[p2]])
                if s < 16:
                    crdeps = [("combd", t, c4_) for c4_ in range(4)]
                else:
                    crdeps = [("combd", s)]
                trk.dma(qY, LCR[p2], lambda e, s=s, t=t, p2=p2: e.dma_start(
                    out=crow[p2][:], in_=combd[s:s + 1, t * TT:(t + 1) * TT]),
                    reads=crdeps, writes=[crow[p2]])
                if s == 0 and t == 0:
                    load_w13(s, first=True)
                    trk.dma(qG, LW2D, lambda e, s=s: e.dma_start(
                        out=w2_sb[:].rearrange("p (kt m) -> p kt m", kt=IC),
                        in_=w2p[s].rearrange("(kt p) m -> p kt m", p=128)),
                        reads=[w2p], writes=[w2_sb])
                trk.op(qT, lambda e, p2=p2: e.matmul(
                    psC[:], ones_row[:], crow[p2][:], start=True, stop=True),
                    reads=[ones_row, crow[p2]], writes=[psC])
                trk.op(qV, lambda e: e.tensor_copy(out=cps[:], in_=psC[:]),
                       reads=[psC], writes=[cps])
                for c in range(IC):
                    b = c % 2

                    def g1g(e, p2=p2, c=c, b=b):
                        mm = None
                        for kt in range(KT):
                            mm = e.matmul(
                                psG[b][:],
                                w13_sb[:, kt * 2816 + c * 128:kt * 2816 + c * 128 + 128],
                                x2d[p2][:, kt * TT:(kt + 1) * TT],
                                start=(kt == 0), stop=(kt == KT - 1))
                        return mm

                    def u1g(e, p2=p2, c=c, b=b):
                        mm = None
                        for kt in range(KT):
                            mm = e.matmul(
                                psU[b][:],
                                w13_sb[:, kt * 2816 + IS + c * 128:kt * 2816 + IS + c * 128 + 128],
                                x2d[p2][:, kt * TT:(kt + 1) * TT],
                                start=(kt == 0), stop=(kt == KT - 1))
                        return mm
                    wkey = ("w13L",) if c < LOCH else ("w13H",)
                    trk.op(qT, g1g, reads=[wkey, x2d[p2]], writes=[psG[b]])
                    trk.op(qT, u1g, reads=[wkey, x2d[p2]], writes=[psU[b]])
                    if (t == nt_real - 1 and c == LOCH - 1 and s + 1 < NE):
                        load_w13(s + 1, half="lo")
                    trk.op(qS, lambda e, b=b: e.activation(
                        out=sg[b][:], in_=psG[b][:], func=AF.Silu),
                        reads=[psG[b]], writes=[sg[b]])
                    trk.op(qV, lambda e, b=b, c=c: e.tensor_mul(
                        out=s_sb[:, c * TT:(c + 1) * TT], in0=sg[b][:],
                        in1=psU[b][:]), reads=[sg[b], psU[b]], writes=[s_sb])
                if t == nt_real - 1 and s + 1 < NE:
                    load_w13(s + 1, half="hi")
                for rt in range(NRT):
                    b = rt % 2

                    def o2g(e, rt=rt, b=b):
                        mm = None
                        for kt in range(IC):
                            mm = e.matmul(
                                psO2[b][:],
                                w2_sb[:, kt * D + rt * 128:kt * D + rt * 128 + 128],
                                s_sb[:, kt * TT:(kt + 1) * TT],
                                start=(kt == 0), stop=(kt == IC - 1))
                        return mm
                    trk.op(qT, o2g, reads=[w2_sb, s_sb], writes=[psO2[b]])
                    trk.op(qV, lambda e, b=b: e.tensor_mul(
                        out=ob[b][:], in0=cps[:], in1=psO2[b][:]),
                        reads=[cps, psO2[b]], writes=[ob[b]])
                    trk.dma(qG, SACCD[rt], lambda e, t=t, rt=rt, b=b: e.dma_start(
                        out=accd[t, rt], in_=ob[b][:], accum_op=ALU.add),
                        reads=[ob[b]], writes=[("accd", t, rt)])
                if t == nt_real - 1 and s + 1 < NE:
                    trk.dma(qG, LW2D, lambda e, s=s: e.dma_start(
                        out=w2_sb[:].rearrange("p (kt m) -> p kt m", kt=IC),
                        in_=w2p[s + 1].rearrange("(kt p) m -> p kt m", p=128)),
                        reads=[w2p], writes=[w2_sb])
        flush_phase()
        pd_.close()

    LAE = [Ctr(es, nc, f"LAE{i}") for i in range(2)]
    SOUT = [Ctr(es, nc, f"SOUT{i}") for i in range(2)]

    if "E" in phases:
        accd = scratch("accd", [nt_real, NRT, 128, TT], F32)
        out = nc.dram_tensor("out", [nt_real * TT, D], F32, kind="ExternalOutput").ap()
        phase_begin()
        pe_ = ExitStack()
        aee = [pe_.enter_context(nc.sbuf_tensor(f"aee{i}", [128, NRT * TT], F32))
               for i in range(2)]
        oy = [pe_.enter_context(nc.sbuf_tensor(f"oy{i}", [128, D], F32))
              for i in range(2)]
        psE = [pe_.enter_context(nc.psum_tensor(f"psE{i}", [128, TT], F32))
               for i in range(2)]

        for t in range(nt_real):
            p2 = t % 2
            trk.dma(qY, LAE[p2], lambda e, t=t, p2=p2: e.dma_start(
                out=aee[p2][:].rearrange("p (rt n) -> p rt n", rt=NRT),
                in_=accd[t].rearrange("rt p n -> p rt n")),
                reads=[("accd", t, r_) for r_ in range(NRT)], writes=[aee[p2]])
            for c4 in range(4):
                pc2 = c4 % 2
                for g4 in range(4):

                    def tpe(e, p2=p2, c4=c4, g4=g4, pc2=pc2):
                        mm = None
                        for j in range(4):
                            rt = g4 * 4 + j
                            mm = e.transpose(
                                psE[pc2][:, j * 128:(j + 1) * 128],
                                aee[p2][:, rt * TT + c4 * 128:rt * TT + c4 * 128 + 128],
                                ident[:])
                        return mm
                    trk.op(qT, tpe, reads=[aee[p2], ident], writes=[psE[pc2]])
                    trk.op(qS, lambda e, g4=g4, pc2=pc2: e.activation(
                        out=oy[pc2][:, g4 * TT:(g4 + 1) * TT], in_=psE[pc2][:],
                        func=AF.Identity), reads=[psE[pc2]], writes=[oy[pc2]])
                r0 = t * TT + c4 * 128
                trk.dma(qY, SOUT[pc2], lambda e, r0=r0, pc2=pc2: e.dma_start(
                    out=out[r0:r0 + 128, :], in_=oy[pc2][:]),
                    reads=[oy[pc2]], writes=[("out", r0)])
        flush_phase()
        pe_.close()

    for s in sorted(trk.dma_sems, key=lambda c: c.name):
        if s.v > 0:
            qG.wait(s, s.v)
    with nc.Block() as blk:
        @blk.gpsimd
        def _(e):
            qG.replay(e)

    es.close()
    return nc


def _pack_inputs(inputs, nt_real=8, n_cores=8, names=None):
    f32 = np.float32
    x = np.ascontiguousarray(np.asarray(inputs["L0_attn_x"]).reshape(32768, D))
    chunk = 4096
    warm = TT
    in_maps = []

    ppk = np.zeros((128, PC_N), f32)

    def col16(v):
        return np.ascontiguousarray(np.asarray(v).reshape(16, 128).T)

    ppk[:, PC_BY:PC_BY + 16] = col16(inputs["L0_attn_b_y"])
    ppk[:, PC_BX:PC_BX + 16] = col16(inputs["L0_attn_b_x"])
    ppk[:, PC_BIG:PC_BIG + 16] = col16(inputs["L0_attn_b_ig"])
    ppk[:, PC_BAG:PC_BAG + 16] = col16(inputs["L0_attn_b_ag"])
    ppk[:, PC_BOUT:PC_BOUT + 16] = col16(inputs["L0_attn_b_out"])
    ppk[:, PC_CB:PC_CB + 16] = col16(inputs["L0_attn_conv_b"])
    cw = np.asarray(inputs["L0_attn_conv_w"]).reshape(4, 16, 128).transpose(2, 1, 0)
    ppk[:, PC_CW:PC_CW + 64] = np.ascontiguousarray(cw.reshape(128, 64))
    ppk[:, PC_AP:PC_AP + 16] = col16(inputs["L0_attn_a_param"])

    gwT_ = np.ascontiguousarray(np.asarray(inputs["L0_ffn_gate_w"]).T)

    w13 = np.asarray(inputs["L0_ffn_w13"])
    sw13T = np.ascontiguousarray(np.asarray(inputs["L0_ffn_sw13"]).T)
    w2 = np.asarray(inputs["L0_ffn_w2"])
    sw2T = np.ascontiguousarray(np.asarray(inputs["L0_ffn_sw2"]).T)
    w13p_ = np.empty((NE, D, 2816), f32)
    w13p_[0:16] = w13.transpose(0, 2, 1)
    w13p_[16] = np.concatenate([sw13T[:, 0:IS], sw13T[:, 2 * IS:3 * IS]], axis=1)
    w13p_[17] = np.concatenate([sw13T[:, IS:2 * IS], sw13T[:, 3 * IS:4 * IS]], axis=1)
    w2p_ = np.empty((NE, IS, D), f32)
    w2p_[0:16] = w2.transpose(0, 2, 1)
    w2p_[16] = sw2T[0:IS]
    w2p_[17] = sw2T[IS:2 * IS]

    common = {
        "ppack": ppk, "gwT": gwT_, "w13p": w13p_, "w2p": w2p_,
        "wy": np.asarray(inputs["L0_attn_w_y"]),
        "wx": np.asarray(inputs["L0_attn_w_x"]),
        "wig": np.asarray(inputs["L0_attn_w_ig"]),
        "wag": np.asarray(inputs["L0_attn_w_ag"]),
        "wo": np.asarray(inputs["L0_attn_w_out"]),
    }
    if names is not None:
        common = {k: v for k, v in common.items() if k in names}
    for c in range(n_cores):
        lo = chunk * c
        if c % 4 == 0:
            sh = np.concatenate([np.zeros((warm, D), f32), x[lo:lo + nt_real * TT]],
                                axis=0)
            flag = 0.0
        else:
            sh = np.ascontiguousarray(x[lo - warm:lo + nt_real * TT])
            flag = 1.0
        m = dict(common)
        m["xs"] = sh
        m["flags"] = np.tile(np.array([[flag, 1.0 - flag]], f32), (128, 1))
        in_maps.append(m)
    return in_maps


def kernel(**inputs):
    n_cores = 8
    nc = build_program()
    in_maps = _pack_inputs(inputs, nt_real=8, n_cores=n_cores)
    res = run_bass_kernel_spmd(nc, in_maps, core_ids=list(range(n_cores)))
    out = np.concatenate([res.results[c]["out"] for c in range(n_cores)], axis=0)
    return out.reshape(2, 16384, D)


# version 06
# speedup vs baseline: 1.0419x; 1.0051x over previous
import numpy as np
import ml_dtypes
from contextlib import ExitStack

import concourse.bass as bass
import concourse.mybir as mybir
from concourse.bass_utils import run_bass_kernel_spmd

F32 = mybir.dt.float32
BF16 = mybir.dt.bfloat16
AF = mybir.ActivationFunctionType
ALU = mybir.AluOpType
AXX = mybir.AxisListType.X

D = 2048
NRT = 16
TT = 512
KT = 16
IC = 11
NE = 18
IS = 1408

PC_BY, PC_BX, PC_BIG, PC_BAG, PC_BOUT, PC_CB = 0, 16, 32, 48, 64, 80
PC_CW = 96
PC_AP = 160
PC_N = 176


class Ctr:
    def __init__(self, es, nc, name):
        self.h = es.enter_context(nc.semaphore(name))
        self.v = 0
        self.name = name


def _key(ap):
    if isinstance(ap, (str, tuple)):
        return ap
    t = getattr(ap, "tensor", None)
    return t.name if t is not None else str(ap)


class Q:
    def __init__(self, name, prog):
        self.name = name
        self.prog = prog
        self.waited = {}
        self.pending = []

    def wait(self, ctr, val):
        if val <= 0:
            return
        if self.waited.get(id(ctr), -1) >= val:
            return
        self.pending.append(lambda e, h=ctr.h, v=val: e.wait_ge(h, v))
        self.waited[id(ctr)] = val

    def raw(self, fn):
        self.pending.append(fn)

    def replay(self, eng):
        for fn in self.pending:
            fn(eng)
        self.pending = []


class Trk:
    def __init__(self):
        self.writer = {}
        self.readers = {}
        self.dma_sems = set()
        self.dma_q = {}

    def _pre(self, q, reads, writes):
        for ap in reads:
            w = self.writer.get(_key(ap))
            if w:
                q.wait(w[0], w[1])
        for ap in writes:
            k = _key(ap)
            w = self.writer.get(k)
            if w:
                q.wait(w[0], w[1])
            for (ctr, val) in self.readers.get(k, {}).values():
                q.wait(ctr, val)

    def _post(self, q, reads, writes, ctr, val):
        for ap in writes:
            k = _key(ap)
            self.writer[k] = (ctr, val)
            self.readers[k] = {}
        for ap in reads:
            self.readers.setdefault(_key(ap), {})[q.name] = (ctr, val)

    def op(self, q, fn, reads=(), writes=(), waits=()):
        self._pre(q, reads, writes)
        for (ctr, val) in waits:
            q.wait(ctr, val)
        q.prog.v += 1
        sh, sv = q.prog.h, q.prog.v
        q.raw(lambda e: fn(e).then_inc(sh, 1))
        self._post(q, reads, writes, q.prog, q.prog.v)

    def dma(self, q, sem, fn, reads=(), writes=(), waits=(), first=True,
            no_drain=False):
        self.dma_sems.add(sem)
        if not no_drain:
            self.dma_q[sem] = q
        if first and sem.v > 0:
            q.wait(sem, sem.v)
        self._pre(q, reads, writes)
        for (ctr, val) in waits:
            q.wait(ctr, val)
        sem.v += 16
        sh = sem.h
        q.raw(lambda e: fn(e).then_inc(sh, 16))
        self._post(q, reads, writes, sem, sem.v)


def build_program(nt_real=8, phases=("0", "A", "B", "C", "D0", "D", "E"), dbg=()):
    nwt = nt_real + 1
    rows = nwt * TT
    nc = bass.Bass("TRN2", target_bir_lowering=False)

    _dram = {}

    def inp(name, shape):
        if name not in _dram:
            _dram[name] = nc.dram_tensor(name, shape, F32, kind="ExternalInput").ap()
        return _dram[name]

    def scratch(name, shape, dt):
        if name not in _dram:
            kind = "ExternalOutput" if name in dbg else "Internal"
            _dram[name] = nc.dram_tensor(name, shape, dt, kind=kind).ap()
        return _dram[name]

    xs = inp("xs", [rows, D])
    flags = inp("flags", [128, 2])
    ppack = inp("ppack", [128, PC_N])

    es = ExitStack()
    pp = es.enter_context(nc.sbuf_tensor("pp", [128, PC_N], F32))
    fl = es.enter_context(nc.sbuf_tensor("fl", [128, 2], F32))
    sl = es.enter_context(nc.sbuf_tensor("sl", [128, 48], F32))
    nbias = es.enter_context(nc.sbuf_tensor("nbias", [128, 32], F32))
    cst = es.enter_context(nc.sbuf_tensor("cst", [128, 8], F32))
    ident = es.enter_context(nc.sbuf_tensor("ident", [128, 128], F32))
    ones_col_bf = es.enter_context(nc.sbuf_tensor("ones_col_bf", [128, 1], BF16))
    ones_row = es.enter_context(nc.sbuf_tensor("ones_row", [1, 128], F32))
    C_ONE, C_EPS, C_INVD, C_MONE, C_TINY = 0, 1, 2, 3, 4

    trk = Trk()
    PB = Ctr(es, nc, "PB")
    NENG = 5
    TEP = Ctr(es, nc, "TEP")
    VEP = Ctr(es, nc, "VEP")
    SEP = Ctr(es, nc, "SEP")
    GPP = Ctr(es, nc, "GPP")
    qT = Q("T", TEP)
    qV = Q("V", VEP)
    qS = Q("S", SEP)
    qG = Q("G", GPP)
    qY = Q("Y", None)
    ALLQ = [qT, qV, qS, qG, qY]

    def flush_phase():
        for sem, q in trk.dma_q.items():
            q.wait(sem, sem.v)
        for q in ALLQ:
            q.raw(lambda e: e.sem_inc(PB.h, 1))
        PB.v += NENG
        with nc.Block() as blk:
            @blk.tensor
            def _(e):
                qT.replay(e)

            @blk.vector
            def _(e):
                qV.replay(e)

            @blk.scalar
            def _(e):
                qS.replay(e)

            @blk.gpsimd
            def _(e):
                qG.replay(e)

            @blk.sync
            def _(e):
                qY.replay(e)

    def phase_begin():
        pbw = PB.v
        if pbw > 0:
            for q in ALLQ:
                q.raw(lambda e, v=pbw: e.wait_ge(PB.h, v))

    L0X = [Ctr(es, nc, f"L0X{i}") for i in range(4)]
    S0X = [Ctr(es, nc, f"S0X{i}") for i in range(4)]
    LPP = Ctr(es, nc, "LPP")
    LFL = Ctr(es, nc, "LFL")
    LWA = [Ctr(es, nc, f"LWA{i}") for i in range(2)]
    LXT = [Ctr(es, nc, f"LXT{i}") for i in range(3)]
    SYY = [Ctr(es, nc, f"SYY{i}") for i in range(2)]
    SCV = [Ctr(es, nc, f"SCV{i}") for i in range(2)]

    n0 = rows // 128

    if "0" in phases:
        xn1 = scratch("xn1", [rows, D], BF16)
        phase_begin()
        p0 = ExitStack()
        NP0 = 4
        xld = [p0.enter_context(nc.sbuf_tensor(f"xld{i}", [128, D], F32)) for i in range(NP0)]
        sqs = [p0.enter_context(nc.sbuf_tensor(f"sqs{i}", [128, D], F32)) for i in range(NP0)]
        xnb = [p0.enter_context(nc.sbuf_tensor(f"xnb{i}", [128, D], BF16)) for i in range(NP0)]
        st0 = [p0.enter_context(nc.sbuf_tensor(f"st0{i}", [128, 8], F32)) for i in range(NP0)]

        trk.dma(qY, LPP, lambda e: e.dma_start(out=pp[:], in_=ppack[:]),
                reads=[ppack], writes=[pp])
        trk.dma(qY, LFL, lambda e: e.dma_start(out=fl[:], in_=flags[:]),
                reads=[flags], writes=[fl])
        trk.op(qV, lambda e: e.memset(ident[:], 0.0), writes=[ident])
        trk.op(qG, lambda e: e.affine_select(
            out=ident[:], in_=ident[:], compare_op=ALU.not_equal,
            fill=1.0, base=0, pattern=[[-1, 128]], channel_multiplier=1),
            reads=[ident], writes=[ident])
        trk.op(qV, lambda e: e.memset(ones_col_bf[:], 1.0), writes=[ones_col_bf])
        trk.op(qV, lambda e: e.memset(ones_row[:], 1.0), writes=[ones_row])
        for col, val in ((C_ONE, 1.0), (C_EPS, 1e-5), (C_INVD, 1.0 / D),
                         (C_MONE, -1.0), (C_TINY, 1e-20)):
            trk.op(qV, lambda e, col=col, val=val: e.memset(
                cst[:, col:col + 1], val), writes=[cst])
        trk.op(qV, lambda e: e.tensor_scalar(
            out=nbias[:, 0:16], in0=pp[:, PC_BIG:PC_BIG + 16],
            scalar1=-1.0, scalar2=None, op0=ALU.mult), reads=[pp], writes=[nbias])
        trk.op(qV, lambda e: e.tensor_scalar(
            out=nbias[:, 16:32], in0=pp[:, PC_BAG:PC_BAG + 16],
            scalar1=-1.0, scalar2=None, op0=ALU.mult), reads=[pp], writes=[nbias])
        trk.op(qS, lambda e: e.activation(
            out=sl[:, 16:32], in_=pp[:, PC_AP:PC_AP + 16], func=AF.Abs),
            reads=[pp], writes=[sl])
        trk.op(qS, lambda e: e.activation(
            out=sl[:, 16:32], in_=sl[:, 16:32], func=AF.Exp, scale=-1.0),
            reads=[sl], writes=[sl])
        trk.op(qS, lambda e: e.activation(
            out=sl[:, 16:32], in_=sl[:, 16:32], func=AF.Ln,
            bias=cst[:, C_ONE:C_ONE + 1]), reads=[sl, cst], writes=[sl])
        trk.op(qV, lambda e: e.tensor_scalar(
            out=sl[:, 0:16], in0=pp[:, PC_AP:PC_AP + 16],
            scalar1=0.0, scalar2=None, op0=ALU.max), reads=[pp, sl], writes=[sl])
        trk.op(qV, lambda e: e.tensor_add(
            out=sl[:, 32:48], in0=sl[:, 0:16], in1=sl[:, 16:32]),
            reads=[sl], writes=[sl])
        trk.op(qV, lambda e: e.tensor_scalar(
            out=sl[:, 0:16], in0=sl[:, 32:48],
            scalar1=-8.0, scalar2=None, op0=ALU.mult), reads=[sl], writes=[sl])
        trk.op(qV, lambda e: e.tensor_scalar(
            out=sl[:, 16:32], in0=sl[:, 32:48],
            scalar1=-16.0, scalar2=None, op0=ALU.mult), reads=[sl], writes=[sl])
        for i in range(n0):
            p = i % 4
            trk.dma(qY, L0X[p], lambda e, i=i, p=p: e.dma_start(
                out=xld[p][:], in_=xs[128 * i:128 * (i + 1), :]), writes=[xld[p]])
            trk.op(qS, lambda e, p=p: e.activation(
                out=sqs[p][:], in_=xld[p][:], func=AF.Square,
                accum_out=st0[p][:, 1:2]),
                reads=[xld[p]], writes=[sqs[p], st0[p]])
            trk.op(qV, lambda e, p=p: e.reduce_sum(
                out=st0[p][:, 0:1], in_=xld[p][:], axis=AXX),
                reads=[xld[p]], writes=[st0[p]])
            trk.op(qV, lambda e, p=p: e.tensor_scalar(
                out=st0[p][:, 2:3], in0=st0[p][:, 0:1],
                scalar1=1.0 / D, scalar2=None, op0=ALU.mult),
                reads=[st0[p]], writes=[st0[p]])
            trk.op(qV, lambda e, p=p: e.tensor_scalar(
                out=st0[p][:, 3:4], in0=st0[p][:, 1:2],
                scalar1=1.0 / D, scalar2=None, op0=ALU.mult),
                reads=[st0[p]], writes=[st0[p]])
            trk.op(qV, lambda e, p=p: e.tensor_tensor(
                out=st0[p][:, 4:5], in0=st0[p][:, 2:3], in1=st0[p][:, 2:3],
                op=ALU.mult), reads=[st0[p]], writes=[st0[p]])
            trk.op(qV, lambda e, p=p: e.tensor_sub(
                out=st0[p][:, 3:4], in0=st0[p][:, 3:4], in1=st0[p][:, 4:5]),
                reads=[st0[p]], writes=[st0[p]])
            trk.op(qV, lambda e, p=p: e.tensor_scalar(
                out=st0[p][:, 3:4], in0=st0[p][:, 3:4],
                scalar1=1e-5, scalar2=None, op0=ALU.add),
                reads=[st0[p]], writes=[st0[p]])
            trk.op(qS, lambda e, p=p: e.activation(
                out=st0[p][:, 4:5], in_=st0[p][:, 3:4], func=AF.Ln),
                reads=[st0[p]], writes=[st0[p]])
            trk.op(qS, lambda e, p=p: e.activation(
                out=st0[p][:, 5:6], in_=st0[p][:, 4:5], func=AF.Exp, scale=-0.5),
                reads=[st0[p]], writes=[st0[p]])
            trk.op(qV, lambda e, p=p: e.tensor_scalar(
                out=xnb[p][:], in0=xld[p][:],
                scalar1=st0[p][:, 2:3], scalar2=st0[p][:, 5:6],
                op0=ALU.subtract, op1=ALU.mult),
                reads=[xld[p], st0[p]], writes=[xnb[p]])
            trk.dma(qY, S0X[p], lambda e, i=i, p=p: e.dma_start(
                out=xn1[128 * i:128 * (i + 1), :], in_=xnb[p][:]),
                reads=[xnb[p]], writes=[("xn1", i)])
        flush_phase()
        p0.close()

    if "A" in phases:
        wy = inp("wy", [D, D])
        wx = inp("wx", [D, D])
        xn1 = scratch("xn1", [rows, D], BF16)
        convd = scratch("convd", [nwt, NRT, 128, TT], F32)
        yd = scratch("yd", [nt_real, NRT, 128, TT], F32)
        phase_begin()
        pa = ExitStack()
        wy_sb = pa.enter_context(nc.sbuf_tensor("wy_sb", [128, KT * D], BF16))
        wx_sb = pa.enter_context(nc.sbuf_tensor("wx_sb", [128, KT * D], BF16))
        xnt = [pa.enter_context(nc.sbuf_tensor(f"xnt{i}", [128, KT * TT], BF16))
               for i in range(3)]
        ybuf = [pa.enter_context(nc.sbuf_tensor(f"ybuf{i}", [128, TT], F32))
                for i in range(2)]
        xbx = [pa.enter_context(nc.sbuf_tensor(f"xbx{i}", [128, TT + 3], F32))
               for i in range(2)]
        cv1 = [pa.enter_context(nc.sbuf_tensor(f"cv1{i}", [128, TT], F32))
               for i in range(2)]
        cv2 = [pa.enter_context(nc.sbuf_tensor(f"cv2{i}", [128, TT], F32))
               for i in range(2)]
        carr = pa.enter_context(nc.sbuf_tensor("carr", [128, NRT * 3], F32))
        psY = [pa.enter_context(nc.psum_tensor(f"psY{i}", [128, TT], F32))
               for i in range(2)]
        psX = [pa.enter_context(nc.psum_tensor(f"psX{i}", [128, TT], F32))
               for i in range(2)]

        trk.dma(qG, LWA[0], lambda e: e.dma_start(
            out=wy_sb[:].rearrange("p (kt m) -> p kt m", kt=KT),
            in_=wy.rearrange("(kt p) m -> p kt m", p=128)),
            reads=[wy], writes=[wy_sb])
        trk.dma(qG, LWA[1], lambda e: e.dma_start(
            out=wx_sb[:].rearrange("p (kt m) -> p kt m", kt=KT),
            in_=wx.rearrange("(kt p) m -> p kt m", p=128)),
            reads=[wx], writes=[wx_sb])
        trk.op(qV, lambda e: e.memset(carr[:], 0.0), writes=[carr])

        def mmgroup(ps, wsb, rt, src, n=KT, w=TT):
            def fn(e):
                mm = None
                for kt in range(n):
                    mm = e.matmul(ps[:],
                                  wsb[:, kt * D + rt * 128:kt * D + rt * 128 + 128],
                                  src[:, kt * w:(kt + 1) * w],
                                  start=(kt == 0), stop=(kt == n - 1))
                return mm
            return fn

        for t in range(nwt):
            p3 = t % 3
            for rt in range(NRT):
                trk.dma(qY, LXT[p3], lambda e, t=t, rt=rt, p3=p3: e.dma_start_transpose(
                    out=xnt[p3][:, rt * TT:(rt + 1) * TT],
                    in_=xn1[TT * t:TT * (t + 1), 128 * rt:128 * (rt + 1)]),
                    reads=[("xn1", j) for j in range(4 * t, 4 * t + 4)],
                    writes=[xnt[p3]], first=(rt == 0))
            for rt in range(NRT):
                b = rt % 2
                if t >= 1:
                    trk.op(qT, mmgroup(psY[b], wy_sb, rt, xnt[p3]),
                           reads=[wy_sb, xnt[p3]], writes=[psY[b]])
                    trk.op(qS, lambda e, rt=rt, b=b: e.activation(
                        out=ybuf[b][:], in_=psY[b][:], func=AF.Gelu,
                        bias=pp[:, PC_BY + rt:PC_BY + rt + 1]),
                        reads=[psY[b], pp], writes=[ybuf[b]])
                    trk.dma(qY, SYY[b], lambda e, t=t, rt=rt, b=b: e.dma_start(
                        out=yd[t - 1, rt], in_=ybuf[b][:]), reads=[ybuf[b]],
                        writes=[("yd", t - 1, rt)])
                trk.op(qT, mmgroup(psX[b], wx_sb, rt, xnt[p3]),
                       reads=[wx_sb, xnt[p3]], writes=[psX[b]])
                trk.op(qV, lambda e, rt=rt, b=b: e.tensor_copy(
                    out=xbx[b][:, 0:3], in_=carr[:, 3 * rt:3 * rt + 3]),
                    reads=[carr], writes=[xbx[b]])
                trk.op(qS, lambda e, rt=rt, b=b: e.activation(
                    out=xbx[b][:, 3:TT + 3], in_=psX[b][:], func=AF.Identity,
                    bias=pp[:, PC_BX + rt:PC_BX + rt + 1]),
                    reads=[psX[b], pp], writes=[xbx[b]])
                cwc = PC_CW + 4 * rt
                trk.op(qV, lambda e, b=b, cwc=cwc: e.tensor_scalar(
                    out=cv1[b][:], in0=xbx[b][:, 0:TT],
                    scalar1=pp[:, cwc:cwc + 1], scalar2=None, op0=ALU.mult),
                    reads=[xbx[b], pp], writes=[cv1[b]])
                trk.op(qV, lambda e, b=b, cwc=cwc: e.scalar_tensor_tensor(
                    out=cv2[b][:], in0=xbx[b][:, 1:TT + 1],
                    scalar=pp[:, cwc + 1:cwc + 2], in1=cv1[b][:],
                    op0=ALU.mult, op1=ALU.add),
                    reads=[xbx[b], pp, cv1[b]], writes=[cv2[b]])
                trk.op(qV, lambda e, b=b, cwc=cwc: e.scalar_tensor_tensor(
                    out=cv1[b][:], in0=xbx[b][:, 2:TT + 2],
                    scalar=pp[:, cwc + 2:cwc + 3], in1=cv2[b][:],
                    op0=ALU.mult, op1=ALU.add),
                    reads=[xbx[b], pp, cv2[b]], writes=[cv1[b]])
                trk.op(qV, lambda e, b=b, cwc=cwc: e.scalar_tensor_tensor(
                    out=cv2[b][:], in0=xbx[b][:, 3:TT + 3],
                    scalar=pp[:, cwc + 3:cwc + 4], in1=cv1[b][:],
                    op0=ALU.mult, op1=ALU.add),
                    reads=[xbx[b], pp, cv1[b]], writes=[cv2[b]])
                trk.op(qV, lambda e, b=b, rt=rt: e.tensor_scalar(
                    out=cv2[b][:], in0=cv2[b][:],
                    scalar1=pp[:, PC_CB + rt:PC_CB + rt + 1],
                    scalar2=None, op0=ALU.add),
                    reads=[cv2[b], pp], writes=[cv2[b]])
                trk.op(qV, lambda e, b=b, rt=rt: e.tensor_copy(
                    out=carr[:, 3 * rt:3 * rt + 3], in_=xbx[b][:, TT:TT + 3]),
                    reads=[xbx[b]], writes=[carr])
                trk.dma(qG, SCV[b], lambda e, t=t, rt=rt, b=b: e.dma_start(
                    out=convd[t, rt], in_=cv2[b][:]), reads=[cv2[b]],
                    writes=[("convd", t, rt)])
        qY.wait(SYY[0], SYY[0].v)
        qY.wait(SYY[1], SYY[1].v)
        qG.wait(SCV[0], SCV[0].v)
        qG.wait(SCV[1], SCV[1].v)
        flush_phase()
        pa.close()

    LCB = [Ctr(es, nc, f"LCB{i}") for i in range(2)]
    LCF = [Ctr(es, nc, f"LCF{i}") for i in range(2)]
    SHH = [Ctr(es, nc, f"SHH{i}") for i in range(2)]
    LWB = [Ctr(es, nc, f"LWB{i}") for i in range(2)]

    if "B" in phases:
        wig = inp("wig", [D, D])
        wag = inp("wag", [D, D])
        convd = scratch("convd", [nwt, NRT, 128, TT], F32)
        hd = scratch("hd", [nt_real, NRT, 128, TT], F32)
        phase_begin()
        pb_ = ExitStack()
        wig_sb = pb_.enter_context(nc.sbuf_tensor("wig_sb", [128, KT * D], BF16))
        wag_sb = pb_.enter_context(nc.sbuf_tensor("wag_sb", [128, KT * D], BF16))
        cbf = [pb_.enter_context(nc.sbuf_tensor(f"cbf{i}", [128, KT * TT], BF16))
               for i in range(2)]
        cf32 = [pb_.enter_context(nc.sbuf_tensor(f"cf32{i}", [128, TT], F32))
                for i in range(2)]
        g1 = [pb_.enter_context(nc.sbuf_tensor(f"g1_{i}", [128, TT], F32))
              for i in range(2)]
        g2 = [pb_.enter_context(nc.sbuf_tensor(f"g2_{i}", [128, TT], F32))
              for i in range(2)]
        abuf = [pb_.enter_context(nc.sbuf_tensor(f"abuf{i}", [128, TT], F32))
                for i in range(2)]
        m2b = [pb_.enter_context(nc.sbuf_tensor(f"m2b{i}", [128, TT], F32))
               for i in range(2)]
        nx1 = [pb_.enter_context(nc.sbuf_tensor(f"nx1_{i}", [128, TT], F32))
               for i in range(2)]
        nxb = [pb_.enter_context(nc.sbuf_tensor(f"nxb{i}", [128, TT], F32))
               for i in range(2)]
        hbf = [pb_.enter_context(nc.sbuf_tensor(f"hbf{i}", [128, TT], F32))
               for i in range(2)]
        hcar = pb_.enter_context(nc.sbuf_tensor("hcar", [128, NRT], F32))
        bsc = pb_.enter_context(nc.sbuf_tensor("bsc", [128, 4], F32))
        psA = [pb_.enter_context(nc.psum_tensor(f"psA{i}", [128, TT], F32))
               for i in range(2)]
        psB = [pb_.enter_context(nc.psum_tensor(f"psB{i}", [128, TT], F32))
               for i in range(2)]

        trk.dma(qG, LWB[0], lambda e: e.dma_start(
            out=wig_sb[:].rearrange("p (kt m) -> p kt m", kt=KT),
            in_=wig.rearrange("(kt p) m -> p kt m", p=128)),
            reads=[wig], writes=[wig_sb])
        trk.dma(qG, LWB[1], lambda e: e.dma_start(
            out=wag_sb[:].rearrange("p (kt m) -> p kt m", kt=KT),
            in_=wag.rearrange("(kt p) m -> p kt m", p=128)),
            reads=[wag], writes=[wag_sb])

        def mmg(ps, wsb, rt, src):
            def fn(e):
                mm = None
                for kt in range(KT):
                    mm = e.matmul(ps[:],
                                  wsb[:, kt * D + rt * 128:kt * D + rt * 128 + 128],
                                  src[:, kt * TT:(kt + 1) * TT],
                                  start=(kt == 0), stop=(kt == KT - 1))
                return mm
            return fn

        for t in range(nwt):
            p2 = t % 2
            trk.dma(qG, LCB[p2], lambda e, t=t, p2=p2: e.dma_start(
                out=cbf[p2][:].rearrange("p (rt n) -> p rt n", rt=NRT),
                in_=convd[t].rearrange("rt p n -> p rt n")),
                reads=[("convd", t, r_) for r_ in range(NRT)], writes=[cbf[p2]])
            for rt in range(NRT):
                b = rt % 2
                trk.dma(qY, LCF[b], lambda e, t=t, rt=rt, b=b: e.dma_start(
                    out=cf32[b][:], in_=convd[t, rt]),
                    reads=[("convd", t, rt)], writes=[cf32[b]])
                trk.op(qT, mmg(psA[b], wag_sb, rt, cbf[p2]),
                       reads=[wag_sb, cbf[p2]], writes=[psA[b]])
                trk.op(qS, lambda e, rt=rt, b=b: e.activation(
                    out=g1[b][:], in_=psA[b][:], func=AF.Sigmoid,
                    bias=pp[:, PC_BAG + rt:PC_BAG + rt + 1]),
                    reads=[psA[b], pp], writes=[g1[b]])
                trk.op(qS, lambda e, rt=rt, b=b: e.activation(
                    out=abuf[b][:], in_=g1[b][:], func=AF.Exp,
                    scale=sl[:, rt:rt + 1]),
                    reads=[g1[b], sl], writes=[abuf[b]])
                trk.op(qS, lambda e, rt=rt, b=b: e.activation(
                    out=m2b[b][:], in_=g1[b][:], func=AF.Exp,
                    scale=sl[:, 16 + rt:16 + rt + 1]),
                    reads=[g1[b], sl], writes=[m2b[b]])
                trk.op(qV, lambda e, b=b: e.tensor_scalar(
                    out=m2b[b][:], in0=m2b[b][:], scalar1=-1.0, scalar2=1.0,
                    op0=ALU.mult, op1=ALU.add), reads=[m2b[b]], writes=[m2b[b]])
                trk.op(qS, lambda e, b=b: e.activation(
                    out=m2b[b][:], in_=m2b[b][:], func=AF.Sqrt),
                    reads=[m2b[b]], writes=[m2b[b]])
                trk.op(qT, mmg(psB[b], wig_sb, rt, cbf[p2]),
                       reads=[wig_sb, cbf[p2]], writes=[psB[b]])
                trk.op(qS, lambda e, rt=rt, b=b: e.activation(
                    out=g2[b][:], in_=psB[b][:], func=AF.Sigmoid,
                    bias=pp[:, PC_BIG + rt:PC_BIG + rt + 1]),
                    reads=[psB[b], pp], writes=[g2[b]])
                trk.op(qV, lambda e, b=b: e.tensor_mul(
                    out=nx1[b][:], in0=g2[b][:], in1=cf32[b][:]),
                    reads=[g2[b], cf32[b]], writes=[nx1[b]])
                trk.op(qV, lambda e, b=b: e.tensor_mul(
                    out=nxb[b][:], in0=nx1[b][:], in1=m2b[b][:]),
                    reads=[nx1[b], m2b[b]], writes=[nxb[b]])
                if t == 1:
                    trk.op(qV, lambda e, b=b: e.tensor_scalar(
                        out=abuf[b][:, 0:1], in0=abuf[b][:, 0:1],
                        scalar1=fl[:, 0:1], scalar2=None, op0=ALU.mult),
                        reads=[abuf[b], fl], writes=[abuf[b]])
                    trk.op(qV, lambda e, b=b: e.tensor_scalar(
                        out=bsc[:, 0:1], in0=nx1[b][:, 0:1],
                        scalar1=fl[:, 1:2], scalar2=None, op0=ALU.mult),
                        reads=[nx1[b], fl], writes=[bsc])
                    trk.op(qV, lambda e, b=b: e.scalar_tensor_tensor(
                        out=nxb[b][:, 0:1], in0=nxb[b][:, 0:1],
                        scalar=fl[:, 0:1], in1=bsc[:, 0:1],
                        op0=ALU.mult, op1=ALU.add),
                        reads=[nxb[b], fl, bsc], writes=[nxb[b]])
                if t == 0:
                    trk.op(qV, lambda e, b=b: e.tensor_tensor_scan(
                        out=hbf[b][:], data0=abuf[b][:], data1=nxb[b][:],
                        initial=0.0, op0=ALU.mult, op1=ALU.add),
                        reads=[abuf[b], nxb[b]], writes=[hbf[b]])
                else:
                    trk.op(qV, lambda e, b=b, rt=rt: e.tensor_tensor_scan(
                        out=hbf[b][:], data0=abuf[b][:], data1=nxb[b][:],
                        initial=hcar[:, rt:rt + 1], op0=ALU.mult, op1=ALU.add),
                        reads=[abuf[b], nxb[b], hcar], writes=[hbf[b]])
                trk.op(qG, lambda e, b=b, rt=rt: e.tensor_copy(
                    out=hcar[:, rt:rt + 1], in_=hbf[b][:, TT - 1:TT]),
                    reads=[hbf[b]], writes=[hcar])
                if t >= 1:
                    trk.dma(qY, SHH[b], lambda e, t=t, rt=rt, b=b: e.dma_start(
                        out=hd[t - 1, rt], in_=hbf[b][:]), reads=[hbf[b]],
                        writes=[("hd", t - 1, rt)])
        qY.wait(SHH[0], SHH[0].v)
        qY.wait(SHH[1], SHH[1].v)
        flush_phase()
        pb_.close()

    LXC = [Ctr(es, nc, f"LXC{i}") for i in range(2)]
    LHH = [Ctr(es, nc, f"LHH{i}") for i in range(2)]
    LYY = [Ctr(es, nc, f"LYY{i}") for i in range(2)]
    SAC = [Ctr(es, nc, f"SAC{i}") for i in range(2)]
    SX2 = [Ctr(es, nc, f"SX2{i}") for i in range(2)]
    LWC = Ctr(es, nc, "LWC")
    LW2C = [Ctr(es, nc, f"LW2C{i}") for i in range(2)]

    if "C" in phases:
        wo = inp("wo", [D, D])
        yd = scratch("yd", [nt_real, NRT, 128, TT], F32)
        hd = scratch("hd", [nt_real, NRT, 128, TT], F32)
        accd = scratch("accd", [nt_real, NRT, 128, TT], F32)
        xn2d = scratch("xn2d", [nt_real, NRT, 128, TT], BF16)
        phase_begin()
        pc_ = ExitStack()
        wo_sb = pc_.enter_context(nc.sbuf_tensor("wo_sb", [128, KT * D], BF16))
        xc = [pc_.enter_context(nc.sbuf_tensor(f"xc{i}", [128, D], F32))
              for i in range(4)]
        hld = [pc_.enter_context(nc.sbuf_tensor(f"hld{i}", [128, 4 * TT], F32))
               for i in range(1)]
        yld = [pc_.enter_context(nc.sbuf_tensor(f"yld{i}", [128, 4 * TT], F32))
               for i in range(1)]
        hybf = [pc_.enter_context(nc.sbuf_tensor(f"hybf{i}", [128, NRT * TT], BF16))
                for i in range(2)]
        x1b4 = [pc_.enter_context(nc.sbuf_tensor(f"x1b4{i}", [128, 4 * TT], F32))
                for i in range(2)]
        x1bf = [pc_.enter_context(nc.sbuf_tensor(f"x1bf{i}", [128, NRT * TT], BF16))
                for i in range(1)]
        x1sq = [pc_.enter_context(nc.sbuf_tensor(f"x1sq{i}", [128, TT], BF16))
                for i in range(2)]
        t2b = [pc_.enter_context(nc.sbuf_tensor(f"t2b{i}", [128, TT], F32))
               for i in range(2)]
        xn2b4 = [pc_.enter_context(nc.sbuf_tensor(f"xn2b4{i}", [128, 4 * TT], BF16))
                 for i in range(2)]
        stt = pc_.enter_context(nc.sbuf_tensor("stt", [1, 4 * TT], F32))
        psO = [pc_.enter_context(nc.psum_tensor(f"psO{i}", [128, TT], F32))
               for i in range(2)]
        psS1 = pc_.enter_context(nc.psum_tensor("psS1", [1, TT], F32))
        psS2 = pc_.enter_context(nc.psum_tensor("psS2", [1, TT], F32))
        psM = pc_.enter_context(nc.psum_tensor("psM", [128, TT], F32))
        psR = pc_.enter_context(nc.psum_tensor("psR", [128, TT], F32))

        trk.dma(qG, LWC, lambda e: e.dma_start(
            out=wo_sb[:].rearrange("p (kt m) -> p kt m", kt=KT),
            in_=wo.rearrange("(kt p) m -> p kt m", p=128)),
            reads=[wo], writes=[wo_sb])
        w2p_ = inp("w2p", [NE, IS, D])
        w2bf = scratch("w2bf", [NE, IS, D], BF16)
        for s_ in range(NE):
            trk.dma(qG, LW2C[s_ % 2], lambda e, s_=s_: e.dma_start(
                out=w2bf[s_], in_=w2p_[s_]),
                reads=[w2p_], writes=[("w2bf", s_)], no_drain=True)

        for t in range(nt_real):
            for c4 in range(4):
                xi = c4
                r0 = TT * (t + 1) + 128 * c4
                trk.dma(qY, LXC[xi % 2], lambda e, r0=r0, xi=xi: e.dma_start(
                    out=xc[xi][:], in_=xs[r0:r0 + 128, :]), writes=[xc[xi]])
            for g4 in range(4):
                b = 0
                trk.dma(qY, LHH[b], lambda e, t=t, g4=g4, b=b: e.dma_start(
                    out=hld[b][:].rearrange("p (r n) -> p r n", r=4),
                    in_=hd[t, 4 * g4:4 * g4 + 4].rearrange("r p n -> p r n")),
                    reads=[("hd", t, 4 * g4 + j) for j in range(4)],
                    writes=[hld[b]])
                trk.dma(qY, LYY[b], lambda e, t=t, g4=g4, b=b: e.dma_start(
                    out=yld[b][:].rearrange("p (r n) -> p r n", r=4),
                    in_=yd[t, 4 * g4:4 * g4 + 4].rearrange("r p n -> p r n")),
                    reads=[("yd", t, 4 * g4 + j) for j in range(4)],
                    writes=[yld[b]])
                trk.op(qV, lambda e, g4=g4, b=b, hp=t % 2: e.tensor_mul(
                    out=hybf[hp][:, 4 * g4 * TT:4 * (g4 + 1) * TT],
                    in0=hld[b][:], in1=yld[b][:]),
                    reads=[hld[b], yld[b]], writes=[hybf[t % 2]])
            for rt in range(NRT):
                b = rt % 2
                xis = [c4 for c4 in range(4)]
                def og(e, rt=rt, b=b, hp=t % 2, xis=tuple(xis)):
                    mm = None
                    for c4 in range(4):
                        mm = e.matmul(psO[b][:, c4 * 128:(c4 + 1) * 128],
                                      xc[xis[c4]][:, rt * 128:(rt + 1) * 128],
                                      ident[:], start=(c4 == 0), stop=False,
                                      is_transpose=True)
                    for kt in range(KT):
                        mm = e.matmul(psO[b][:],
                                      wo_sb[:, kt * D + rt * 128:kt * D + rt * 128 + 128],
                                      hybf[hp][:, kt * TT:(kt + 1) * TT],
                                      start=False, stop=(kt == KT - 1))
                    return mm
                trk.op(qT, og,
                       reads=[wo_sb, hybf[t % 2], ident] + [xc[x_] for x_ in xis],
                       writes=[psO[b]])
                g4, j4 = rt // 4, rt % 4
                bq = g4 % 2
                trk.op(qS, lambda e, rt=rt, b=b, bq=bq, j4=j4: e.activation(
                    out=x1b4[bq][:, j4 * TT:(j4 + 1) * TT], in_=psO[b][:],
                    func=AF.Identity,
                    bias=pp[:, PC_BOUT + rt:PC_BOUT + rt + 1]),
                    reads=[psO[b], pp], writes=[("x1b4", bq, j4)])
                if j4 == 3:
                    trk.dma(qY, SAC[bq], lambda e, t=t, g4=g4, bq=bq: e.dma_start(
                        out=accd[t, 4 * g4:4 * g4 + 4].rearrange("r p n -> p r n"),
                        in_=x1b4[bq][:].rearrange("p (r n) -> p r n", r=4)),
                        reads=[("x1b4", bq, j_) for j_ in range(4)],
                        writes=[("accd", t, 4 * g4 + j_) for j_ in range(4)])
                trk.op(qS, lambda e, rt=rt, bq=bq, j4=j4: e.activation(
                    out=x1bf[0][:, rt * TT:(rt + 1) * TT],
                    in_=x1b4[bq][:, j4 * TT:(j4 + 1) * TT],
                    func=AF.Identity), reads=[("x1b4", bq, j4)], writes=[x1bf[0]])
                trk.op(qS, lambda e, rt=rt, b=b, bq=bq, j4=j4: e.activation(
                    out=x1sq[b][:], in_=x1b4[bq][:, j4 * TT:(j4 + 1) * TT],
                    func=AF.Square),
                    reads=[("x1b4", bq, j4)], writes=[x1sq[b]])
                trk.op(qT, lambda e, rt=rt: e.matmul(
                    psS1[:], ones_col_bf[:], x1bf[0][:, rt * TT:(rt + 1) * TT],
                    start=(rt == 0), stop=(rt == NRT - 1)),
                    reads=[ones_col_bf, x1bf[0]], writes=[psS1])
                trk.op(qT, lambda e, rt=rt, b=b: e.matmul(
                    psS2[:], ones_col_bf[:], x1sq[b][:],
                    start=(rt == 0), stop=(rt == NRT - 1)),
                    reads=[ones_col_bf, x1sq[b]], writes=[psS2])
            trk.op(qV, lambda e: e.tensor_scalar(
                out=stt[0:1, 0:TT], in0=psS1[:], scalar1=1.0 / D, scalar2=None,
                op0=ALU.mult), reads=[psS1], writes=[stt])
            trk.op(qV, lambda e: e.tensor_scalar(
                out=stt[0:1, TT:2 * TT], in0=psS2[:], scalar1=1.0 / D,
                scalar2=None, op0=ALU.mult), reads=[psS2], writes=[stt])
            trk.op(qV, lambda e: e.tensor_mul(
                out=stt[0:1, 3 * TT:4 * TT], in0=stt[0:1, 0:TT],
                in1=stt[0:1, 0:TT]), reads=[stt], writes=[stt])
            trk.op(qV, lambda e: e.tensor_sub(
                out=stt[0:1, TT:2 * TT], in0=stt[0:1, TT:2 * TT],
                in1=stt[0:1, 3 * TT:4 * TT]), reads=[stt], writes=[stt])
            trk.op(qV, lambda e: e.tensor_scalar(
                out=stt[0:1, TT:2 * TT], in0=stt[0:1, TT:2 * TT],
                scalar1=1e-5, scalar2=None, op0=ALU.add),
                reads=[stt], writes=[stt])
            trk.op(qS, lambda e: e.activation(
                out=stt[0:1, 3 * TT:4 * TT], in_=stt[0:1, TT:2 * TT], func=AF.Ln),
                reads=[stt], writes=[stt])
            trk.op(qS, lambda e: e.activation(
                out=stt[0:1, 2 * TT:3 * TT], in_=stt[0:1, 3 * TT:4 * TT],
                func=AF.Exp, scale=-0.5), reads=[stt], writes=[stt])
            trk.op(qT, lambda e: e.matmul(
                psM[:], ones_row[:], stt[0:1, 0:TT], start=True, stop=True),
                reads=[ones_row, stt], writes=[psM])
            trk.op(qT, lambda e: e.matmul(
                psR[:], ones_row[:], stt[0:1, 2 * TT:3 * TT], start=True,
                stop=True), reads=[ones_row, stt], writes=[psR])
            for rt in range(NRT):
                b = rt % 2
                g4, j4 = rt // 4, rt % 4
                bq = g4 % 2
                trk.op(qV, lambda e, rt=rt, b=b: e.tensor_sub(
                    out=t2b[b][:], in0=x1bf[0][:, rt * TT:(rt + 1) * TT],
                    in1=psM[:]), reads=[x1bf[0], psM], writes=[t2b[b]])
                trk.op(qV, lambda e, rt=rt, b=b, bq=bq, j4=j4: e.tensor_mul(
                    out=xn2b4[bq][:, j4 * TT:(j4 + 1) * TT], in0=t2b[b][:],
                    in1=psR[:]),
                    reads=[t2b[b], psR], writes=[("xn2b4", bq, j4)])
                if j4 == 3:
                    trk.dma(qY, SX2[bq], lambda e, t=t, g4=g4, bq=bq: e.dma_start(
                        out=xn2d[t, 4 * g4:4 * g4 + 4].rearrange("r p n -> p r n"),
                        in_=xn2b4[bq][:].rearrange("p (r n) -> p r n", r=4)),
                        reads=[("xn2b4", bq, j_) for j_ in range(4)],
                        writes=[("xn2d", t, 4 * g4 + j_) for j_ in range(4)])
        qY.wait(SAC[0], SAC[0].v)
        qY.wait(SAC[1], SAC[1].v)
        qY.wait(SX2[0], SX2[0].v)
        qY.wait(SX2[1], SX2[1].v)
        flush_phase()
        pc_.close()

    LX20 = [Ctr(es, nc, f"LX20{i}") for i in range(2)]
    LGW = Ctr(es, nc, "LGW")
    SCB = Ctr(es, nc, "SCB")
    SCB1 = Ctr(es, nc, "SCB1")

    if "D0" in phases:
        gwT = inp("gwT", [D, 16])
        xn2d = scratch("xn2d", [nt_real, NRT, 128, TT], BF16)
        combd = scratch("combd", [NE, nt_real * TT], F32)
        phase_begin()
        pd0 = ExitStack()
        gw_sb = pd0.enter_context(nc.sbuf_tensor("gw_sb", [128, KT * 16], BF16))
        x2t = [pd0.enter_context(nc.sbuf_tensor(f"x2t{i}", [128, KT * TT], BF16))
               for i in range(2)]
        scf = pd0.enter_context(nc.sbuf_tensor("scf", [16, TT], F32))
        sct = pd0.enter_context(nc.sbuf_tensor("sct", [128, 64], F32))
        esc = pd0.enter_context(nc.sbuf_tensor("esc", [128, 16], F32))
        prb = pd0.enter_context(nc.sbuf_tensor("prb", [128, 16], F32))
        wrk = pd0.enter_context(nc.sbuf_tensor("wrk", [128, 16], F32))
        msc = pd0.enter_context(nc.sbuf_tensor("msc", [128, 16], F32))
        cmb = pd0.enter_context(nc.sbuf_tensor("cmb", [128, 16], F32))
        b0 = pd0.enter_context(nc.sbuf_tensor("b0", [128, 1], F32))
        b1 = pd0.enter_context(nc.sbuf_tensor("b1", [128, 1], F32))
        b2 = pd0.enter_context(nc.sbuf_tensor("b2", [128, 1], F32))
        ccrow = pd0.enter_context(nc.sbuf_tensor("ccrow", [16, 128], F32))
        onr = pd0.enter_context(nc.sbuf_tensor("onr", [1, nt_real * TT], F32))
        psGt = pd0.enter_context(nc.psum_tensor("psGt", [16, TT], F32))
        ptg = pd0.enter_context(nc.psum_tensor("ptg", [128, 64], F32))
        pcc = pd0.enter_context(nc.psum_tensor("pcc", [16, 128], F32))

        trk.dma(qG, LGW, lambda e: e.dma_start(
            out=gw_sb[:].rearrange("p (kt m) -> p kt m", kt=KT),
            in_=gwT.rearrange("(kt p) m -> p kt m", p=128)),
            reads=[gwT], writes=[gw_sb])
        trk.op(qV, lambda e: e.memset(onr[:], 1.0), writes=[onr])
        trk.dma(qY, SCB1, lambda e: e.dma_start(
            out=combd[16:17, :], in_=onr[:]), reads=[onr], writes=[("combd", 16)])
        trk.dma(qY, SCB1, lambda e: e.dma_start(
            out=combd[17:18, :], in_=onr[:]), reads=[onr], writes=[("combd", 17)],
            first=False)

        for t in range(nt_real):
            p2 = t % 2
            trk.dma(qY, LX20[p2], lambda e, t=t, p2=p2: e.dma_start(
                out=x2t[p2][:].rearrange("p (rt n) -> p rt n", rt=NRT),
                in_=xn2d[t].rearrange("rt p n -> p rt n")),
                reads=[("xn2d", t, r_) for r_ in range(NRT)], writes=[x2t[p2]])
            def gg(e, p2=p2):
                mm = None
                for kt in range(KT):
                    mm = e.matmul(psGt[:], gw_sb[:, kt * 16:(kt + 1) * 16],
                                  x2t[p2][:, kt * TT:(kt + 1) * TT],
                                  start=(kt == 0), stop=(kt == KT - 1))
                return mm
            trk.op(qT, gg, reads=[gw_sb, x2t[p2]], writes=[psGt])
            trk.op(qS, lambda e: e.activation(
                out=scf[:], in_=psGt[:], func=AF.Identity),
                reads=[psGt], writes=[scf])
            def tg(e):
                mm = None
                for c4 in range(4):
                    mm = e.transpose(ptg[:, c4 * 16:(c4 + 1) * 16],
                                     scf[0:16, c4 * 128:(c4 + 1) * 128],
                                     ident[0:16, 0:16])
                return mm
            trk.op(qT, tg, reads=[scf, ident], writes=[ptg])
            trk.op(qS, lambda e: e.activation(
                out=sct[:], in_=ptg[:], func=AF.Identity),
                reads=[ptg], writes=[sct])
            for c4 in range(4):
                ch = sct[:, c4 * 16:(c4 + 1) * 16]
                trk.op(qV, lambda e, ch=ch: e.reduce_max(
                    out=b0[:], in_=ch, axis=AXX, negate=True),
                    reads=[sct], writes=[b0])
                trk.op(qS, lambda e, ch=ch: e.activation(
                    out=esc[:], in_=ch, func=AF.Exp, bias=b0[:]),
                    reads=[sct, b0], writes=[esc])
                trk.op(qV, lambda e: e.reduce_sum(
                    out=b1[:], in_=esc[:], axis=AXX), reads=[esc], writes=[b1])
                trk.op(qV, lambda e: e.reciprocal(out=b2[:], in_=b1[:]),
                       reads=[b1], writes=[b2])
                trk.op(qV, lambda e: e.tensor_scalar(
                    out=prb[:], in0=esc[:], scalar1=b2[:], scalar2=None,
                    op0=ALU.mult), reads=[esc, b2], writes=[prb])
                trk.op(qV, lambda e: e.tensor_copy(out=wrk[:], in_=prb[:]),
                       reads=[prb], writes=[wrk])
                for it in range(5):
                    trk.op(qV, lambda e: e.reduce_max(
                        out=b0[:], in_=wrk[:], axis=AXX),
                        reads=[wrk], writes=[b0])
                    trk.op(qV, lambda e: e.tensor_scalar(
                        out=msc[:], in0=wrk[:], scalar1=b0[:], scalar2=2.0,
                        op0=ALU.is_ge, op1=ALU.mult),
                        reads=[wrk, b0], writes=[msc])
                    trk.op(qV, lambda e: e.tensor_sub(
                        out=wrk[:], in0=wrk[:], in1=msc[:]),
                        reads=[wrk, msc], writes=[wrk])
                trk.op(qV, lambda e: e.reduce_max(
                    out=b0[:], in_=wrk[:], axis=AXX), reads=[wrk], writes=[b0])
                trk.op(qV, lambda e: e.tensor_scalar(
                    out=msc[:], in0=prb[:], scalar1=b0[:], scalar2=None,
                    op0=ALU.is_ge), reads=[prb, b0], writes=[msc])
                trk.op(qV, lambda e: e.tensor_mul(
                    out=wrk[:], in0=prb[:], in1=msc[:]),
                    reads=[prb, msc], writes=[wrk])
                trk.op(qV, lambda e: e.reduce_sum(
                    out=b1[:], in_=wrk[:], axis=AXX), reads=[wrk], writes=[b1])
                trk.op(qV, lambda e: e.tensor_scalar(
                    out=b1[:], in0=b1[:], scalar1=1e-20, scalar2=None,
                    op0=ALU.add), reads=[b1], writes=[b1])
                trk.op(qV, lambda e: e.reciprocal(out=b2[:], in_=b1[:]),
                       reads=[b1], writes=[b2])
                trk.op(qV, lambda e: e.tensor_scalar(
                    out=cmb[:], in0=wrk[:], scalar1=b2[:], scalar2=None,
                    op0=ALU.mult), reads=[wrk, b2], writes=[cmb])
                trk.op(qT, lambda e: e.transpose(pcc[:], cmb[:], ident[:]),
                       reads=[cmb, ident], writes=[pcc])
                trk.op(qS, lambda e: e.activation(
                    out=ccrow[:], in_=pcc[:], func=AF.Identity),
                    reads=[pcc], writes=[ccrow])
                c0 = t * TT + c4 * 128
                trk.dma(qY, SCB, lambda e, c0=c0: e.dma_start(
                    out=combd[0:16, c0:c0 + 128], in_=ccrow[:]),
                    reads=[ccrow], writes=[("combd", t, c4)])
        qY.wait(SCB, SCB.v)
        flush_phase()
        pd0.close()

    LW13 = Ctr(es, nc, "LW13")
    LW13H = Ctr(es, nc, "LW13H")
    LW2D = Ctr(es, nc, "LW2D")
    LX2D = [Ctr(es, nc, f"LX2D{i}") for i in range(2)]
    LCR = [Ctr(es, nc, f"LCR{i}") for i in range(2)]
    SACCD = [Ctr(es, nc, f"SACCD{i}") for i in range(NRT)]

    if "D" in phases:
        w13p = inp("w13p", [NE, D, 2816])
        w2bf = scratch("w2bf", [NE, IS, D], BF16)
        accd = scratch("accd", [nt_real, NRT, 128, TT], F32)
        xn2d = scratch("xn2d", [nt_real, NRT, 128, TT], BF16)
        combd = scratch("combd", [NE, nt_real * TT], F32)
        phase_begin()
        pd_ = ExitStack()
        w13_sb = pd_.enter_context(nc.sbuf_tensor("w13_sb", [128, KT * 2816], BF16))
        w2_sb = pd_.enter_context(nc.sbuf_tensor("w2_sb", [128, IC * D], BF16))
        x2d = [pd_.enter_context(nc.sbuf_tensor(f"x2d{i}", [128, KT * TT], BF16))
               for i in range(2)]
        s_sb = pd_.enter_context(nc.sbuf_tensor("s_sb", [128, IC * TT], BF16))
        sg = [pd_.enter_context(nc.sbuf_tensor(f"sg{i}", [128, TT], F32))
              for i in range(2)]
        ob = [pd_.enter_context(nc.sbuf_tensor(f"ob{i}", [128, TT], F32))
              for i in range(2)]
        cps = pd_.enter_context(nc.sbuf_tensor("cps", [128, TT], F32))
        crow = [pd_.enter_context(nc.sbuf_tensor(f"crow{i}", [1, TT], F32))
                for i in range(2)]
        psG = [pd_.enter_context(nc.psum_tensor(f"psG{i}", [128, TT], F32))
               for i in range(2)]
        psU = [pd_.enter_context(nc.psum_tensor(f"psU{i}", [128, TT], F32))
               for i in range(2)]
        psC = pd_.enter_context(nc.psum_tensor("psC", [128, TT], F32))
        psO2 = [pd_.enter_context(nc.psum_tensor(f"psO2{i}", [128, TT], F32))
                for i in range(2)]

        LOCH = 6

        def load_w13(s, first=False, half=None):
            wv = w13_sb[:].rearrange("p (kt m) -> p kt m", kt=KT)
            sv = w13p[s].rearrange("(kt p) m -> p kt m", p=128)
            if half in (None, "lo"):
                trk.dma(qG, LW13, lambda e: e.dma_start(
                    out=wv[:, :, 0:LOCH * 128], in_=sv[:, :, 0:LOCH * 128]),
                    reads=[w13p], writes=[("w13L",)])
                trk.dma(qG, LW13, lambda e: e.dma_start(
                    out=wv[:, :, IS:IS + LOCH * 128],
                    in_=sv[:, :, IS:IS + LOCH * 128]),
                    reads=[w13p], writes=[("w13L",)], first=False)
            if half in (None, "hi"):
                trk.dma(qG, LW13H, lambda e: e.dma_start(
                    out=wv[:, :, LOCH * 128:IS], in_=sv[:, :, LOCH * 128:IS]),
                    reads=[w13p], writes=[("w13H",)])
                trk.dma(qG, LW13H, lambda e: e.dma_start(
                    out=wv[:, :, IS + LOCH * 128:2 * IS],
                    in_=sv[:, :, IS + LOCH * 128:2 * IS]),
                    reads=[w13p], writes=[("w13H",)], first=False)

        for s in range(NE):
            for t in range(nt_real):
                p2 = t % 2
                trk.dma(qY, LX2D[p2], lambda e, t=t, p2=p2: e.dma_start(
                    out=x2d[p2][:].rearrange("p (rt n) -> p rt n", rt=NRT),
                    in_=xn2d[t].rearrange("rt p n -> p rt n")),
                    reads=[("xn2d", t, r_) for r_ in range(NRT)],
                    writes=[x2d[p2]])
                if s < 16:
                    crdeps = [("combd", t, c4_) for c4_ in range(4)]
                else:
                    crdeps = [("combd", s)]
                trk.dma(qY, LCR[p2], lambda e, s=s, t=t, p2=p2: e.dma_start(
                    out=crow[p2][:], in_=combd[s:s + 1, t * TT:(t + 1) * TT]),
                    reads=crdeps, writes=[crow[p2]])
                if s == 0 and t == 0:
                    load_w13(s, first=True)
                    trk.dma(qY, LW2D, lambda e, s=s: e.dma_start(
                        out=w2_sb[:].rearrange("p (kt m) -> p kt m", kt=IC),
                        in_=w2bf[s].rearrange("(kt p) m -> p kt m", p=128)),
                        reads=[("w2bf", s)], writes=[w2_sb])
                trk.op(qT, lambda e, p2=p2: e.matmul(
                    psC[:], ones_row[:], crow[p2][:], start=True, stop=True),
                    reads=[ones_row, crow[p2]], writes=[psC])
                trk.op(qV, lambda e: e.tensor_copy(out=cps[:], in_=psC[:]),
                       reads=[psC], writes=[cps])
                for c in range(IC):
                    b = c % 2

                    def g1g(e, p2=p2, c=c, b=b):
                        mm = None
                        for kt in range(KT):
                            mm = e.matmul(
                                psG[b][:],
                                w13_sb[:, kt * 2816 + c * 128:kt * 2816 + c * 128 + 128],
                                x2d[p2][:, kt * TT:(kt + 1) * TT],
                                start=(kt == 0), stop=(kt == KT - 1))
                        return mm

                    def u1g(e, p2=p2, c=c, b=b):
                        mm = None
                        for kt in range(KT):
                            mm = e.matmul(
                                psU[b][:],
                                w13_sb[:, kt * 2816 + IS + c * 128:kt * 2816 + IS + c * 128 + 128],
                                x2d[p2][:, kt * TT:(kt + 1) * TT],
                                start=(kt == 0), stop=(kt == KT - 1))
                        return mm
                    wkey = ("w13L",) if c < LOCH else ("w13H",)
                    trk.op(qT, g1g, reads=[wkey, x2d[p2]], writes=[psG[b]])
                    trk.op(qT, u1g, reads=[wkey, x2d[p2]], writes=[psU[b]])
                    if (t == nt_real - 1 and c == LOCH - 1 and s + 1 < NE):
                        load_w13(s + 1, half="lo")
                    trk.op(qS, lambda e, b=b: e.activation(
                        out=sg[b][:], in_=psG[b][:], func=AF.Silu),
                        reads=[psG[b]], writes=[sg[b]])
                    trk.op(qV, lambda e, b=b, c=c: e.tensor_mul(
                        out=s_sb[:, c * TT:(c + 1) * TT], in0=sg[b][:],
                        in1=psU[b][:]), reads=[sg[b], psU[b]], writes=[s_sb])
                if t == nt_real - 1 and s + 1 < NE:
                    load_w13(s + 1, half="hi")
                for rt in range(NRT):
                    b = rt % 2

                    def o2g(e, rt=rt, b=b):
                        mm = None
                        for kt in range(IC):
                            mm = e.matmul(
                                psO2[b][:],
                                w2_sb[:, kt * D + rt * 128:kt * D + rt * 128 + 128],
                                s_sb[:, kt * TT:(kt + 1) * TT],
                                start=(kt == 0), stop=(kt == IC - 1))
                        return mm
                    trk.op(qT, o2g, reads=[w2_sb, s_sb], writes=[psO2[b]])
                    trk.op(qV, lambda e, b=b: e.tensor_mul(
                        out=ob[b][:], in0=cps[:], in1=psO2[b][:]),
                        reads=[cps, psO2[b]], writes=[ob[b]])
                    trk.dma(qG, SACCD[rt], lambda e, t=t, rt=rt, b=b: e.dma_start(
                        out=accd[t, rt], in_=ob[b][:], accum_op=ALU.add),
                        reads=[ob[b]], writes=[("accd", t, rt)])
                if t == nt_real - 1 and s + 1 < NE:
                    trk.dma(qY, LW2D, lambda e, s=s: e.dma_start(
                        out=w2_sb[:].rearrange("p (kt m) -> p kt m", kt=IC),
                        in_=w2bf[s + 1].rearrange("(kt p) m -> p kt m", p=128)),
                        reads=[("w2bf", s + 1)], writes=[w2_sb])
        flush_phase()
        pd_.close()

    LAE = [Ctr(es, nc, f"LAE{i}") for i in range(2)]
    SOUT = [Ctr(es, nc, f"SOUT{i}") for i in range(2)]

    if "E" in phases:
        accd = scratch("accd", [nt_real, NRT, 128, TT], F32)
        out = nc.dram_tensor("out", [nt_real * TT, D], F32, kind="ExternalOutput").ap()
        phase_begin()
        pe_ = ExitStack()
        aee = [pe_.enter_context(nc.sbuf_tensor(f"aee{i}", [128, NRT * TT], F32))
               for i in range(2)]
        oy = [pe_.enter_context(nc.sbuf_tensor(f"oy{i}", [128, D], F32))
              for i in range(2)]
        psE = [pe_.enter_context(nc.psum_tensor(f"psE{i}", [128, TT], F32))
               for i in range(2)]

        for t in range(nt_real):
            p2 = t % 2
            trk.dma(qY, LAE[p2], lambda e, t=t, p2=p2: e.dma_start(
                out=aee[p2][:].rearrange("p (rt n) -> p rt n", rt=NRT),
                in_=accd[t].rearrange("rt p n -> p rt n")),
                reads=[("accd", t, r_) for r_ in range(NRT)], writes=[aee[p2]])
            for c4 in range(4):
                pc2 = c4 % 2
                for g4 in range(4):

                    def tpe(e, p2=p2, c4=c4, g4=g4, pc2=pc2):
                        mm = None
                        for j in range(4):
                            rt = g4 * 4 + j
                            mm = e.transpose(
                                psE[pc2][:, j * 128:(j + 1) * 128],
                                aee[p2][:, rt * TT + c4 * 128:rt * TT + c4 * 128 + 128],
                                ident[:])
                        return mm
                    trk.op(qT, tpe, reads=[aee[p2], ident], writes=[psE[pc2]])
                    trk.op(qS, lambda e, g4=g4, pc2=pc2: e.activation(
                        out=oy[pc2][:, g4 * TT:(g4 + 1) * TT], in_=psE[pc2][:],
                        func=AF.Identity), reads=[psE[pc2]], writes=[oy[pc2]])
                r0 = t * TT + c4 * 128
                trk.dma(qY, SOUT[pc2], lambda e, r0=r0, pc2=pc2: e.dma_start(
                    out=out[r0:r0 + 128, :], in_=oy[pc2][:]),
                    reads=[oy[pc2]], writes=[("out", r0)])
        flush_phase()
        pe_.close()

    for s in sorted(trk.dma_sems, key=lambda c: c.name):
        if s.v > 0:
            qG.wait(s, s.v)
    with nc.Block() as blk:
        @blk.gpsimd
        def _(e):
            qG.replay(e)

    es.close()
    return nc


def _pack_inputs(inputs, nt_real=8, n_cores=8, names=None):
    f32 = np.float32
    x = np.ascontiguousarray(np.asarray(inputs["L0_attn_x"]).reshape(32768, D))
    chunk = 4096
    warm = TT
    in_maps = []

    ppk = np.zeros((128, PC_N), f32)

    def col16(v):
        return np.ascontiguousarray(np.asarray(v).reshape(16, 128).T)

    ppk[:, PC_BY:PC_BY + 16] = col16(inputs["L0_attn_b_y"])
    ppk[:, PC_BX:PC_BX + 16] = col16(inputs["L0_attn_b_x"])
    ppk[:, PC_BIG:PC_BIG + 16] = col16(inputs["L0_attn_b_ig"])
    ppk[:, PC_BAG:PC_BAG + 16] = col16(inputs["L0_attn_b_ag"])
    ppk[:, PC_BOUT:PC_BOUT + 16] = col16(inputs["L0_attn_b_out"])
    ppk[:, PC_CB:PC_CB + 16] = col16(inputs["L0_attn_conv_b"])
    cw = np.asarray(inputs["L0_attn_conv_w"]).reshape(4, 16, 128).transpose(2, 1, 0)
    ppk[:, PC_CW:PC_CW + 64] = np.ascontiguousarray(cw.reshape(128, 64))
    ppk[:, PC_AP:PC_AP + 16] = col16(inputs["L0_attn_a_param"])

    gwT_ = np.ascontiguousarray(np.asarray(inputs["L0_ffn_gate_w"]).T)

    w13 = np.asarray(inputs["L0_ffn_w13"])
    sw13T = np.ascontiguousarray(np.asarray(inputs["L0_ffn_sw13"]).T)
    w2 = np.asarray(inputs["L0_ffn_w2"])
    sw2T = np.ascontiguousarray(np.asarray(inputs["L0_ffn_sw2"]).T)
    w13p_ = np.empty((NE, D, 2816), f32)
    w13p_[0:16] = w13.transpose(0, 2, 1)
    w13p_[16] = np.concatenate([sw13T[:, 0:IS], sw13T[:, 2 * IS:3 * IS]], axis=1)
    w13p_[17] = np.concatenate([sw13T[:, IS:2 * IS], sw13T[:, 3 * IS:4 * IS]], axis=1)
    w2p_ = np.empty((NE, IS, D), f32)
    w2p_[0:16] = w2.transpose(0, 2, 1)
    w2p_[16] = sw2T[0:IS]
    w2p_[17] = sw2T[IS:2 * IS]

    common = {
        "ppack": ppk, "gwT": gwT_, "w13p": w13p_, "w2p": w2p_,
        "wy": np.asarray(inputs["L0_attn_w_y"]),
        "wx": np.asarray(inputs["L0_attn_w_x"]),
        "wig": np.asarray(inputs["L0_attn_w_ig"]),
        "wag": np.asarray(inputs["L0_attn_w_ag"]),
        "wo": np.asarray(inputs["L0_attn_w_out"]),
    }
    if names is not None:
        common = {k: v for k, v in common.items() if k in names}
    for c in range(n_cores):
        lo = chunk * c
        if c % 4 == 0:
            sh = np.concatenate([np.zeros((warm, D), f32), x[lo:lo + nt_real * TT]],
                                axis=0)
            flag = 0.0
        else:
            sh = np.ascontiguousarray(x[lo - warm:lo + nt_real * TT])
            flag = 1.0
        m = dict(common)
        m["xs"] = sh
        m["flags"] = np.tile(np.array([[flag, 1.0 - flag]], f32), (128, 1))
        in_maps.append(m)
    return in_maps


def kernel(**inputs):
    n_cores = 8
    nc = build_program()
    in_maps = _pack_inputs(inputs, nt_real=8, n_cores=n_cores)
    res = run_bass_kernel_spmd(nc, in_maps, core_ids=list(range(n_cores)))
    out = np.concatenate([res.results[c]["out"] for c in range(n_cores)], axis=0)
    return out.reshape(2, 16384, D)


# version 07
# speedup vs baseline: 1.1738x; 1.1265x over previous
import numpy as np
import ml_dtypes
from contextlib import ExitStack

import concourse.bass as bass
import concourse.mybir as mybir
from concourse.bass_utils import run_bass_kernel_spmd

F32 = mybir.dt.float32
BF16 = mybir.dt.bfloat16
AF = mybir.ActivationFunctionType
ALU = mybir.AluOpType
AXX = mybir.AxisListType.X

D = 2048
NRT = 16
TT = 512
KT = 16
IC = 11
NE = 18
IS = 1408

PC_BY, PC_BX, PC_BIG, PC_BAG, PC_BOUT, PC_CB = 0, 16, 32, 48, 64, 80
PC_CW = 96
PC_AP = 160
PC_N = 176


class Ctr:
    def __init__(self, es, nc, name):
        self.h = es.enter_context(nc.semaphore(name))
        self.v = 0
        self.name = name


def _key(ap):
    if isinstance(ap, (str, tuple)):
        return ap
    t = getattr(ap, "tensor", None)
    return t.name if t is not None else str(ap)


class Q:
    def __init__(self, name, prog):
        self.name = name
        self.prog = prog
        self.waited = {}
        self.pending = []

    def wait(self, ctr, val):
        if val <= 0:
            return
        if self.waited.get(id(ctr), -1) >= val:
            return
        self.pending.append(lambda e, h=ctr.h, v=val: e.wait_ge(h, v))
        self.waited[id(ctr)] = val

    def raw(self, fn):
        self.pending.append(fn)

    def replay(self, eng):
        for fn in self.pending:
            fn(eng)
        self.pending = []


class Trk:
    def __init__(self):
        self.writer = {}
        self.readers = {}
        self.dma_sems = set()
        self.dma_q = {}

    def _pre(self, q, reads, writes):
        for ap in reads:
            w = self.writer.get(_key(ap))
            if w:
                q.wait(w[0], w[1])
        for ap in writes:
            k = _key(ap)
            w = self.writer.get(k)
            if w:
                q.wait(w[0], w[1])
            for (ctr, val) in self.readers.get(k, {}).values():
                q.wait(ctr, val)

    def _post(self, q, reads, writes, ctr, val):
        for ap in writes:
            k = _key(ap)
            self.writer[k] = (ctr, val)
            self.readers[k] = {}
        for ap in reads:
            self.readers.setdefault(_key(ap), {})[q.name] = (ctr, val)

    def op(self, q, fn, reads=(), writes=(), waits=()):
        self._pre(q, reads, writes)
        for (ctr, val) in waits:
            q.wait(ctr, val)
        q.prog.v += 1
        sh, sv = q.prog.h, q.prog.v
        q.raw(lambda e: fn(e).then_inc(sh, 1))
        self._post(q, reads, writes, q.prog, q.prog.v)

    def dma(self, q, sem, fn, reads=(), writes=(), waits=(), first=True,
            no_drain=False):
        self.dma_sems.add(sem)
        if not no_drain:
            self.dma_q[sem] = q
        if first and sem.v > 0:
            q.wait(sem, sem.v)
        self._pre(q, reads, writes)
        for (ctr, val) in waits:
            q.wait(ctr, val)
        sem.v += 16
        sh = sem.h
        q.raw(lambda e: fn(e).then_inc(sh, 16))
        self._post(q, reads, writes, sem, sem.v)


def build_program(nt_real=8, phases=("0", "A", "B", "C", "D0", "D", "E"), dbg=()):
    nwt = nt_real + 1
    rows = nwt * TT
    nc = bass.Bass("TRN2", target_bir_lowering=False)

    _dram = {}

    def inp(name, shape):
        if name not in _dram:
            _dram[name] = nc.dram_tensor(name, shape, F32, kind="ExternalInput").ap()
        return _dram[name]

    def scratch(name, shape, dt):
        if name not in _dram:
            kind = "ExternalOutput" if name in dbg else "Internal"
            _dram[name] = nc.dram_tensor(name, shape, dt, kind=kind).ap()
        return _dram[name]

    xs = inp("xs", [rows, D])
    flags = inp("flags", [128, 2])
    ppack = inp("ppack", [128, PC_N])

    es = ExitStack()
    pp = es.enter_context(nc.sbuf_tensor("pp", [128, PC_N], F32))
    fl = es.enter_context(nc.sbuf_tensor("fl", [128, 2], F32))
    sl = es.enter_context(nc.sbuf_tensor("sl", [128, 48], F32))
    nbias = es.enter_context(nc.sbuf_tensor("nbias", [128, 32], F32))
    cst = es.enter_context(nc.sbuf_tensor("cst", [128, 8], F32))
    ident = es.enter_context(nc.sbuf_tensor("ident", [128, 128], F32))
    ones_col_bf = es.enter_context(nc.sbuf_tensor("ones_col_bf", [128, 1], BF16))
    ones_row = es.enter_context(nc.sbuf_tensor("ones_row", [1, 128], F32))
    C_ONE, C_EPS, C_INVD, C_MONE, C_TINY = 0, 1, 2, 3, 4

    trk = Trk()
    PB = Ctr(es, nc, "PB")
    NENG = 5
    TEP = Ctr(es, nc, "TEP")
    VEP = Ctr(es, nc, "VEP")
    SEP = Ctr(es, nc, "SEP")
    GPP = Ctr(es, nc, "GPP")
    qT = Q("T", TEP)
    qV = Q("V", VEP)
    qS = Q("S", SEP)
    qG = Q("G", GPP)
    qY = Q("Y", None)
    ALLQ = [qT, qV, qS, qG, qY]

    def flush_phase():
        for sem, q in trk.dma_q.items():
            q.wait(sem, sem.v)
        for q in ALLQ:
            q.raw(lambda e: e.sem_inc(PB.h, 1))
        PB.v += NENG
        with nc.Block() as blk:
            @blk.tensor
            def _(e):
                qT.replay(e)

            @blk.vector
            def _(e):
                qV.replay(e)

            @blk.scalar
            def _(e):
                qS.replay(e)

            @blk.gpsimd
            def _(e):
                qG.replay(e)

            @blk.sync
            def _(e):
                qY.replay(e)

    def phase_begin():
        pbw = PB.v
        if pbw > 0:
            for q in ALLQ:
                q.raw(lambda e, v=pbw: e.wait_ge(PB.h, v))

    L0X = [Ctr(es, nc, f"L0X{i}") for i in range(4)]
    S0X = [Ctr(es, nc, f"S0X{i}") for i in range(4)]
    LPP = Ctr(es, nc, "LPP")
    LFL = Ctr(es, nc, "LFL")
    LWA = [Ctr(es, nc, f"LWA{i}") for i in range(2)]
    LXT = [Ctr(es, nc, f"LXT{i}") for i in range(3)]
    SYY = [Ctr(es, nc, f"SYY{i}") for i in range(2)]
    SCV = [Ctr(es, nc, f"SCV{i}") for i in range(2)]

    n0 = rows // 128

    if "0" in phases:
        xn1 = scratch("xn1", [rows, D], BF16)
        phase_begin()
        p0 = ExitStack()
        NP0 = 4
        xld = [p0.enter_context(nc.sbuf_tensor(f"xld{i}", [128, D], F32)) for i in range(NP0)]
        sqs = [p0.enter_context(nc.sbuf_tensor(f"sqs{i}", [128, D], F32)) for i in range(NP0)]
        xnb = [p0.enter_context(nc.sbuf_tensor(f"xnb{i}", [128, D], BF16)) for i in range(NP0)]
        st0 = [p0.enter_context(nc.sbuf_tensor(f"st0{i}", [128, 8], F32)) for i in range(NP0)]

        trk.dma(qY, LPP, lambda e: e.dma_start(out=pp[:], in_=ppack[:]),
                reads=[ppack], writes=[pp])
        trk.dma(qY, LFL, lambda e: e.dma_start(out=fl[:], in_=flags[:]),
                reads=[flags], writes=[fl])
        trk.op(qV, lambda e: e.memset(ident[:], 0.0), writes=[ident])
        trk.op(qG, lambda e: e.affine_select(
            out=ident[:], in_=ident[:], compare_op=ALU.not_equal,
            fill=1.0, base=0, pattern=[[-1, 128]], channel_multiplier=1),
            reads=[ident], writes=[ident])
        trk.op(qV, lambda e: e.memset(ones_col_bf[:], 1.0), writes=[ones_col_bf])
        trk.op(qV, lambda e: e.memset(ones_row[:], 1.0), writes=[ones_row])
        for col, val in ((C_ONE, 1.0), (C_EPS, 1e-5), (C_INVD, 1.0 / D),
                         (C_MONE, -1.0), (C_TINY, 1e-20)):
            trk.op(qV, lambda e, col=col, val=val: e.memset(
                cst[:, col:col + 1], val), writes=[cst])
        trk.op(qV, lambda e: e.tensor_scalar(
            out=nbias[:, 0:16], in0=pp[:, PC_BIG:PC_BIG + 16],
            scalar1=-1.0, scalar2=None, op0=ALU.mult), reads=[pp], writes=[nbias])
        trk.op(qV, lambda e: e.tensor_scalar(
            out=nbias[:, 16:32], in0=pp[:, PC_BAG:PC_BAG + 16],
            scalar1=-1.0, scalar2=None, op0=ALU.mult), reads=[pp], writes=[nbias])
        trk.op(qS, lambda e: e.activation(
            out=sl[:, 16:32], in_=pp[:, PC_AP:PC_AP + 16], func=AF.Abs),
            reads=[pp], writes=[sl])
        trk.op(qS, lambda e: e.activation(
            out=sl[:, 16:32], in_=sl[:, 16:32], func=AF.Exp, scale=-1.0),
            reads=[sl], writes=[sl])
        trk.op(qS, lambda e: e.activation(
            out=sl[:, 16:32], in_=sl[:, 16:32], func=AF.Ln,
            bias=cst[:, C_ONE:C_ONE + 1]), reads=[sl, cst], writes=[sl])
        trk.op(qV, lambda e: e.tensor_scalar(
            out=sl[:, 0:16], in0=pp[:, PC_AP:PC_AP + 16],
            scalar1=0.0, scalar2=None, op0=ALU.max), reads=[pp, sl], writes=[sl])
        trk.op(qV, lambda e: e.tensor_add(
            out=sl[:, 32:48], in0=sl[:, 0:16], in1=sl[:, 16:32]),
            reads=[sl], writes=[sl])
        trk.op(qV, lambda e: e.tensor_scalar(
            out=sl[:, 0:16], in0=sl[:, 32:48],
            scalar1=-8.0, scalar2=None, op0=ALU.mult), reads=[sl], writes=[sl])
        trk.op(qV, lambda e: e.tensor_scalar(
            out=sl[:, 16:32], in0=sl[:, 32:48],
            scalar1=-16.0, scalar2=None, op0=ALU.mult), reads=[sl], writes=[sl])
        for i in range(n0):
            p = i % 4
            trk.dma(qY, L0X[p], lambda e, i=i, p=p: e.dma_start(
                out=xld[p][:], in_=xs[128 * i:128 * (i + 1), :]), writes=[xld[p]])
            trk.op(qS, lambda e, p=p: e.activation(
                out=sqs[p][:], in_=xld[p][:], func=AF.Square,
                accum_out=st0[p][:, 1:2]),
                reads=[xld[p]], writes=[sqs[p], st0[p]])
            trk.op(qV, lambda e, p=p: e.reduce_sum(
                out=st0[p][:, 0:1], in_=xld[p][:], axis=AXX),
                reads=[xld[p]], writes=[st0[p]])
            trk.op(qV, lambda e, p=p: e.tensor_scalar(
                out=st0[p][:, 2:3], in0=st0[p][:, 0:1],
                scalar1=1.0 / D, scalar2=None, op0=ALU.mult),
                reads=[st0[p]], writes=[st0[p]])
            trk.op(qV, lambda e, p=p: e.tensor_scalar(
                out=st0[p][:, 3:4], in0=st0[p][:, 1:2],
                scalar1=1.0 / D, scalar2=None, op0=ALU.mult),
                reads=[st0[p]], writes=[st0[p]])
            trk.op(qV, lambda e, p=p: e.tensor_tensor(
                out=st0[p][:, 4:5], in0=st0[p][:, 2:3], in1=st0[p][:, 2:3],
                op=ALU.mult), reads=[st0[p]], writes=[st0[p]])
            trk.op(qV, lambda e, p=p: e.tensor_sub(
                out=st0[p][:, 3:4], in0=st0[p][:, 3:4], in1=st0[p][:, 4:5]),
                reads=[st0[p]], writes=[st0[p]])
            trk.op(qV, lambda e, p=p: e.tensor_scalar(
                out=st0[p][:, 3:4], in0=st0[p][:, 3:4],
                scalar1=1e-5, scalar2=None, op0=ALU.add),
                reads=[st0[p]], writes=[st0[p]])
            trk.op(qS, lambda e, p=p: e.activation(
                out=st0[p][:, 4:5], in_=st0[p][:, 3:4], func=AF.Ln),
                reads=[st0[p]], writes=[st0[p]])
            trk.op(qS, lambda e, p=p: e.activation(
                out=st0[p][:, 5:6], in_=st0[p][:, 4:5], func=AF.Exp, scale=-0.5),
                reads=[st0[p]], writes=[st0[p]])
            trk.op(qV, lambda e, p=p: e.tensor_scalar(
                out=xnb[p][:], in0=xld[p][:],
                scalar1=st0[p][:, 2:3], scalar2=st0[p][:, 5:6],
                op0=ALU.subtract, op1=ALU.mult),
                reads=[xld[p], st0[p]], writes=[xnb[p]])
            trk.dma(qY, S0X[p], lambda e, i=i, p=p: e.dma_start(
                out=xn1[128 * i:128 * (i + 1), :], in_=xnb[p][:]),
                reads=[xnb[p]], writes=[("xn1", i)])
        flush_phase()
        p0.close()

    if "A" in phases:
        wy = inp("wy", [D, D])
        wx = inp("wx", [D, D])
        xn1 = scratch("xn1", [rows, D], BF16)
        convd = scratch("convd", [nwt, NRT, 128, TT], F32)
        yd = scratch("yd", [nt_real, NRT, 128, TT], F32)
        phase_begin()
        pa = ExitStack()
        wy_sb = pa.enter_context(nc.sbuf_tensor("wy_sb", [128, KT * D], BF16))
        wx_sb = pa.enter_context(nc.sbuf_tensor("wx_sb", [128, KT * D], BF16))
        xnt = [pa.enter_context(nc.sbuf_tensor(f"xnt{i}", [128, KT * TT], BF16))
               for i in range(3)]
        ybuf = [pa.enter_context(nc.sbuf_tensor(f"ybuf{i}", [128, TT], F32))
                for i in range(2)]
        xbx = [pa.enter_context(nc.sbuf_tensor(f"xbx{i}", [128, TT + 3], F32))
               for i in range(2)]
        cv1 = [pa.enter_context(nc.sbuf_tensor(f"cv1{i}", [128, TT], F32))
               for i in range(2)]
        cv2 = [pa.enter_context(nc.sbuf_tensor(f"cv2{i}", [128, TT], F32))
               for i in range(2)]
        carr = pa.enter_context(nc.sbuf_tensor("carr", [128, NRT * 3], F32))
        psY = [pa.enter_context(nc.psum_tensor(f"psY{i}", [128, TT], F32))
               for i in range(2)]
        psX = [pa.enter_context(nc.psum_tensor(f"psX{i}", [128, TT], F32))
               for i in range(2)]

        trk.dma(qG, LWA[0], lambda e: e.dma_start(
            out=wy_sb[:].rearrange("p (kt m) -> p kt m", kt=KT),
            in_=wy.rearrange("(kt p) m -> p kt m", p=128)),
            reads=[wy], writes=[wy_sb])
        trk.dma(qG, LWA[1], lambda e: e.dma_start(
            out=wx_sb[:].rearrange("p (kt m) -> p kt m", kt=KT),
            in_=wx.rearrange("(kt p) m -> p kt m", p=128)),
            reads=[wx], writes=[wx_sb])
        trk.op(qV, lambda e: e.memset(carr[:], 0.0), writes=[carr])

        def mmgroup(ps, wsb, rt, src, n=KT, w=TT):
            def fn(e):
                mm = None
                for kt in range(n):
                    mm = e.matmul(ps[:],
                                  wsb[:, kt * D + rt * 128:kt * D + rt * 128 + 128],
                                  src[:, kt * w:(kt + 1) * w],
                                  start=(kt == 0), stop=(kt == n - 1))
                return mm
            return fn

        for t in range(nwt):
            p3 = t % 3
            for rt in range(NRT):
                trk.dma(qY, LXT[p3], lambda e, t=t, rt=rt, p3=p3: e.dma_start_transpose(
                    out=xnt[p3][:, rt * TT:(rt + 1) * TT],
                    in_=xn1[TT * t:TT * (t + 1), 128 * rt:128 * (rt + 1)]),
                    reads=[("xn1", j) for j in range(4 * t, 4 * t + 4)],
                    writes=[xnt[p3]], first=(rt == 0))
            for rt in range(NRT):
                b = rt % 2
                if t >= 1:
                    trk.op(qT, mmgroup(psY[b], wy_sb, rt, xnt[p3]),
                           reads=[wy_sb, xnt[p3]], writes=[psY[b]])
                    trk.op(qS, lambda e, rt=rt, b=b: e.activation(
                        out=ybuf[b][:], in_=psY[b][:], func=AF.Gelu,
                        bias=pp[:, PC_BY + rt:PC_BY + rt + 1]),
                        reads=[psY[b], pp], writes=[ybuf[b]])
                    trk.dma(qY, SYY[b], lambda e, t=t, rt=rt, b=b: e.dma_start(
                        out=yd[t - 1, rt], in_=ybuf[b][:]), reads=[ybuf[b]],
                        writes=[("yd", t - 1, rt)])
                trk.op(qT, mmgroup(psX[b], wx_sb, rt, xnt[p3]),
                       reads=[wx_sb, xnt[p3]], writes=[psX[b]])
                trk.op(qV, lambda e, rt=rt, b=b: e.tensor_copy(
                    out=xbx[b][:, 0:3], in_=carr[:, 3 * rt:3 * rt + 3]),
                    reads=[carr], writes=[xbx[b]])
                trk.op(qS, lambda e, rt=rt, b=b: e.activation(
                    out=xbx[b][:, 3:TT + 3], in_=psX[b][:], func=AF.Identity,
                    bias=pp[:, PC_BX + rt:PC_BX + rt + 1]),
                    reads=[psX[b], pp], writes=[xbx[b]])
                cwc = PC_CW + 4 * rt
                trk.op(qV, lambda e, b=b, cwc=cwc: e.tensor_scalar(
                    out=cv1[b][:], in0=xbx[b][:, 0:TT],
                    scalar1=pp[:, cwc:cwc + 1], scalar2=None, op0=ALU.mult),
                    reads=[xbx[b], pp], writes=[cv1[b]])
                trk.op(qV, lambda e, b=b, cwc=cwc: e.scalar_tensor_tensor(
                    out=cv2[b][:], in0=xbx[b][:, 1:TT + 1],
                    scalar=pp[:, cwc + 1:cwc + 2], in1=cv1[b][:],
                    op0=ALU.mult, op1=ALU.add),
                    reads=[xbx[b], pp, cv1[b]], writes=[cv2[b]])
                trk.op(qV, lambda e, b=b, cwc=cwc: e.scalar_tensor_tensor(
                    out=cv1[b][:], in0=xbx[b][:, 2:TT + 2],
                    scalar=pp[:, cwc + 2:cwc + 3], in1=cv2[b][:],
                    op0=ALU.mult, op1=ALU.add),
                    reads=[xbx[b], pp, cv2[b]], writes=[cv1[b]])
                trk.op(qV, lambda e, b=b, cwc=cwc: e.scalar_tensor_tensor(
                    out=cv2[b][:], in0=xbx[b][:, 3:TT + 3],
                    scalar=pp[:, cwc + 3:cwc + 4], in1=cv1[b][:],
                    op0=ALU.mult, op1=ALU.add),
                    reads=[xbx[b], pp, cv1[b]], writes=[cv2[b]])
                trk.op(qV, lambda e, b=b, rt=rt: e.tensor_scalar(
                    out=cv2[b][:], in0=cv2[b][:],
                    scalar1=pp[:, PC_CB + rt:PC_CB + rt + 1],
                    scalar2=None, op0=ALU.add),
                    reads=[cv2[b], pp], writes=[cv2[b]])
                trk.op(qV, lambda e, b=b, rt=rt: e.tensor_copy(
                    out=carr[:, 3 * rt:3 * rt + 3], in_=xbx[b][:, TT:TT + 3]),
                    reads=[xbx[b]], writes=[carr])
                trk.dma(qG, SCV[b], lambda e, t=t, rt=rt, b=b: e.dma_start(
                    out=convd[t, rt], in_=cv2[b][:]), reads=[cv2[b]],
                    writes=[("convd", t, rt)])
        qY.wait(SYY[0], SYY[0].v)
        qY.wait(SYY[1], SYY[1].v)
        qG.wait(SCV[0], SCV[0].v)
        qG.wait(SCV[1], SCV[1].v)
        flush_phase()
        pa.close()

    LCB = [Ctr(es, nc, f"LCB{i}") for i in range(2)]
    LCF = [Ctr(es, nc, f"LCF{i}") for i in range(2)]
    SHH = [Ctr(es, nc, f"SHH{i}") for i in range(2)]
    LWB = [Ctr(es, nc, f"LWB{i}") for i in range(2)]

    if "B" in phases:
        wig = inp("wig", [D, D])
        wag = inp("wag", [D, D])
        convd = scratch("convd", [nwt, NRT, 128, TT], F32)
        hd = scratch("hd", [nt_real, NRT, 128, TT], F32)
        phase_begin()
        pb_ = ExitStack()
        wig_sb = pb_.enter_context(nc.sbuf_tensor("wig_sb", [128, KT * D], BF16))
        wag_sb = pb_.enter_context(nc.sbuf_tensor("wag_sb", [128, KT * D], BF16))
        cbf = [pb_.enter_context(nc.sbuf_tensor(f"cbf{i}", [128, KT * TT], BF16))
               for i in range(2)]
        cf32 = [pb_.enter_context(nc.sbuf_tensor(f"cf32{i}", [128, TT], F32))
                for i in range(2)]
        g1 = [pb_.enter_context(nc.sbuf_tensor(f"g1_{i}", [128, TT], F32))
              for i in range(2)]
        g2 = [pb_.enter_context(nc.sbuf_tensor(f"g2_{i}", [128, TT], F32))
              for i in range(2)]
        abuf = [pb_.enter_context(nc.sbuf_tensor(f"abuf{i}", [128, TT], F32))
                for i in range(2)]
        m2b = [pb_.enter_context(nc.sbuf_tensor(f"m2b{i}", [128, TT], F32))
               for i in range(2)]
        nx1 = [pb_.enter_context(nc.sbuf_tensor(f"nx1_{i}", [128, TT], F32))
               for i in range(2)]
        nxb = [pb_.enter_context(nc.sbuf_tensor(f"nxb{i}", [128, TT], F32))
               for i in range(2)]
        hbf = [pb_.enter_context(nc.sbuf_tensor(f"hbf{i}", [128, TT], F32))
               for i in range(2)]
        hcar = pb_.enter_context(nc.sbuf_tensor("hcar", [128, NRT], F32))
        bsc = pb_.enter_context(nc.sbuf_tensor("bsc", [128, 4], F32))
        psA = [pb_.enter_context(nc.psum_tensor(f"psA{i}", [128, TT], F32))
               for i in range(2)]
        psB = [pb_.enter_context(nc.psum_tensor(f"psB{i}", [128, TT], F32))
               for i in range(2)]

        trk.dma(qG, LWB[0], lambda e: e.dma_start(
            out=wig_sb[:].rearrange("p (kt m) -> p kt m", kt=KT),
            in_=wig.rearrange("(kt p) m -> p kt m", p=128)),
            reads=[wig], writes=[wig_sb])
        trk.dma(qG, LWB[1], lambda e: e.dma_start(
            out=wag_sb[:].rearrange("p (kt m) -> p kt m", kt=KT),
            in_=wag.rearrange("(kt p) m -> p kt m", p=128)),
            reads=[wag], writes=[wag_sb])

        def mmg(ps, wsb, rt, src):
            def fn(e):
                mm = None
                for kt in range(KT):
                    mm = e.matmul(ps[:],
                                  wsb[:, kt * D + rt * 128:kt * D + rt * 128 + 128],
                                  src[:, kt * TT:(kt + 1) * TT],
                                  start=(kt == 0), stop=(kt == KT - 1))
                return mm
            return fn

        for t in range(nwt):
            p2 = t % 2
            trk.dma(qG, LCB[p2], lambda e, t=t, p2=p2: e.dma_start(
                out=cbf[p2][:].rearrange("p (rt n) -> p rt n", rt=NRT),
                in_=convd[t].rearrange("rt p n -> p rt n")),
                reads=[("convd", t, r_) for r_ in range(NRT)], writes=[cbf[p2]])
            for rt in range(NRT):
                b = rt % 2
                trk.dma(qY, LCF[b], lambda e, t=t, rt=rt, b=b: e.dma_start(
                    out=cf32[b][:], in_=convd[t, rt]),
                    reads=[("convd", t, rt)], writes=[cf32[b]])
                trk.op(qT, mmg(psA[b], wag_sb, rt, cbf[p2]),
                       reads=[wag_sb, cbf[p2]], writes=[psA[b]])
                trk.op(qS, lambda e, rt=rt, b=b: e.activation(
                    out=g1[b][:], in_=psA[b][:], func=AF.Sigmoid,
                    bias=pp[:, PC_BAG + rt:PC_BAG + rt + 1]),
                    reads=[psA[b], pp], writes=[g1[b]])
                trk.op(qS, lambda e, rt=rt, b=b: e.activation(
                    out=abuf[b][:], in_=g1[b][:], func=AF.Exp,
                    scale=sl[:, rt:rt + 1]),
                    reads=[g1[b], sl], writes=[abuf[b]])
                trk.op(qS, lambda e, rt=rt, b=b: e.activation(
                    out=m2b[b][:], in_=g1[b][:], func=AF.Exp,
                    scale=sl[:, 16 + rt:16 + rt + 1]),
                    reads=[g1[b], sl], writes=[m2b[b]])
                trk.op(qV, lambda e, b=b: e.tensor_scalar(
                    out=m2b[b][:], in0=m2b[b][:], scalar1=-1.0, scalar2=1.0,
                    op0=ALU.mult, op1=ALU.add), reads=[m2b[b]], writes=[m2b[b]])
                trk.op(qS, lambda e, b=b: e.activation(
                    out=m2b[b][:], in_=m2b[b][:], func=AF.Sqrt),
                    reads=[m2b[b]], writes=[m2b[b]])
                trk.op(qT, mmg(psB[b], wig_sb, rt, cbf[p2]),
                       reads=[wig_sb, cbf[p2]], writes=[psB[b]])
                trk.op(qS, lambda e, rt=rt, b=b: e.activation(
                    out=g2[b][:], in_=psB[b][:], func=AF.Sigmoid,
                    bias=pp[:, PC_BIG + rt:PC_BIG + rt + 1]),
                    reads=[psB[b], pp], writes=[g2[b]])
                trk.op(qV, lambda e, b=b: e.tensor_mul(
                    out=nx1[b][:], in0=g2[b][:], in1=cf32[b][:]),
                    reads=[g2[b], cf32[b]], writes=[nx1[b]])
                trk.op(qV, lambda e, b=b: e.tensor_mul(
                    out=nxb[b][:], in0=nx1[b][:], in1=m2b[b][:]),
                    reads=[nx1[b], m2b[b]], writes=[nxb[b]])
                if t == 1:
                    trk.op(qV, lambda e, b=b: e.tensor_scalar(
                        out=abuf[b][:, 0:1], in0=abuf[b][:, 0:1],
                        scalar1=fl[:, 0:1], scalar2=None, op0=ALU.mult),
                        reads=[abuf[b], fl], writes=[abuf[b]])
                    trk.op(qV, lambda e, b=b: e.tensor_scalar(
                        out=bsc[:, 0:1], in0=nx1[b][:, 0:1],
                        scalar1=fl[:, 1:2], scalar2=None, op0=ALU.mult),
                        reads=[nx1[b], fl], writes=[bsc])
                    trk.op(qV, lambda e, b=b: e.scalar_tensor_tensor(
                        out=nxb[b][:, 0:1], in0=nxb[b][:, 0:1],
                        scalar=fl[:, 0:1], in1=bsc[:, 0:1],
                        op0=ALU.mult, op1=ALU.add),
                        reads=[nxb[b], fl, bsc], writes=[nxb[b]])
                if t == 0:
                    trk.op(qV, lambda e, b=b: e.tensor_tensor_scan(
                        out=hbf[b][:], data0=abuf[b][:], data1=nxb[b][:],
                        initial=0.0, op0=ALU.mult, op1=ALU.add),
                        reads=[abuf[b], nxb[b]], writes=[hbf[b]])
                else:
                    trk.op(qV, lambda e, b=b, rt=rt: e.tensor_tensor_scan(
                        out=hbf[b][:], data0=abuf[b][:], data1=nxb[b][:],
                        initial=hcar[:, rt:rt + 1], op0=ALU.mult, op1=ALU.add),
                        reads=[abuf[b], nxb[b], hcar], writes=[hbf[b]])
                trk.op(qG, lambda e, b=b, rt=rt: e.tensor_copy(
                    out=hcar[:, rt:rt + 1], in_=hbf[b][:, TT - 1:TT]),
                    reads=[hbf[b]], writes=[hcar])
                if t >= 1:
                    trk.dma(qY, SHH[b], lambda e, t=t, rt=rt, b=b: e.dma_start(
                        out=hd[t - 1, rt], in_=hbf[b][:]), reads=[hbf[b]],
                        writes=[("hd", t - 1, rt)])
        qY.wait(SHH[0], SHH[0].v)
        qY.wait(SHH[1], SHH[1].v)
        flush_phase()
        pb_.close()

    LXC = [Ctr(es, nc, f"LXC{i}") for i in range(2)]
    LHH = [Ctr(es, nc, f"LHH{i}") for i in range(2)]
    LYY = [Ctr(es, nc, f"LYY{i}") for i in range(2)]
    SAC = [Ctr(es, nc, f"SAC{i}") for i in range(2)]
    SX2 = [Ctr(es, nc, f"SX2{i}") for i in range(2)]
    LWC = Ctr(es, nc, "LWC")
    LW2C = [Ctr(es, nc, f"LW2C{i}") for i in range(2)]

    if "C" in phases:
        wo = inp("wo", [D, D])
        yd = scratch("yd", [nt_real, NRT, 128, TT], F32)
        hd = scratch("hd", [nt_real, NRT, 128, TT], F32)
        accd = scratch("accd", [nt_real, NRT, 128, TT], F32)
        xn2d = scratch("xn2d", [nt_real, NRT, 128, TT], BF16)
        phase_begin()
        pc_ = ExitStack()
        wo_sb = pc_.enter_context(nc.sbuf_tensor("wo_sb", [128, KT * D], BF16))
        xc = [pc_.enter_context(nc.sbuf_tensor(f"xc{i}", [128, D], F32))
              for i in range(4)]
        hld = [pc_.enter_context(nc.sbuf_tensor(f"hld{i}", [128, 4 * TT], F32))
               for i in range(2)]
        yld = [pc_.enter_context(nc.sbuf_tensor(f"yld{i}", [128, 4 * TT], F32))
               for i in range(2)]
        hybf = [pc_.enter_context(nc.sbuf_tensor(f"hybf{i}", [128, NRT * TT], BF16))
                for i in range(2)]
        x1b4 = [pc_.enter_context(nc.sbuf_tensor(f"x1b4{i}", [128, 4 * TT], F32))
                for i in range(2)]
        x1bf = [pc_.enter_context(nc.sbuf_tensor(f"x1bf{i}", [128, NRT * TT], BF16))
                for i in range(1)]
        x1sq = [pc_.enter_context(nc.sbuf_tensor(f"x1sq{i}", [128, TT], BF16))
                for i in range(1)]
        t2b = [pc_.enter_context(nc.sbuf_tensor(f"t2b{i}", [128, TT], F32))
               for i in range(2)]
        xn2b = [pc_.enter_context(nc.sbuf_tensor(f"xn2b{i}", [128, TT], BF16))
                for i in range(2)]
        stt = pc_.enter_context(nc.sbuf_tensor("stt", [1, 3 * TT], F32))
        psO = [pc_.enter_context(nc.psum_tensor(f"psO{i}", [128, TT], F32))
               for i in range(2)]
        psS1 = pc_.enter_context(nc.psum_tensor("psS1", [1, TT], F32))
        psS2 = pc_.enter_context(nc.psum_tensor("psS2", [1, TT], F32))
        psM = pc_.enter_context(nc.psum_tensor("psM", [128, TT], F32))
        psR = pc_.enter_context(nc.psum_tensor("psR", [128, TT], F32))

        trk.dma(qG, LWC, lambda e: e.dma_start(
            out=wo_sb[:].rearrange("p (kt m) -> p kt m", kt=KT),
            in_=wo.rearrange("(kt p) m -> p kt m", p=128)),
            reads=[wo], writes=[wo_sb])
        for t in range(nt_real):
            for c4 in range(4):
                xi = c4
                r0 = TT * (t + 1) + 128 * c4
                trk.dma(qY, LXC[xi % 2], lambda e, r0=r0, xi=xi: e.dma_start(
                    out=xc[xi][:], in_=xs[r0:r0 + 128, :]), writes=[xc[xi]])
            for g4 in range(4):
                b = g4 % 2
                trk.dma(qY, LHH[b], lambda e, t=t, g4=g4, b=b: e.dma_start(
                    out=hld[b][:].rearrange("p (r n) -> p r n", r=4),
                    in_=hd[t, 4 * g4:4 * g4 + 4].rearrange("r p n -> p r n")),
                    reads=[("hd", t, 4 * g4 + j) for j in range(4)],
                    writes=[hld[b]])
                trk.dma(qY, LYY[b], lambda e, t=t, g4=g4, b=b: e.dma_start(
                    out=yld[b][:].rearrange("p (r n) -> p r n", r=4),
                    in_=yd[t, 4 * g4:4 * g4 + 4].rearrange("r p n -> p r n")),
                    reads=[("yd", t, 4 * g4 + j) for j in range(4)],
                    writes=[yld[b]])
                trk.op(qV, lambda e, g4=g4, b=b, hp=t % 2: e.tensor_mul(
                    out=hybf[hp][:, 4 * g4 * TT:4 * (g4 + 1) * TT],
                    in0=hld[b][:], in1=yld[b][:]),
                    reads=[hld[b], yld[b]], writes=[hybf[t % 2]])
            for rt in range(NRT):
                b = rt % 2
                xis = [c4 for c4 in range(4)]
                def og(e, rt=rt, b=b, hp=t % 2, xis=tuple(xis)):
                    mm = None
                    for c4 in range(4):
                        mm = e.matmul(psO[b][:, c4 * 128:(c4 + 1) * 128],
                                      xc[xis[c4]][:, rt * 128:(rt + 1) * 128],
                                      ident[:], start=(c4 == 0), stop=False,
                                      is_transpose=True)
                    for kt in range(KT):
                        mm = e.matmul(psO[b][:],
                                      wo_sb[:, kt * D + rt * 128:kt * D + rt * 128 + 128],
                                      hybf[hp][:, kt * TT:(kt + 1) * TT],
                                      start=False, stop=(kt == KT - 1))
                    return mm
                trk.op(qT, og,
                       reads=[wo_sb, hybf[t % 2], ident] + [xc[x_] for x_ in xis],
                       writes=[psO[b]])
                g4, j4 = rt // 4, rt % 4
                bq = g4 % 2
                trk.op(qS, lambda e, rt=rt, b=b, bq=bq, j4=j4: e.activation(
                    out=x1b4[bq][:, j4 * TT:(j4 + 1) * TT], in_=psO[b][:],
                    func=AF.Identity,
                    bias=pp[:, PC_BOUT + rt:PC_BOUT + rt + 1]),
                    reads=[psO[b], pp], writes=[("x1b4", bq, j4)])
                if j4 == 3:
                    trk.dma(qY, SAC[bq], lambda e, t=t, g4=g4, bq=bq: e.dma_start(
                        out=accd[t, 4 * g4:4 * g4 + 4].rearrange("r p n -> p r n"),
                        in_=x1b4[bq][:].rearrange("p (r n) -> p r n", r=4)),
                        reads=[("x1b4", bq, j_) for j_ in range(4)],
                        writes=[("accd", t, 4 * g4 + j_) for j_ in range(4)])
                trk.op(qS, lambda e, rt=rt, bq=bq, j4=j4: e.activation(
                    out=x1bf[0][:, rt * TT:(rt + 1) * TT],
                    in_=x1b4[bq][:, j4 * TT:(j4 + 1) * TT],
                    func=AF.Identity), reads=[("x1b4", bq, j4)], writes=[x1bf[0]])
                trk.op(qS, lambda e, rt=rt, bq=bq, j4=j4: e.activation(
                    out=x1sq[0][:], in_=x1b4[bq][:, j4 * TT:(j4 + 1) * TT],
                    func=AF.Square),
                    reads=[("x1b4", bq, j4)], writes=[x1sq[0]])
                trk.op(qT, lambda e, rt=rt: e.matmul(
                    psS1[:], ones_col_bf[:], x1bf[0][:, rt * TT:(rt + 1) * TT],
                    start=(rt == 0), stop=(rt == NRT - 1)),
                    reads=[ones_col_bf, x1bf[0]], writes=[psS1])
                trk.op(qT, lambda e, rt=rt: e.matmul(
                    psS2[:], ones_col_bf[:], x1sq[0][:],
                    start=(rt == 0), stop=(rt == NRT - 1)),
                    reads=[ones_col_bf, x1sq[0]], writes=[psS2])
            trk.op(qV, lambda e: e.tensor_scalar(
                out=stt[0:1, 0:TT], in0=psS1[:], scalar1=1.0 / D, scalar2=None,
                op0=ALU.mult), reads=[psS1], writes=[stt])
            trk.op(qV, lambda e: e.tensor_scalar(
                out=stt[0:1, TT:2 * TT], in0=psS2[:], scalar1=1.0 / D,
                scalar2=None, op0=ALU.mult), reads=[psS2], writes=[stt])
            trk.op(qV, lambda e: e.tensor_mul(
                out=stt[0:1, 2 * TT:3 * TT], in0=stt[0:1, 0:TT],
                in1=stt[0:1, 0:TT]), reads=[stt], writes=[stt])
            trk.op(qV, lambda e: e.tensor_sub(
                out=stt[0:1, TT:2 * TT], in0=stt[0:1, TT:2 * TT],
                in1=stt[0:1, 2 * TT:3 * TT]), reads=[stt], writes=[stt])
            trk.op(qV, lambda e: e.tensor_scalar(
                out=stt[0:1, TT:2 * TT], in0=stt[0:1, TT:2 * TT],
                scalar1=1e-5, scalar2=None, op0=ALU.add),
                reads=[stt], writes=[stt])
            trk.op(qS, lambda e: e.activation(
                out=stt[0:1, 2 * TT:3 * TT], in_=stt[0:1, TT:2 * TT], func=AF.Ln),
                reads=[stt], writes=[stt])
            trk.op(qS, lambda e: e.activation(
                out=stt[0:1, 2 * TT:3 * TT], in_=stt[0:1, 2 * TT:3 * TT],
                func=AF.Exp, scale=-0.5), reads=[stt], writes=[stt])
            trk.op(qT, lambda e: e.matmul(
                psM[:], ones_row[:], stt[0:1, 0:TT], start=True, stop=True),
                reads=[ones_row, stt], writes=[psM])
            trk.op(qT, lambda e: e.matmul(
                psR[:], ones_row[:], stt[0:1, 2 * TT:3 * TT], start=True,
                stop=True), reads=[ones_row, stt], writes=[psR])
            for rt in range(NRT):
                b = rt % 2
                g4, j4 = rt // 4, rt % 4
                bq = g4 % 2
                trk.op(qV, lambda e, rt=rt, b=b: e.tensor_sub(
                    out=t2b[b][:], in0=x1bf[0][:, rt * TT:(rt + 1) * TT],
                    in1=psM[:]), reads=[x1bf[0], psM], writes=[t2b[b]])
                trk.op(qV, lambda e, rt=rt, b=b: e.tensor_mul(
                    out=xn2b[b][:], in0=t2b[b][:], in1=psR[:]),
                    reads=[t2b[b], psR], writes=[xn2b[b]])
                trk.dma(qY, SX2[b], lambda e, t=t, rt=rt, b=b: e.dma_start(
                    out=xn2d[t, rt], in_=xn2b[b][:]), reads=[xn2b[b]],
                    writes=[("xn2d", t, rt)])
        qY.wait(SAC[0], SAC[0].v)
        qY.wait(SAC[1], SAC[1].v)
        qY.wait(SX2[0], SX2[0].v)
        qY.wait(SX2[1], SX2[1].v)
        flush_phase()
        pc_.close()

    LX20 = [Ctr(es, nc, f"LX20{i}") for i in range(2)]
    LGW = Ctr(es, nc, "LGW")
    SCB = Ctr(es, nc, "SCB")
    SCB1 = Ctr(es, nc, "SCB1")

    if "D0" in phases:
        gwT = inp("gwT", [D, 16])
        xn2d = scratch("xn2d", [nt_real, NRT, 128, TT], BF16)
        combd = scratch("combd", [NE, nt_real * TT], F32)
        phase_begin()
        pd0 = ExitStack()
        gw_sb = pd0.enter_context(nc.sbuf_tensor("gw_sb", [128, KT * 16], BF16))
        x2t = [pd0.enter_context(nc.sbuf_tensor(f"x2t{i}", [128, KT * TT], BF16))
               for i in range(2)]
        scf = pd0.enter_context(nc.sbuf_tensor("scf", [16, TT], F32))
        sct = pd0.enter_context(nc.sbuf_tensor("sct", [128, 64], F32))
        esc = pd0.enter_context(nc.sbuf_tensor("esc", [128, 16], F32))
        prb = pd0.enter_context(nc.sbuf_tensor("prb", [128, 16], F32))
        wrk = pd0.enter_context(nc.sbuf_tensor("wrk", [128, 16], F32))
        msc = pd0.enter_context(nc.sbuf_tensor("msc", [128, 16], F32))
        cmb = pd0.enter_context(nc.sbuf_tensor("cmb", [128, 16], F32))
        b0 = pd0.enter_context(nc.sbuf_tensor("b0", [128, 1], F32))
        b1 = pd0.enter_context(nc.sbuf_tensor("b1", [128, 1], F32))
        b2 = pd0.enter_context(nc.sbuf_tensor("b2", [128, 1], F32))
        ccrow = pd0.enter_context(nc.sbuf_tensor("ccrow", [16, 128], F32))
        onr = pd0.enter_context(nc.sbuf_tensor("onr", [1, nt_real * TT], F32))
        psGt = pd0.enter_context(nc.psum_tensor("psGt", [16, TT], F32))
        ptg = pd0.enter_context(nc.psum_tensor("ptg", [128, 64], F32))
        pcc = pd0.enter_context(nc.psum_tensor("pcc", [16, 128], F32))

        trk.dma(qG, LGW, lambda e: e.dma_start(
            out=gw_sb[:].rearrange("p (kt m) -> p kt m", kt=KT),
            in_=gwT.rearrange("(kt p) m -> p kt m", p=128)),
            reads=[gwT], writes=[gw_sb])
        trk.op(qV, lambda e: e.memset(onr[:], 1.0), writes=[onr])
        trk.dma(qY, SCB1, lambda e: e.dma_start(
            out=combd[16:17, :], in_=onr[:]), reads=[onr], writes=[("combd", 16)])
        trk.dma(qY, SCB1, lambda e: e.dma_start(
            out=combd[17:18, :], in_=onr[:]), reads=[onr], writes=[("combd", 17)],
            first=False)

        for t in range(nt_real):
            p2 = t % 2
            trk.dma(qY, LX20[p2], lambda e, t=t, p2=p2: e.dma_start(
                out=x2t[p2][:].rearrange("p (rt n) -> p rt n", rt=NRT),
                in_=xn2d[t].rearrange("rt p n -> p rt n")),
                reads=[("xn2d", t, r_) for r_ in range(NRT)], writes=[x2t[p2]])
            def gg(e, p2=p2):
                mm = None
                for kt in range(KT):
                    mm = e.matmul(psGt[:], gw_sb[:, kt * 16:(kt + 1) * 16],
                                  x2t[p2][:, kt * TT:(kt + 1) * TT],
                                  start=(kt == 0), stop=(kt == KT - 1))
                return mm
            trk.op(qT, gg, reads=[gw_sb, x2t[p2]], writes=[psGt])
            trk.op(qS, lambda e: e.activation(
                out=scf[:], in_=psGt[:], func=AF.Identity),
                reads=[psGt], writes=[scf])
            def tg(e):
                mm = None
                for c4 in range(4):
                    mm = e.transpose(ptg[:, c4 * 16:(c4 + 1) * 16],
                                     scf[0:16, c4 * 128:(c4 + 1) * 128],
                                     ident[0:16, 0:16])
                return mm
            trk.op(qT, tg, reads=[scf, ident], writes=[ptg])
            trk.op(qS, lambda e: e.activation(
                out=sct[:], in_=ptg[:], func=AF.Identity),
                reads=[ptg], writes=[sct])
            for c4 in range(4):
                ch = sct[:, c4 * 16:(c4 + 1) * 16]
                trk.op(qV, lambda e, ch=ch: e.reduce_max(
                    out=b0[:], in_=ch, axis=AXX, negate=True),
                    reads=[sct], writes=[b0])
                trk.op(qS, lambda e, ch=ch: e.activation(
                    out=esc[:], in_=ch, func=AF.Exp, bias=b0[:]),
                    reads=[sct, b0], writes=[esc])
                trk.op(qV, lambda e: e.reduce_sum(
                    out=b1[:], in_=esc[:], axis=AXX), reads=[esc], writes=[b1])
                trk.op(qV, lambda e: e.reciprocal(out=b2[:], in_=b1[:]),
                       reads=[b1], writes=[b2])
                trk.op(qV, lambda e: e.tensor_scalar(
                    out=prb[:], in0=esc[:], scalar1=b2[:], scalar2=None,
                    op0=ALU.mult), reads=[esc, b2], writes=[prb])
                trk.op(qV, lambda e: e.tensor_copy(out=wrk[:], in_=prb[:]),
                       reads=[prb], writes=[wrk])
                for it in range(5):
                    trk.op(qV, lambda e: e.reduce_max(
                        out=b0[:], in_=wrk[:], axis=AXX),
                        reads=[wrk], writes=[b0])
                    trk.op(qV, lambda e: e.tensor_scalar(
                        out=msc[:], in0=wrk[:], scalar1=b0[:], scalar2=2.0,
                        op0=ALU.is_ge, op1=ALU.mult),
                        reads=[wrk, b0], writes=[msc])
                    trk.op(qV, lambda e: e.tensor_sub(
                        out=wrk[:], in0=wrk[:], in1=msc[:]),
                        reads=[wrk, msc], writes=[wrk])
                trk.op(qV, lambda e: e.reduce_max(
                    out=b0[:], in_=wrk[:], axis=AXX), reads=[wrk], writes=[b0])
                trk.op(qV, lambda e: e.tensor_scalar(
                    out=msc[:], in0=prb[:], scalar1=b0[:], scalar2=None,
                    op0=ALU.is_ge), reads=[prb, b0], writes=[msc])
                trk.op(qV, lambda e: e.tensor_mul(
                    out=wrk[:], in0=prb[:], in1=msc[:]),
                    reads=[prb, msc], writes=[wrk])
                trk.op(qV, lambda e: e.reduce_sum(
                    out=b1[:], in_=wrk[:], axis=AXX), reads=[wrk], writes=[b1])
                trk.op(qV, lambda e: e.tensor_scalar(
                    out=b1[:], in0=b1[:], scalar1=1e-20, scalar2=None,
                    op0=ALU.add), reads=[b1], writes=[b1])
                trk.op(qV, lambda e: e.reciprocal(out=b2[:], in_=b1[:]),
                       reads=[b1], writes=[b2])
                trk.op(qV, lambda e: e.tensor_scalar(
                    out=cmb[:], in0=wrk[:], scalar1=b2[:], scalar2=None,
                    op0=ALU.mult), reads=[wrk, b2], writes=[cmb])
                trk.op(qT, lambda e: e.transpose(pcc[:], cmb[:], ident[:]),
                       reads=[cmb, ident], writes=[pcc])
                trk.op(qS, lambda e: e.activation(
                    out=ccrow[:], in_=pcc[:], func=AF.Identity),
                    reads=[pcc], writes=[ccrow])
                c0 = t * TT + c4 * 128
                trk.dma(qY, SCB, lambda e, c0=c0: e.dma_start(
                    out=combd[0:16, c0:c0 + 128], in_=ccrow[:]),
                    reads=[ccrow], writes=[("combd", t, c4)])
        qY.wait(SCB, SCB.v)
        flush_phase()
        pd0.close()

    LW13 = Ctr(es, nc, "LW13")
    LW13H = Ctr(es, nc, "LW13H")
    LW2D = Ctr(es, nc, "LW2D")
    LX2D = [Ctr(es, nc, f"LX2D{i}") for i in range(2)]
    LCR = [Ctr(es, nc, f"LCR{i}") for i in range(2)]
    SACCD = [Ctr(es, nc, f"SACCD{i}") for i in range(NRT)]

    if "D" in phases:
        w13p = inp("w13p", [NE, D, 2816])
        w2p = inp("w2p", [NE, IS, D])
        accd = scratch("accd", [nt_real, NRT, 128, TT], F32)
        xn2d = scratch("xn2d", [nt_real, NRT, 128, TT], BF16)
        combd = scratch("combd", [NE, nt_real * TT], F32)
        phase_begin()
        pd_ = ExitStack()
        w13_sb = pd_.enter_context(nc.sbuf_tensor("w13_sb", [128, KT * 2816], BF16))
        w2_sb = pd_.enter_context(nc.sbuf_tensor("w2_sb", [128, IC * D], BF16))
        x2d = [pd_.enter_context(nc.sbuf_tensor(f"x2d{i}", [128, KT * TT], BF16))
               for i in range(2)]
        s_sb = pd_.enter_context(nc.sbuf_tensor("s_sb", [128, IC * TT], BF16))
        sg = [pd_.enter_context(nc.sbuf_tensor(f"sg{i}", [128, TT], F32))
              for i in range(2)]
        ob = [pd_.enter_context(nc.sbuf_tensor(f"ob{i}", [128, TT], F32))
              for i in range(2)]
        cps = pd_.enter_context(nc.sbuf_tensor("cps", [128, TT], F32))
        crow = [pd_.enter_context(nc.sbuf_tensor(f"crow{i}", [1, TT], F32))
                for i in range(2)]
        psG = [pd_.enter_context(nc.psum_tensor(f"psG{i}", [128, TT], F32))
               for i in range(2)]
        psU = [pd_.enter_context(nc.psum_tensor(f"psU{i}", [128, TT], F32))
               for i in range(2)]
        psC = pd_.enter_context(nc.psum_tensor("psC", [128, TT], F32))
        psO2 = [pd_.enter_context(nc.psum_tensor(f"psO2{i}", [128, TT], F32))
                for i in range(2)]

        LOCH = 6

        def load_w13(s, first=False, half=None):
            wv = w13_sb[:].rearrange("p (kt m) -> p kt m", kt=KT)
            sv = w13p[s].rearrange("(kt p) m -> p kt m", p=128)
            if half in (None, "lo"):
                trk.dma(qG, LW13, lambda e: e.dma_start(
                    out=wv[:, :, 0:LOCH * 128], in_=sv[:, :, 0:LOCH * 128]),
                    reads=[w13p], writes=[("w13L",)])
                trk.dma(qG, LW13, lambda e: e.dma_start(
                    out=wv[:, :, IS:IS + LOCH * 128],
                    in_=sv[:, :, IS:IS + LOCH * 128]),
                    reads=[w13p], writes=[("w13L",)], first=False)
            if half in (None, "hi"):
                trk.dma(qG, LW13H, lambda e: e.dma_start(
                    out=wv[:, :, LOCH * 128:IS], in_=sv[:, :, LOCH * 128:IS]),
                    reads=[w13p], writes=[("w13H",)])
                trk.dma(qG, LW13H, lambda e: e.dma_start(
                    out=wv[:, :, IS + LOCH * 128:2 * IS],
                    in_=sv[:, :, IS + LOCH * 128:2 * IS]),
                    reads=[w13p], writes=[("w13H",)], first=False)

        for s in range(NE):
            for t in range(nt_real):
                p2 = t % 2
                trk.dma(qY, LX2D[p2], lambda e, t=t, p2=p2: e.dma_start(
                    out=x2d[p2][:].rearrange("p (rt n) -> p rt n", rt=NRT),
                    in_=xn2d[t].rearrange("rt p n -> p rt n")),
                    reads=[("xn2d", t, r_) for r_ in range(NRT)],
                    writes=[x2d[p2]])
                if s < 16:
                    crdeps = [("combd", t, c4_) for c4_ in range(4)]
                else:
                    crdeps = [("combd", s)]
                trk.dma(qY, LCR[p2], lambda e, s=s, t=t, p2=p2: e.dma_start(
                    out=crow[p2][:], in_=combd[s:s + 1, t * TT:(t + 1) * TT]),
                    reads=crdeps, writes=[crow[p2]])
                if s == 0 and t == 0:
                    load_w13(s, first=True)
                    trk.dma(qG, LW2D, lambda e, s=s: e.dma_start(
                        out=w2_sb[:].rearrange("p (kt m) -> p kt m", kt=IC),
                        in_=w2p[s].rearrange("(kt p) m -> p kt m", p=128)),
                        reads=[w2p], writes=[w2_sb])
                trk.op(qT, lambda e, p2=p2: e.matmul(
                    psC[:], ones_row[:], crow[p2][:], start=True, stop=True),
                    reads=[ones_row, crow[p2]], writes=[psC])
                trk.op(qV, lambda e: e.tensor_copy(out=cps[:], in_=psC[:]),
                       reads=[psC], writes=[cps])
                for c in range(IC):
                    b = c % 2

                    def g1g(e, p2=p2, c=c, b=b):
                        mm = None
                        for kt in range(KT):
                            mm = e.matmul(
                                psG[b][:],
                                w13_sb[:, kt * 2816 + c * 128:kt * 2816 + c * 128 + 128],
                                x2d[p2][:, kt * TT:(kt + 1) * TT],
                                start=(kt == 0), stop=(kt == KT - 1))
                        return mm

                    def u1g(e, p2=p2, c=c, b=b):
                        mm = None
                        for kt in range(KT):
                            mm = e.matmul(
                                psU[b][:],
                                w13_sb[:, kt * 2816 + IS + c * 128:kt * 2816 + IS + c * 128 + 128],
                                x2d[p2][:, kt * TT:(kt + 1) * TT],
                                start=(kt == 0), stop=(kt == KT - 1))
                        return mm
                    wkey = ("w13L",) if c < LOCH else ("w13H",)
                    trk.op(qT, g1g, reads=[wkey, x2d[p2]], writes=[psG[b]])
                    trk.op(qT, u1g, reads=[wkey, x2d[p2]], writes=[psU[b]])
                    if (t == nt_real - 1 and c == LOCH - 1 and s + 1 < NE):
                        load_w13(s + 1, half="lo")
                    trk.op(qS, lambda e, b=b: e.activation(
                        out=sg[b][:], in_=psG[b][:], func=AF.Silu),
                        reads=[psG[b]], writes=[sg[b]])
                    trk.op(qV, lambda e, b=b, c=c: e.tensor_mul(
                        out=s_sb[:, c * TT:(c + 1) * TT], in0=sg[b][:],
                        in1=psU[b][:]), reads=[sg[b], psU[b]], writes=[s_sb])
                if t == nt_real - 1 and s + 1 < NE:
                    load_w13(s + 1, half="hi")
                for rt in range(NRT):
                    b = rt % 2

                    def o2g(e, rt=rt, b=b):
                        mm = None
                        for kt in range(IC):
                            mm = e.matmul(
                                psO2[b][:],
                                w2_sb[:, kt * D + rt * 128:kt * D + rt * 128 + 128],
                                s_sb[:, kt * TT:(kt + 1) * TT],
                                start=(kt == 0), stop=(kt == IC - 1))
                        return mm
                    trk.op(qT, o2g, reads=[w2_sb, s_sb], writes=[psO2[b]])
                    trk.op(qV, lambda e, b=b: e.tensor_mul(
                        out=ob[b][:], in0=cps[:], in1=psO2[b][:]),
                        reads=[cps, psO2[b]], writes=[ob[b]])
                    trk.dma(qG, SACCD[rt], lambda e, t=t, rt=rt, b=b: e.dma_start(
                        out=accd[t, rt], in_=ob[b][:], accum_op=ALU.add),
                        reads=[ob[b]], writes=[("accd", t, rt)])
                if t == nt_real - 1 and s + 1 < NE:
                    trk.dma(qG, LW2D, lambda e, s=s: e.dma_start(
                        out=w2_sb[:].rearrange("p (kt m) -> p kt m", kt=IC),
                        in_=w2p[s + 1].rearrange("(kt p) m -> p kt m", p=128)),
                        reads=[w2p], writes=[w2_sb])
        flush_phase()
        pd_.close()

    LAE = [Ctr(es, nc, f"LAE{i}") for i in range(2)]
    SOUT = [Ctr(es, nc, f"SOUT{i}") for i in range(2)]

    if "E" in phases:
        accd = scratch("accd", [nt_real, NRT, 128, TT], F32)
        out = nc.dram_tensor("out", [nt_real * TT, D], F32, kind="ExternalOutput").ap()
        phase_begin()
        pe_ = ExitStack()
        aee = [pe_.enter_context(nc.sbuf_tensor(f"aee{i}", [128, NRT * TT], F32))
               for i in range(2)]
        oy = [pe_.enter_context(nc.sbuf_tensor(f"oy{i}", [128, D], F32))
              for i in range(2)]
        psE = [pe_.enter_context(nc.psum_tensor(f"psE{i}", [128, TT], F32))
               for i in range(2)]

        for t in range(nt_real):
            p2 = t % 2
            trk.dma(qY, LAE[p2], lambda e, t=t, p2=p2: e.dma_start(
                out=aee[p2][:].rearrange("p (rt n) -> p rt n", rt=NRT),
                in_=accd[t].rearrange("rt p n -> p rt n")),
                reads=[("accd", t, r_) for r_ in range(NRT)], writes=[aee[p2]])
            for c4 in range(4):
                pc2 = c4 % 2
                for g4 in range(4):

                    def tpe(e, p2=p2, c4=c4, g4=g4, pc2=pc2):
                        mm = None
                        for j in range(4):
                            rt = g4 * 4 + j
                            mm = e.transpose(
                                psE[pc2][:, j * 128:(j + 1) * 128],
                                aee[p2][:, rt * TT + c4 * 128:rt * TT + c4 * 128 + 128],
                                ident[:])
                        return mm
                    trk.op(qT, tpe, reads=[aee[p2], ident], writes=[psE[pc2]])
                    trk.op(qS, lambda e, g4=g4, pc2=pc2: e.activation(
                        out=oy[pc2][:, g4 * TT:(g4 + 1) * TT], in_=psE[pc2][:],
                        func=AF.Identity), reads=[psE[pc2]], writes=[oy[pc2]])
                r0 = t * TT + c4 * 128
                trk.dma(qY, SOUT[pc2], lambda e, r0=r0, pc2=pc2: e.dma_start(
                    out=out[r0:r0 + 128, :], in_=oy[pc2][:]),
                    reads=[oy[pc2]], writes=[("out", r0)])
        flush_phase()
        pe_.close()

    for s in sorted(trk.dma_sems, key=lambda c: c.name):
        if s.v > 0:
            qG.wait(s, s.v)
    with nc.Block() as blk:
        @blk.gpsimd
        def _(e):
            qG.replay(e)

    es.close()
    return nc


def _pack_inputs(inputs, nt_real=8, n_cores=8, names=None):
    f32 = np.float32
    x = np.ascontiguousarray(np.asarray(inputs["L0_attn_x"]).reshape(32768, D))
    chunk = 4096
    warm = TT
    in_maps = []

    ppk = np.zeros((128, PC_N), f32)

    def col16(v):
        return np.ascontiguousarray(np.asarray(v).reshape(16, 128).T)

    ppk[:, PC_BY:PC_BY + 16] = col16(inputs["L0_attn_b_y"])
    ppk[:, PC_BX:PC_BX + 16] = col16(inputs["L0_attn_b_x"])
    ppk[:, PC_BIG:PC_BIG + 16] = col16(inputs["L0_attn_b_ig"])
    ppk[:, PC_BAG:PC_BAG + 16] = col16(inputs["L0_attn_b_ag"])
    ppk[:, PC_BOUT:PC_BOUT + 16] = col16(inputs["L0_attn_b_out"])
    ppk[:, PC_CB:PC_CB + 16] = col16(inputs["L0_attn_conv_b"])
    cw = np.asarray(inputs["L0_attn_conv_w"]).reshape(4, 16, 128).transpose(2, 1, 0)
    ppk[:, PC_CW:PC_CW + 64] = np.ascontiguousarray(cw.reshape(128, 64))
    ppk[:, PC_AP:PC_AP + 16] = col16(inputs["L0_attn_a_param"])

    gwT_ = np.ascontiguousarray(np.asarray(inputs["L0_ffn_gate_w"]).T)

    w13 = np.asarray(inputs["L0_ffn_w13"])
    sw13T = np.ascontiguousarray(np.asarray(inputs["L0_ffn_sw13"]).T)
    w2 = np.asarray(inputs["L0_ffn_w2"])
    sw2T = np.ascontiguousarray(np.asarray(inputs["L0_ffn_sw2"]).T)
    w13p_ = np.empty((NE, D, 2816), f32)
    w13p_[0:16] = w13.transpose(0, 2, 1)
    w13p_[16] = np.concatenate([sw13T[:, 0:IS], sw13T[:, 2 * IS:3 * IS]], axis=1)
    w13p_[17] = np.concatenate([sw13T[:, IS:2 * IS], sw13T[:, 3 * IS:4 * IS]], axis=1)
    w2p_ = np.empty((NE, IS, D), f32)
    w2p_[0:16] = w2.transpose(0, 2, 1)
    w2p_[16] = sw2T[0:IS]
    w2p_[17] = sw2T[IS:2 * IS]

    common = {
        "ppack": ppk, "gwT": gwT_, "w13p": w13p_, "w2p": w2p_,
        "wy": np.asarray(inputs["L0_attn_w_y"]),
        "wx": np.asarray(inputs["L0_attn_w_x"]),
        "wig": np.asarray(inputs["L0_attn_w_ig"]),
        "wag": np.asarray(inputs["L0_attn_w_ag"]),
        "wo": np.asarray(inputs["L0_attn_w_out"]),
    }
    if names is not None:
        common = {k: v for k, v in common.items() if k in names}
    for c in range(n_cores):
        lo = chunk * c
        if c % 4 == 0:
            sh = np.concatenate([np.zeros((warm, D), f32), x[lo:lo + nt_real * TT]],
                                axis=0)
            flag = 0.0
        else:
            sh = np.ascontiguousarray(x[lo - warm:lo + nt_real * TT])
            flag = 1.0
        m = dict(common)
        m["xs"] = sh
        m["flags"] = np.tile(np.array([[flag, 1.0 - flag]], f32), (128, 1))
        in_maps.append(m)
    return in_maps


def kernel(**inputs):
    n_cores = 8
    nc = build_program()
    in_maps = _pack_inputs(inputs, nt_real=8, n_cores=n_cores)
    res = run_bass_kernel_spmd(nc, in_maps, core_ids=list(range(n_cores)))
    out = np.concatenate([res.results[c]["out"] for c in range(n_cores)], axis=0)
    return out.reshape(2, 16384, D)


# version 08
# speedup vs baseline: 1.1937x; 1.0170x over previous
import numpy as np
import ml_dtypes
from contextlib import ExitStack

import concourse.bass as bass
import concourse.mybir as mybir
from concourse.bass_utils import run_bass_kernel_spmd

F32 = mybir.dt.float32
BF16 = mybir.dt.bfloat16
AF = mybir.ActivationFunctionType
ALU = mybir.AluOpType
AXX = mybir.AxisListType.X

D = 2048
NRT = 16
TT = 512
KT = 16
IC = 11
NE = 18
IS = 1408

PC_BY, PC_BX, PC_BIG, PC_BAG, PC_BOUT, PC_CB = 0, 16, 32, 48, 64, 80
PC_CW = 96
PC_AP = 160
PC_N = 176


class Ctr:
    def __init__(self, es, nc, name):
        self.h = es.enter_context(nc.semaphore(name))
        self.v = 0
        self.name = name


def _key(ap):
    if isinstance(ap, (str, tuple)):
        return ap
    t = getattr(ap, "tensor", None)
    return t.name if t is not None else str(ap)


class Q:
    def __init__(self, name, prog):
        self.name = name
        self.prog = prog
        self.waited = {}
        self.pending = []

    def wait(self, ctr, val):
        if val <= 0:
            return
        if self.waited.get(id(ctr), -1) >= val:
            return
        self.pending.append(lambda e, h=ctr.h, v=val: e.wait_ge(h, v))
        self.waited[id(ctr)] = val

    def raw(self, fn):
        self.pending.append(fn)

    def replay(self, eng):
        for fn in self.pending:
            fn(eng)
        self.pending = []


class Trk:
    def __init__(self):
        self.writer = {}
        self.readers = {}
        self.dma_sems = set()
        self.dma_q = {}

    def _pre(self, q, reads, writes):
        for ap in reads:
            w = self.writer.get(_key(ap))
            if w:
                q.wait(w[0], w[1])
        for ap in writes:
            k = _key(ap)
            w = self.writer.get(k)
            if w:
                q.wait(w[0], w[1])
            for (ctr, val) in self.readers.get(k, {}).values():
                q.wait(ctr, val)

    def _post(self, q, reads, writes, ctr, val):
        for ap in writes:
            k = _key(ap)
            self.writer[k] = (ctr, val)
            self.readers[k] = {}
        for ap in reads:
            self.readers.setdefault(_key(ap), {})[q.name] = (ctr, val)

    def op(self, q, fn, reads=(), writes=(), waits=()):
        self._pre(q, reads, writes)
        for (ctr, val) in waits:
            q.wait(ctr, val)
        q.prog.v += 1
        sh, sv = q.prog.h, q.prog.v
        q.raw(lambda e: fn(e).then_inc(sh, 1))
        self._post(q, reads, writes, q.prog, q.prog.v)

    def dma(self, q, sem, fn, reads=(), writes=(), waits=(), first=True,
            no_drain=False):
        self.dma_sems.add(sem)
        if not no_drain:
            self.dma_q[sem] = q
        if first and sem.v > 0:
            q.wait(sem, sem.v)
        self._pre(q, reads, writes)
        for (ctr, val) in waits:
            q.wait(ctr, val)
        sem.v += 16
        sh = sem.h
        q.raw(lambda e: fn(e).then_inc(sh, 16))
        self._post(q, reads, writes, sem, sem.v)


def build_program(nt_real=8, phases=("0", "A", "B", "C", "D0", "D", "E"), dbg=()):
    nwt = nt_real + 1
    rows = nwt * TT
    nc = bass.Bass("TRN2", target_bir_lowering=False)

    _dram = {}

    def inp(name, shape):
        if name not in _dram:
            _dram[name] = nc.dram_tensor(name, shape, F32, kind="ExternalInput").ap()
        return _dram[name]

    def scratch(name, shape, dt):
        if name not in _dram:
            kind = "ExternalOutput" if name in dbg else "Internal"
            _dram[name] = nc.dram_tensor(name, shape, dt, kind=kind).ap()
        return _dram[name]

    xs = inp("xs", [rows, D])
    flags = inp("flags", [128, 2])
    ppack = inp("ppack", [128, PC_N])

    es = ExitStack()
    pp = es.enter_context(nc.sbuf_tensor("pp", [128, PC_N], F32))
    fl = es.enter_context(nc.sbuf_tensor("fl", [128, 2], F32))
    sl = es.enter_context(nc.sbuf_tensor("sl", [128, 48], F32))
    nbias = es.enter_context(nc.sbuf_tensor("nbias", [128, 32], F32))
    cst = es.enter_context(nc.sbuf_tensor("cst", [128, 8], F32))
    ident = es.enter_context(nc.sbuf_tensor("ident", [128, 128], F32))
    ones_col_bf = es.enter_context(nc.sbuf_tensor("ones_col_bf", [128, 1], BF16))
    ones_row = es.enter_context(nc.sbuf_tensor("ones_row", [1, 128], F32))
    C_ONE, C_EPS, C_INVD, C_MONE, C_TINY = 0, 1, 2, 3, 4

    trk = Trk()
    PB = Ctr(es, nc, "PB")
    NENG = 5
    TEP = Ctr(es, nc, "TEP")
    VEP = Ctr(es, nc, "VEP")
    SEP = Ctr(es, nc, "SEP")
    GPP = Ctr(es, nc, "GPP")
    qT = Q("T", TEP)
    qV = Q("V", VEP)
    qS = Q("S", SEP)
    qG = Q("G", GPP)
    qY = Q("Y", None)
    ALLQ = [qT, qV, qS, qG, qY]

    def flush_phase():
        for sem, q in trk.dma_q.items():
            q.wait(sem, sem.v)
        for q in ALLQ:
            q.raw(lambda e: e.sem_inc(PB.h, 1))
        PB.v += NENG
        with nc.Block() as blk:
            @blk.tensor
            def _(e):
                qT.replay(e)

            @blk.vector
            def _(e):
                qV.replay(e)

            @blk.scalar
            def _(e):
                qS.replay(e)

            @blk.gpsimd
            def _(e):
                qG.replay(e)

            @blk.sync
            def _(e):
                qY.replay(e)

    def phase_begin():
        pbw = PB.v
        if pbw > 0:
            for q in ALLQ:
                q.raw(lambda e, v=pbw: e.wait_ge(PB.h, v))

    L0X = [Ctr(es, nc, f"L0X{i}") for i in range(4)]
    S0X = [Ctr(es, nc, f"S0X{i}") for i in range(4)]
    LPP = Ctr(es, nc, "LPP")
    LFL = Ctr(es, nc, "LFL")
    LWA = [Ctr(es, nc, f"LWA{i}") for i in range(2)]
    LXT = [Ctr(es, nc, f"LXT{i}") for i in range(3)]
    SYY = [Ctr(es, nc, f"SYY{i}") for i in range(2)]
    SCV = [Ctr(es, nc, f"SCV{i}") for i in range(2)]

    n0 = rows // 128

    if "0" in phases:
        xn1 = scratch("xn1", [rows, D], BF16)
        phase_begin()
        p0 = ExitStack()
        NP0 = 4
        xld = [p0.enter_context(nc.sbuf_tensor(f"xld{i}", [128, D], F32)) for i in range(NP0)]
        sqs = [p0.enter_context(nc.sbuf_tensor(f"sqs{i}", [128, D], F32)) for i in range(NP0)]
        xnb = [p0.enter_context(nc.sbuf_tensor(f"xnb{i}", [128, D], BF16)) for i in range(NP0)]
        st0 = [p0.enter_context(nc.sbuf_tensor(f"st0{i}", [128, 8], F32)) for i in range(NP0)]

        trk.dma(qY, LPP, lambda e: e.dma_start(out=pp[:], in_=ppack[:]),
                reads=[ppack], writes=[pp])
        trk.dma(qY, LFL, lambda e: e.dma_start(out=fl[:], in_=flags[:]),
                reads=[flags], writes=[fl])
        trk.op(qV, lambda e: e.memset(ident[:], 0.0), writes=[ident])
        trk.op(qG, lambda e: e.affine_select(
            out=ident[:], in_=ident[:], compare_op=ALU.not_equal,
            fill=1.0, base=0, pattern=[[-1, 128]], channel_multiplier=1),
            reads=[ident], writes=[ident])
        trk.op(qV, lambda e: e.memset(ones_col_bf[:], 1.0), writes=[ones_col_bf])
        trk.op(qV, lambda e: e.memset(ones_row[:], 1.0), writes=[ones_row])
        for col, val in ((C_ONE, 1.0), (C_EPS, 1e-5), (C_INVD, 1.0 / D),
                         (C_MONE, -1.0), (C_TINY, 1e-20)):
            trk.op(qV, lambda e, col=col, val=val: e.memset(
                cst[:, col:col + 1], val), writes=[cst])
        trk.op(qV, lambda e: e.tensor_scalar(
            out=nbias[:, 0:16], in0=pp[:, PC_BIG:PC_BIG + 16],
            scalar1=-1.0, scalar2=None, op0=ALU.mult), reads=[pp], writes=[nbias])
        trk.op(qV, lambda e: e.tensor_scalar(
            out=nbias[:, 16:32], in0=pp[:, PC_BAG:PC_BAG + 16],
            scalar1=-1.0, scalar2=None, op0=ALU.mult), reads=[pp], writes=[nbias])
        trk.op(qS, lambda e: e.activation(
            out=sl[:, 16:32], in_=pp[:, PC_AP:PC_AP + 16], func=AF.Abs),
            reads=[pp], writes=[sl])
        trk.op(qS, lambda e: e.activation(
            out=sl[:, 16:32], in_=sl[:, 16:32], func=AF.Exp, scale=-1.0),
            reads=[sl], writes=[sl])
        trk.op(qS, lambda e: e.activation(
            out=sl[:, 16:32], in_=sl[:, 16:32], func=AF.Ln,
            bias=cst[:, C_ONE:C_ONE + 1]), reads=[sl, cst], writes=[sl])
        trk.op(qV, lambda e: e.tensor_scalar(
            out=sl[:, 0:16], in0=pp[:, PC_AP:PC_AP + 16],
            scalar1=0.0, scalar2=None, op0=ALU.max), reads=[pp, sl], writes=[sl])
        trk.op(qV, lambda e: e.tensor_add(
            out=sl[:, 32:48], in0=sl[:, 0:16], in1=sl[:, 16:32]),
            reads=[sl], writes=[sl])
        trk.op(qV, lambda e: e.tensor_scalar(
            out=sl[:, 0:16], in0=sl[:, 32:48],
            scalar1=-8.0, scalar2=None, op0=ALU.mult), reads=[sl], writes=[sl])
        trk.op(qV, lambda e: e.tensor_scalar(
            out=sl[:, 16:32], in0=sl[:, 32:48],
            scalar1=-16.0, scalar2=None, op0=ALU.mult), reads=[sl], writes=[sl])
        for i in range(n0):
            p = i % 4
            trk.dma(qY, L0X[p], lambda e, i=i, p=p: e.dma_start(
                out=xld[p][:], in_=xs[128 * i:128 * (i + 1), :]), writes=[xld[p]])
            trk.op(qS, lambda e, p=p: e.activation(
                out=sqs[p][:], in_=xld[p][:], func=AF.Square,
                accum_out=st0[p][:, 1:2]),
                reads=[xld[p]], writes=[sqs[p], st0[p]])
            trk.op(qV, lambda e, p=p: e.reduce_sum(
                out=st0[p][:, 0:1], in_=xld[p][:], axis=AXX),
                reads=[xld[p]], writes=[st0[p]])
            trk.op(qV, lambda e, p=p: e.tensor_scalar(
                out=st0[p][:, 2:3], in0=st0[p][:, 0:1],
                scalar1=1.0 / D, scalar2=None, op0=ALU.mult),
                reads=[st0[p]], writes=[st0[p]])
            trk.op(qV, lambda e, p=p: e.tensor_scalar(
                out=st0[p][:, 3:4], in0=st0[p][:, 1:2],
                scalar1=1.0 / D, scalar2=None, op0=ALU.mult),
                reads=[st0[p]], writes=[st0[p]])
            trk.op(qV, lambda e, p=p: e.tensor_tensor(
                out=st0[p][:, 4:5], in0=st0[p][:, 2:3], in1=st0[p][:, 2:3],
                op=ALU.mult), reads=[st0[p]], writes=[st0[p]])
            trk.op(qV, lambda e, p=p: e.tensor_sub(
                out=st0[p][:, 3:4], in0=st0[p][:, 3:4], in1=st0[p][:, 4:5]),
                reads=[st0[p]], writes=[st0[p]])
            trk.op(qV, lambda e, p=p: e.tensor_scalar(
                out=st0[p][:, 3:4], in0=st0[p][:, 3:4],
                scalar1=1e-5, scalar2=None, op0=ALU.add),
                reads=[st0[p]], writes=[st0[p]])
            trk.op(qS, lambda e, p=p: e.activation(
                out=st0[p][:, 4:5], in_=st0[p][:, 3:4], func=AF.Ln),
                reads=[st0[p]], writes=[st0[p]])
            trk.op(qS, lambda e, p=p: e.activation(
                out=st0[p][:, 5:6], in_=st0[p][:, 4:5], func=AF.Exp, scale=-0.5),
                reads=[st0[p]], writes=[st0[p]])
            trk.op(qV, lambda e, p=p: e.tensor_scalar(
                out=xnb[p][:], in0=xld[p][:],
                scalar1=st0[p][:, 2:3], scalar2=st0[p][:, 5:6],
                op0=ALU.subtract, op1=ALU.mult),
                reads=[xld[p], st0[p]], writes=[xnb[p]])
            trk.dma(qY, S0X[p], lambda e, i=i, p=p: e.dma_start(
                out=xn1[128 * i:128 * (i + 1), :], in_=xnb[p][:]),
                reads=[xnb[p]], writes=[("xn1", i)])
        flush_phase()
        p0.close()

    if "A" in phases:
        wy = inp("wy", [D, D])
        wx = inp("wx", [D, D])
        xn1 = scratch("xn1", [rows, D], BF16)
        convd = scratch("convd", [nwt, NRT, 128, TT], F32)
        yd = scratch("yd", [nt_real, NRT, 128, TT], F32)
        phase_begin()
        pa = ExitStack()
        wy_sb = pa.enter_context(nc.sbuf_tensor("wy_sb", [128, KT * D], BF16))
        wx_sb = pa.enter_context(nc.sbuf_tensor("wx_sb", [128, KT * D], BF16))
        xnt = [pa.enter_context(nc.sbuf_tensor(f"xnt{i}", [128, KT * TT], BF16))
               for i in range(3)]
        ybuf = [pa.enter_context(nc.sbuf_tensor(f"ybuf{i}", [128, TT], F32))
                for i in range(2)]
        xbx = [pa.enter_context(nc.sbuf_tensor(f"xbx{i}", [128, TT + 3], F32))
               for i in range(2)]
        cv1 = [pa.enter_context(nc.sbuf_tensor(f"cv1{i}", [128, TT], F32))
               for i in range(2)]
        cv2 = [pa.enter_context(nc.sbuf_tensor(f"cv2{i}", [128, TT], F32))
               for i in range(2)]
        carr = pa.enter_context(nc.sbuf_tensor("carr", [128, NRT * 3], F32))
        psY = [pa.enter_context(nc.psum_tensor(f"psY{i}", [128, TT], F32))
               for i in range(2)]
        psX = [pa.enter_context(nc.psum_tensor(f"psX{i}", [128, TT], F32))
               for i in range(2)]

        trk.dma(qG, LWA[0], lambda e: e.dma_start(
            out=wy_sb[:].rearrange("p (kt m) -> p kt m", kt=KT),
            in_=wy.rearrange("(kt p) m -> p kt m", p=128)),
            reads=[wy], writes=[wy_sb])
        trk.dma(qG, LWA[1], lambda e: e.dma_start(
            out=wx_sb[:].rearrange("p (kt m) -> p kt m", kt=KT),
            in_=wx.rearrange("(kt p) m -> p kt m", p=128)),
            reads=[wx], writes=[wx_sb])
        trk.op(qV, lambda e: e.memset(carr[:], 0.0), writes=[carr])

        def mmgroup(ps, wsb, rt, src, n=KT, w=TT):
            def fn(e):
                mm = None
                for kt in range(n):
                    mm = e.matmul(ps[:],
                                  wsb[:, kt * D + rt * 128:kt * D + rt * 128 + 128],
                                  src[:, kt * w:(kt + 1) * w],
                                  start=(kt == 0), stop=(kt == n - 1))
                return mm
            return fn

        for t in range(nwt):
            p3 = t % 3
            for rt in range(NRT):
                trk.dma(qY, LXT[p3], lambda e, t=t, rt=rt, p3=p3: e.dma_start_transpose(
                    out=xnt[p3][:, rt * TT:(rt + 1) * TT],
                    in_=xn1[TT * t:TT * (t + 1), 128 * rt:128 * (rt + 1)]),
                    reads=[("xn1", j) for j in range(4 * t, 4 * t + 4)],
                    writes=[xnt[p3]], first=(rt == 0))
            for rt in range(NRT):
                b = rt % 2
                if t >= 1:
                    trk.op(qT, mmgroup(psY[b], wy_sb, rt, xnt[p3]),
                           reads=[wy_sb, xnt[p3]], writes=[psY[b]])
                    trk.op(qS, lambda e, rt=rt, b=b: e.activation(
                        out=ybuf[b][:], in_=psY[b][:], func=AF.Gelu,
                        bias=pp[:, PC_BY + rt:PC_BY + rt + 1]),
                        reads=[psY[b], pp], writes=[ybuf[b]])
                    trk.dma(qY, SYY[b], lambda e, t=t, rt=rt, b=b: e.dma_start(
                        out=yd[t - 1, rt], in_=ybuf[b][:]), reads=[ybuf[b]],
                        writes=[("yd", t - 1, rt)])
                trk.op(qT, mmgroup(psX[b], wx_sb, rt, xnt[p3]),
                       reads=[wx_sb, xnt[p3]], writes=[psX[b]])
                trk.op(qV, lambda e, rt=rt, b=b: e.tensor_copy(
                    out=xbx[b][:, 0:3], in_=carr[:, 3 * rt:3 * rt + 3]),
                    reads=[carr], writes=[xbx[b]])
                trk.op(qS, lambda e, rt=rt, b=b: e.activation(
                    out=xbx[b][:, 3:TT + 3], in_=psX[b][:], func=AF.Identity,
                    bias=pp[:, PC_BX + rt:PC_BX + rt + 1]),
                    reads=[psX[b], pp], writes=[xbx[b]])
                cwc = PC_CW + 4 * rt
                trk.op(qV, lambda e, b=b, cwc=cwc: e.tensor_scalar(
                    out=cv1[b][:], in0=xbx[b][:, 0:TT],
                    scalar1=pp[:, cwc:cwc + 1], scalar2=None, op0=ALU.mult),
                    reads=[xbx[b], pp], writes=[cv1[b]])
                trk.op(qV, lambda e, b=b, cwc=cwc: e.scalar_tensor_tensor(
                    out=cv2[b][:], in0=xbx[b][:, 1:TT + 1],
                    scalar=pp[:, cwc + 1:cwc + 2], in1=cv1[b][:],
                    op0=ALU.mult, op1=ALU.add),
                    reads=[xbx[b], pp, cv1[b]], writes=[cv2[b]])
                trk.op(qV, lambda e, b=b, cwc=cwc: e.scalar_tensor_tensor(
                    out=cv1[b][:], in0=xbx[b][:, 2:TT + 2],
                    scalar=pp[:, cwc + 2:cwc + 3], in1=cv2[b][:],
                    op0=ALU.mult, op1=ALU.add),
                    reads=[xbx[b], pp, cv2[b]], writes=[cv1[b]])
                trk.op(qV, lambda e, b=b, cwc=cwc: e.scalar_tensor_tensor(
                    out=cv2[b][:], in0=xbx[b][:, 3:TT + 3],
                    scalar=pp[:, cwc + 3:cwc + 4], in1=cv1[b][:],
                    op0=ALU.mult, op1=ALU.add),
                    reads=[xbx[b], pp, cv1[b]], writes=[cv2[b]])
                trk.op(qV, lambda e, b=b, rt=rt: e.tensor_scalar(
                    out=cv2[b][:], in0=cv2[b][:],
                    scalar1=pp[:, PC_CB + rt:PC_CB + rt + 1],
                    scalar2=None, op0=ALU.add),
                    reads=[cv2[b], pp], writes=[cv2[b]])
                trk.op(qV, lambda e, b=b, rt=rt: e.tensor_copy(
                    out=carr[:, 3 * rt:3 * rt + 3], in_=xbx[b][:, TT:TT + 3]),
                    reads=[xbx[b]], writes=[carr])
                trk.dma(qG, SCV[b], lambda e, t=t, rt=rt, b=b: e.dma_start(
                    out=convd[t, rt], in_=cv2[b][:]), reads=[cv2[b]],
                    writes=[("convd", t, rt)])
        qY.wait(SYY[0], SYY[0].v)
        qY.wait(SYY[1], SYY[1].v)
        qG.wait(SCV[0], SCV[0].v)
        qG.wait(SCV[1], SCV[1].v)
        flush_phase()
        pa.close()

    LCB = [Ctr(es, nc, f"LCB{i}") for i in range(2)]
    LCF = [Ctr(es, nc, f"LCF{i}") for i in range(2)]
    SHH = [Ctr(es, nc, f"SHH{i}") for i in range(2)]
    LWB = [Ctr(es, nc, f"LWB{i}") for i in range(2)]

    if "B" in phases:
        wig = inp("wig", [D, D])
        wag = inp("wag", [D, D])
        convd = scratch("convd", [nwt, NRT, 128, TT], F32)
        hd = scratch("hd", [nt_real, NRT, 128, TT], F32)
        phase_begin()
        pb_ = ExitStack()
        wig_sb = pb_.enter_context(nc.sbuf_tensor("wig_sb", [128, KT * D], BF16))
        wag_sb = pb_.enter_context(nc.sbuf_tensor("wag_sb", [128, KT * D], BF16))
        cbf = [pb_.enter_context(nc.sbuf_tensor(f"cbf{i}", [128, KT * TT], BF16))
               for i in range(2)]
        cf32 = [pb_.enter_context(nc.sbuf_tensor(f"cf32{i}", [128, TT], F32))
                for i in range(2)]
        g1 = [pb_.enter_context(nc.sbuf_tensor(f"g1_{i}", [128, TT], F32))
              for i in range(2)]
        g2 = [pb_.enter_context(nc.sbuf_tensor(f"g2_{i}", [128, TT], F32))
              for i in range(2)]
        abuf = [pb_.enter_context(nc.sbuf_tensor(f"abuf{i}", [128, TT], F32))
                for i in range(2)]
        m2b = [pb_.enter_context(nc.sbuf_tensor(f"m2b{i}", [128, TT], F32))
               for i in range(2)]
        nx1 = [pb_.enter_context(nc.sbuf_tensor(f"nx1_{i}", [128, TT], F32))
               for i in range(2)]
        nxb = [pb_.enter_context(nc.sbuf_tensor(f"nxb{i}", [128, TT], F32))
               for i in range(2)]
        hbf = [pb_.enter_context(nc.sbuf_tensor(f"hbf{i}", [128, TT], F32))
               for i in range(2)]
        hcar = pb_.enter_context(nc.sbuf_tensor("hcar", [128, NRT], F32))
        bsc = pb_.enter_context(nc.sbuf_tensor("bsc", [128, 4], F32))
        psA = [pb_.enter_context(nc.psum_tensor(f"psA{i}", [128, TT], F32))
               for i in range(2)]
        psB = [pb_.enter_context(nc.psum_tensor(f"psB{i}", [128, TT], F32))
               for i in range(2)]

        trk.dma(qG, LWB[0], lambda e: e.dma_start(
            out=wig_sb[:].rearrange("p (kt m) -> p kt m", kt=KT),
            in_=wig.rearrange("(kt p) m -> p kt m", p=128)),
            reads=[wig], writes=[wig_sb])
        trk.dma(qG, LWB[1], lambda e: e.dma_start(
            out=wag_sb[:].rearrange("p (kt m) -> p kt m", kt=KT),
            in_=wag.rearrange("(kt p) m -> p kt m", p=128)),
            reads=[wag], writes=[wag_sb])

        def mmg(ps, wsb, rt, src):
            def fn(e):
                mm = None
                for kt in range(KT):
                    mm = e.matmul(ps[:],
                                  wsb[:, kt * D + rt * 128:kt * D + rt * 128 + 128],
                                  src[:, kt * TT:(kt + 1) * TT],
                                  start=(kt == 0), stop=(kt == KT - 1))
                return mm
            return fn

        for t in range(nwt):
            p2 = t % 2
            trk.dma(qG, LCB[p2], lambda e, t=t, p2=p2: e.dma_start(
                out=cbf[p2][:].rearrange("p (rt n) -> p rt n", rt=NRT),
                in_=convd[t].rearrange("rt p n -> p rt n")),
                reads=[("convd", t, r_) for r_ in range(NRT)], writes=[cbf[p2]])
            for rt in range(NRT):
                b = rt % 2
                trk.dma(qY, LCF[b], lambda e, t=t, rt=rt, b=b: e.dma_start(
                    out=cf32[b][:], in_=convd[t, rt]),
                    reads=[("convd", t, rt)], writes=[cf32[b]])
                trk.op(qT, mmg(psA[b], wag_sb, rt, cbf[p2]),
                       reads=[wag_sb, cbf[p2]], writes=[psA[b]])
                trk.op(qT, mmg(psB[b], wig_sb, rt, cbf[p2]),
                       reads=[wig_sb, cbf[p2]], writes=[psB[b]])
                trk.op(qS, lambda e, rt=rt, b=b: e.activation(
                    out=g1[b][:], in_=psA[b][:], func=AF.Sigmoid,
                    bias=pp[:, PC_BAG + rt:PC_BAG + rt + 1]),
                    reads=[psA[b], pp], writes=[g1[b]])
                trk.op(qS, lambda e, rt=rt, b=b: e.activation(
                    out=g2[b][:], in_=psB[b][:], func=AF.Sigmoid,
                    bias=pp[:, PC_BIG + rt:PC_BIG + rt + 1]),
                    reads=[psB[b], pp], writes=[g2[b]])
                trk.op(qS, lambda e, rt=rt, b=b: e.activation(
                    out=abuf[b][:], in_=g1[b][:], func=AF.Exp,
                    scale=sl[:, rt:rt + 1]),
                    reads=[g1[b], sl], writes=[abuf[b]])
                trk.op(qS, lambda e, rt=rt, b=b: e.activation(
                    out=m2b[b][:], in_=g1[b][:], func=AF.Exp,
                    scale=sl[:, 16 + rt:16 + rt + 1]),
                    reads=[g1[b], sl], writes=[m2b[b]])
                trk.op(qV, lambda e, b=b: e.tensor_scalar(
                    out=m2b[b][:], in0=m2b[b][:], scalar1=-1.0, scalar2=1.0,
                    op0=ALU.mult, op1=ALU.add), reads=[m2b[b]], writes=[m2b[b]])
                trk.op(qS, lambda e, b=b: e.activation(
                    out=m2b[b][:], in_=m2b[b][:], func=AF.Sqrt),
                    reads=[m2b[b]], writes=[m2b[b]])
                trk.op(qV, lambda e, b=b: e.tensor_mul(
                    out=nx1[b][:], in0=g2[b][:], in1=cf32[b][:]),
                    reads=[g2[b], cf32[b]], writes=[nx1[b]])
                trk.op(qV, lambda e, b=b: e.tensor_mul(
                    out=nxb[b][:], in0=nx1[b][:], in1=m2b[b][:]),
                    reads=[nx1[b], m2b[b]], writes=[nxb[b]])
                if t == 1:
                    trk.op(qV, lambda e, b=b: e.tensor_scalar(
                        out=abuf[b][:, 0:1], in0=abuf[b][:, 0:1],
                        scalar1=fl[:, 0:1], scalar2=None, op0=ALU.mult),
                        reads=[abuf[b], fl], writes=[abuf[b]])
                    trk.op(qV, lambda e, b=b: e.tensor_scalar(
                        out=bsc[:, 0:1], in0=nx1[b][:, 0:1],
                        scalar1=fl[:, 1:2], scalar2=None, op0=ALU.mult),
                        reads=[nx1[b], fl], writes=[bsc])
                    trk.op(qV, lambda e, b=b: e.scalar_tensor_tensor(
                        out=nxb[b][:, 0:1], in0=nxb[b][:, 0:1],
                        scalar=fl[:, 0:1], in1=bsc[:, 0:1],
                        op0=ALU.mult, op1=ALU.add),
                        reads=[nxb[b], fl, bsc], writes=[nxb[b]])
                if t == 0:
                    trk.op(qV, lambda e, b=b: e.tensor_tensor_scan(
                        out=hbf[b][:], data0=abuf[b][:], data1=nxb[b][:],
                        initial=0.0, op0=ALU.mult, op1=ALU.add),
                        reads=[abuf[b], nxb[b]], writes=[hbf[b]])
                else:
                    trk.op(qV, lambda e, b=b, rt=rt: e.tensor_tensor_scan(
                        out=hbf[b][:], data0=abuf[b][:], data1=nxb[b][:],
                        initial=hcar[:, rt:rt + 1], op0=ALU.mult, op1=ALU.add),
                        reads=[abuf[b], nxb[b], hcar], writes=[hbf[b]])
                trk.op(qG, lambda e, b=b, rt=rt: e.tensor_copy(
                    out=hcar[:, rt:rt + 1], in_=hbf[b][:, TT - 1:TT]),
                    reads=[hbf[b]], writes=[hcar])
                if t >= 1:
                    trk.dma(qY, SHH[b], lambda e, t=t, rt=rt, b=b: e.dma_start(
                        out=hd[t - 1, rt], in_=hbf[b][:]), reads=[hbf[b]],
                        writes=[("hd", t - 1, rt)])
        qY.wait(SHH[0], SHH[0].v)
        qY.wait(SHH[1], SHH[1].v)
        flush_phase()
        pb_.close()

    LXC = [Ctr(es, nc, f"LXC{i}") for i in range(2)]
    LHH = [Ctr(es, nc, f"LHH{i}") for i in range(2)]
    LYY = [Ctr(es, nc, f"LYY{i}") for i in range(2)]
    SAC = [Ctr(es, nc, f"SAC{i}") for i in range(2)]
    SX2 = [Ctr(es, nc, f"SX2{i}") for i in range(2)]
    LWC = Ctr(es, nc, "LWC")
    LW2C = [Ctr(es, nc, f"LW2C{i}") for i in range(2)]

    if "C" in phases:
        wo = inp("wo", [D, D])
        yd = scratch("yd", [nt_real, NRT, 128, TT], F32)
        hd = scratch("hd", [nt_real, NRT, 128, TT], F32)
        accd = scratch("accd", [nt_real, NRT, 128, TT], F32)
        xn2d = scratch("xn2d", [nt_real, NRT, 128, TT], BF16)
        phase_begin()
        pc_ = ExitStack()
        wo_sb = pc_.enter_context(nc.sbuf_tensor("wo_sb", [128, KT * D], BF16))
        xc = [pc_.enter_context(nc.sbuf_tensor(f"xc{i}", [128, D], F32))
              for i in range(4)]
        hld = [pc_.enter_context(nc.sbuf_tensor(f"hld{i}", [128, 4 * TT], F32))
               for i in range(2)]
        yld = [pc_.enter_context(nc.sbuf_tensor(f"yld{i}", [128, 4 * TT], F32))
               for i in range(2)]
        hybf = [pc_.enter_context(nc.sbuf_tensor(f"hybf{i}", [128, NRT * TT], BF16))
                for i in range(2)]
        x1b4 = [pc_.enter_context(nc.sbuf_tensor(f"x1b4{i}", [128, 4 * TT], F32))
                for i in range(2)]
        x1bf = [pc_.enter_context(nc.sbuf_tensor(f"x1bf{i}", [128, NRT * TT], BF16))
                for i in range(1)]
        x1sq = [pc_.enter_context(nc.sbuf_tensor(f"x1sq{i}", [128, TT], BF16))
                for i in range(1)]
        t2b = [pc_.enter_context(nc.sbuf_tensor(f"t2b{i}", [128, TT], F32))
               for i in range(2)]
        xn2b = [pc_.enter_context(nc.sbuf_tensor(f"xn2b{i}", [128, TT], BF16))
                for i in range(2)]
        stt = pc_.enter_context(nc.sbuf_tensor("stt", [1, 3 * TT], F32))
        psO = [pc_.enter_context(nc.psum_tensor(f"psO{i}", [128, TT], F32))
               for i in range(2)]
        psS1 = pc_.enter_context(nc.psum_tensor("psS1", [1, TT], F32))
        psS2 = pc_.enter_context(nc.psum_tensor("psS2", [1, TT], F32))
        psM = pc_.enter_context(nc.psum_tensor("psM", [128, TT], F32))
        psR = pc_.enter_context(nc.psum_tensor("psR", [128, TT], F32))

        trk.dma(qG, LWC, lambda e: e.dma_start(
            out=wo_sb[:].rearrange("p (kt m) -> p kt m", kt=KT),
            in_=wo.rearrange("(kt p) m -> p kt m", p=128)),
            reads=[wo], writes=[wo_sb])
        for t in range(nt_real):
            for c4 in range(4):
                xi = c4
                r0 = TT * (t + 1) + 128 * c4
                trk.dma(qY, LXC[xi % 2], lambda e, r0=r0, xi=xi: e.dma_start(
                    out=xc[xi][:], in_=xs[r0:r0 + 128, :]), writes=[xc[xi]])
            for g4 in range(4):
                b = g4 % 2
                trk.dma(qY, LHH[b], lambda e, t=t, g4=g4, b=b: e.dma_start(
                    out=hld[b][:].rearrange("p (r n) -> p r n", r=4),
                    in_=hd[t, 4 * g4:4 * g4 + 4].rearrange("r p n -> p r n")),
                    reads=[("hd", t, 4 * g4 + j) for j in range(4)],
                    writes=[hld[b]])
                trk.dma(qY, LYY[b], lambda e, t=t, g4=g4, b=b: e.dma_start(
                    out=yld[b][:].rearrange("p (r n) -> p r n", r=4),
                    in_=yd[t, 4 * g4:4 * g4 + 4].rearrange("r p n -> p r n")),
                    reads=[("yd", t, 4 * g4 + j) for j in range(4)],
                    writes=[yld[b]])
                trk.op(qV, lambda e, g4=g4, b=b, hp=t % 2: e.tensor_mul(
                    out=hybf[hp][:, 4 * g4 * TT:4 * (g4 + 1) * TT],
                    in0=hld[b][:], in1=yld[b][:]),
                    reads=[hld[b], yld[b]], writes=[hybf[t % 2]])
            def emit_og(rt):
                b = rt % 2
                xis = tuple(range(4))
                def og(e, rt=rt, b=b, hp=t % 2, xis=xis):
                    mm = None
                    for c4 in range(4):
                        mm = e.matmul(psO[b][:, c4 * 128:(c4 + 1) * 128],
                                      xc[xis[c4]][:, rt * 128:(rt + 1) * 128],
                                      ident[:], start=(c4 == 0), stop=False,
                                      is_transpose=True)
                    for kt in range(KT):
                        mm = e.matmul(psO[b][:],
                                      wo_sb[:, kt * D + rt * 128:kt * D + rt * 128 + 128],
                                      hybf[hp][:, kt * TT:(kt + 1) * TT],
                                      start=False, stop=(kt == KT - 1))
                    return mm
                trk.op(qT, og,
                       reads=[wo_sb, hybf[t % 2], ident] + [xc[x_] for x_ in xis],
                       writes=[psO[b]])

            def emit_post(rt):
                b = rt % 2
                g4, j4 = rt // 4, rt % 4
                bq = g4 % 2
                trk.op(qS, lambda e, rt=rt, b=b, bq=bq, j4=j4: e.activation(
                    out=x1b4[bq][:, j4 * TT:(j4 + 1) * TT], in_=psO[b][:],
                    func=AF.Identity,
                    bias=pp[:, PC_BOUT + rt:PC_BOUT + rt + 1]),
                    reads=[psO[b], pp], writes=[("x1b4", bq, j4)])
                trk.op(qS, lambda e, rt=rt, bq=bq, j4=j4: e.activation(
                    out=x1bf[0][:, rt * TT:(rt + 1) * TT],
                    in_=x1b4[bq][:, j4 * TT:(j4 + 1) * TT],
                    func=AF.Identity), reads=[("x1b4", bq, j4)], writes=[x1bf[0]])
                trk.op(qS, lambda e, rt=rt, bq=bq, j4=j4: e.activation(
                    out=x1sq[0][:], in_=x1b4[bq][:, j4 * TT:(j4 + 1) * TT],
                    func=AF.Square),
                    reads=[("x1b4", bq, j4)], writes=[x1sq[0]])
                trk.op(qT, lambda e, rt=rt: e.matmul(
                    psS1[:], ones_col_bf[:], x1bf[0][:, rt * TT:(rt + 1) * TT],
                    start=(rt == 0), stop=(rt == NRT - 1)),
                    reads=[ones_col_bf, x1bf[0]], writes=[psS1])
                trk.op(qT, lambda e, rt=rt: e.matmul(
                    psS2[:], ones_col_bf[:], x1sq[0][:],
                    start=(rt == 0), stop=(rt == NRT - 1)),
                    reads=[ones_col_bf, x1sq[0]], writes=[psS2])
                if j4 == 3:
                    trk.dma(qY, SAC[bq], lambda e, t=t, g4=g4, bq=bq: e.dma_start(
                        out=accd[t, 4 * g4:4 * g4 + 4].rearrange("r p n -> p r n"),
                        in_=x1b4[bq][:].rearrange("p (r n) -> p r n", r=4)),
                        reads=[("x1b4", bq, j_) for j_ in range(4)],
                        writes=[("accd", t, 4 * g4 + j_) for j_ in range(4)])

            for rt in range(NRT + 1):
                if rt < NRT:
                    emit_og(rt)
                if rt >= 1:
                    emit_post(rt - 1)
            trk.op(qV, lambda e: e.tensor_scalar(
                out=stt[0:1, 0:TT], in0=psS1[:], scalar1=1.0 / D, scalar2=None,
                op0=ALU.mult), reads=[psS1], writes=[stt])
            trk.op(qV, lambda e: e.tensor_scalar(
                out=stt[0:1, TT:2 * TT], in0=psS2[:], scalar1=1.0 / D,
                scalar2=None, op0=ALU.mult), reads=[psS2], writes=[stt])
            trk.op(qV, lambda e: e.tensor_mul(
                out=stt[0:1, 2 * TT:3 * TT], in0=stt[0:1, 0:TT],
                in1=stt[0:1, 0:TT]), reads=[stt], writes=[stt])
            trk.op(qV, lambda e: e.tensor_sub(
                out=stt[0:1, TT:2 * TT], in0=stt[0:1, TT:2 * TT],
                in1=stt[0:1, 2 * TT:3 * TT]), reads=[stt], writes=[stt])
            trk.op(qV, lambda e: e.tensor_scalar(
                out=stt[0:1, TT:2 * TT], in0=stt[0:1, TT:2 * TT],
                scalar1=1e-5, scalar2=None, op0=ALU.add),
                reads=[stt], writes=[stt])
            trk.op(qS, lambda e: e.activation(
                out=stt[0:1, 2 * TT:3 * TT], in_=stt[0:1, TT:2 * TT], func=AF.Ln),
                reads=[stt], writes=[stt])
            trk.op(qS, lambda e: e.activation(
                out=stt[0:1, 2 * TT:3 * TT], in_=stt[0:1, 2 * TT:3 * TT],
                func=AF.Exp, scale=-0.5), reads=[stt], writes=[stt])
            trk.op(qT, lambda e: e.matmul(
                psM[:], ones_row[:], stt[0:1, 0:TT], start=True, stop=True),
                reads=[ones_row, stt], writes=[psM])
            trk.op(qT, lambda e: e.matmul(
                psR[:], ones_row[:], stt[0:1, 2 * TT:3 * TT], start=True,
                stop=True), reads=[ones_row, stt], writes=[psR])
            for rt in range(NRT):
                b = rt % 2
                g4, j4 = rt // 4, rt % 4
                bq = g4 % 2
                trk.op(qV, lambda e, rt=rt, b=b: e.tensor_sub(
                    out=t2b[b][:], in0=x1bf[0][:, rt * TT:(rt + 1) * TT],
                    in1=psM[:]), reads=[x1bf[0], psM], writes=[t2b[b]])
                trk.op(qV, lambda e, rt=rt, b=b: e.tensor_mul(
                    out=xn2b[b][:], in0=t2b[b][:], in1=psR[:]),
                    reads=[t2b[b], psR], writes=[xn2b[b]])
                trk.dma(qY, SX2[b], lambda e, t=t, rt=rt, b=b: e.dma_start(
                    out=xn2d[t, rt], in_=xn2b[b][:]), reads=[xn2b[b]],
                    writes=[("xn2d", t, rt)])
        qY.wait(SAC[0], SAC[0].v)
        qY.wait(SAC[1], SAC[1].v)
        qY.wait(SX2[0], SX2[0].v)
        qY.wait(SX2[1], SX2[1].v)
        flush_phase()
        pc_.close()

    LX20 = [Ctr(es, nc, f"LX20{i}") for i in range(2)]
    LGW = Ctr(es, nc, "LGW")
    SCB = Ctr(es, nc, "SCB")
    SCB1 = Ctr(es, nc, "SCB1")

    if "D0" in phases:
        gwT = inp("gwT", [D, 16])
        xn2d = scratch("xn2d", [nt_real, NRT, 128, TT], BF16)
        combd = scratch("combd", [NE, nt_real * TT], F32)
        phase_begin()
        pd0 = ExitStack()
        gw_sb = pd0.enter_context(nc.sbuf_tensor("gw_sb", [128, KT * 16], BF16))
        x2t = [pd0.enter_context(nc.sbuf_tensor(f"x2t{i}", [128, KT * TT], BF16))
               for i in range(2)]
        scf = pd0.enter_context(nc.sbuf_tensor("scf", [16, TT], F32))
        sct = pd0.enter_context(nc.sbuf_tensor("sct", [128, 64], F32))
        esc = pd0.enter_context(nc.sbuf_tensor("esc", [128, 16], F32))
        prb = pd0.enter_context(nc.sbuf_tensor("prb", [128, 16], F32))
        wrk = pd0.enter_context(nc.sbuf_tensor("wrk", [128, 16], F32))
        msc = pd0.enter_context(nc.sbuf_tensor("msc", [128, 16], F32))
        cmb = pd0.enter_context(nc.sbuf_tensor("cmb", [128, 16], F32))
        b0 = pd0.enter_context(nc.sbuf_tensor("b0", [128, 1], F32))
        b1 = pd0.enter_context(nc.sbuf_tensor("b1", [128, 1], F32))
        b2 = pd0.enter_context(nc.sbuf_tensor("b2", [128, 1], F32))
        ccrow = pd0.enter_context(nc.sbuf_tensor("ccrow", [16, 128], F32))
        onr = pd0.enter_context(nc.sbuf_tensor("onr", [1, nt_real * TT], F32))
        psGt = pd0.enter_context(nc.psum_tensor("psGt", [16, TT], F32))
        ptg = pd0.enter_context(nc.psum_tensor("ptg", [128, 64], F32))
        pcc = pd0.enter_context(nc.psum_tensor("pcc", [16, 128], F32))

        trk.dma(qG, LGW, lambda e: e.dma_start(
            out=gw_sb[:].rearrange("p (kt m) -> p kt m", kt=KT),
            in_=gwT.rearrange("(kt p) m -> p kt m", p=128)),
            reads=[gwT], writes=[gw_sb])
        trk.op(qV, lambda e: e.memset(onr[:], 1.0), writes=[onr])
        trk.dma(qY, SCB1, lambda e: e.dma_start(
            out=combd[16:17, :], in_=onr[:]), reads=[onr], writes=[("combd", 16)])
        trk.dma(qY, SCB1, lambda e: e.dma_start(
            out=combd[17:18, :], in_=onr[:]), reads=[onr], writes=[("combd", 17)],
            first=False)

        for t in range(nt_real):
            p2 = t % 2
            trk.dma(qY, LX20[p2], lambda e, t=t, p2=p2: e.dma_start(
                out=x2t[p2][:].rearrange("p (rt n) -> p rt n", rt=NRT),
                in_=xn2d[t].rearrange("rt p n -> p rt n")),
                reads=[("xn2d", t, r_) for r_ in range(NRT)], writes=[x2t[p2]])
            def gg(e, p2=p2):
                mm = None
                for kt in range(KT):
                    mm = e.matmul(psGt[:], gw_sb[:, kt * 16:(kt + 1) * 16],
                                  x2t[p2][:, kt * TT:(kt + 1) * TT],
                                  start=(kt == 0), stop=(kt == KT - 1))
                return mm
            trk.op(qT, gg, reads=[gw_sb, x2t[p2]], writes=[psGt])
            trk.op(qS, lambda e: e.activation(
                out=scf[:], in_=psGt[:], func=AF.Identity),
                reads=[psGt], writes=[scf])
            def tg(e):
                mm = None
                for c4 in range(4):
                    mm = e.transpose(ptg[:, c4 * 16:(c4 + 1) * 16],
                                     scf[0:16, c4 * 128:(c4 + 1) * 128],
                                     ident[0:16, 0:16])
                return mm
            trk.op(qT, tg, reads=[scf, ident], writes=[ptg])
            trk.op(qS, lambda e: e.activation(
                out=sct[:], in_=ptg[:], func=AF.Identity),
                reads=[ptg], writes=[sct])
            for c4 in range(4):
                ch = sct[:, c4 * 16:(c4 + 1) * 16]
                trk.op(qV, lambda e, ch=ch: e.reduce_max(
                    out=b0[:], in_=ch, axis=AXX, negate=True),
                    reads=[sct], writes=[b0])
                trk.op(qS, lambda e, ch=ch: e.activation(
                    out=esc[:], in_=ch, func=AF.Exp, bias=b0[:]),
                    reads=[sct, b0], writes=[esc])
                trk.op(qV, lambda e: e.reduce_sum(
                    out=b1[:], in_=esc[:], axis=AXX), reads=[esc], writes=[b1])
                trk.op(qV, lambda e: e.reciprocal(out=b2[:], in_=b1[:]),
                       reads=[b1], writes=[b2])
                trk.op(qV, lambda e: e.tensor_scalar(
                    out=prb[:], in0=esc[:], scalar1=b2[:], scalar2=None,
                    op0=ALU.mult), reads=[esc, b2], writes=[prb])
                trk.op(qV, lambda e: e.tensor_copy(out=wrk[:], in_=prb[:]),
                       reads=[prb], writes=[wrk])
                for it in range(5):
                    trk.op(qV, lambda e: e.reduce_max(
                        out=b0[:], in_=wrk[:], axis=AXX),
                        reads=[wrk], writes=[b0])
                    trk.op(qV, lambda e: e.tensor_scalar(
                        out=msc[:], in0=wrk[:], scalar1=b0[:], scalar2=2.0,
                        op0=ALU.is_ge, op1=ALU.mult),
                        reads=[wrk, b0], writes=[msc])
                    trk.op(qV, lambda e: e.tensor_sub(
                        out=wrk[:], in0=wrk[:], in1=msc[:]),
                        reads=[wrk, msc], writes=[wrk])
                trk.op(qV, lambda e: e.reduce_max(
                    out=b0[:], in_=wrk[:], axis=AXX), reads=[wrk], writes=[b0])
                trk.op(qV, lambda e: e.tensor_scalar(
                    out=msc[:], in0=prb[:], scalar1=b0[:], scalar2=None,
                    op0=ALU.is_ge), reads=[prb, b0], writes=[msc])
                trk.op(qV, lambda e: e.tensor_mul(
                    out=wrk[:], in0=prb[:], in1=msc[:]),
                    reads=[prb, msc], writes=[wrk])
                trk.op(qV, lambda e: e.reduce_sum(
                    out=b1[:], in_=wrk[:], axis=AXX), reads=[wrk], writes=[b1])
                trk.op(qV, lambda e: e.tensor_scalar(
                    out=b1[:], in0=b1[:], scalar1=1e-20, scalar2=None,
                    op0=ALU.add), reads=[b1], writes=[b1])
                trk.op(qV, lambda e: e.reciprocal(out=b2[:], in_=b1[:]),
                       reads=[b1], writes=[b2])
                trk.op(qV, lambda e: e.tensor_scalar(
                    out=cmb[:], in0=wrk[:], scalar1=b2[:], scalar2=None,
                    op0=ALU.mult), reads=[wrk, b2], writes=[cmb])
                trk.op(qT, lambda e: e.transpose(pcc[:], cmb[:], ident[:]),
                       reads=[cmb, ident], writes=[pcc])
                trk.op(qS, lambda e: e.activation(
                    out=ccrow[:], in_=pcc[:], func=AF.Identity),
                    reads=[pcc], writes=[ccrow])
                c0 = t * TT + c4 * 128
                trk.dma(qY, SCB, lambda e, c0=c0: e.dma_start(
                    out=combd[0:16, c0:c0 + 128], in_=ccrow[:]),
                    reads=[ccrow], writes=[("combd", t, c4)])
        qY.wait(SCB, SCB.v)
        flush_phase()
        pd0.close()

    LW13 = Ctr(es, nc, "LW13")
    LW13H = Ctr(es, nc, "LW13H")
    LW2D = Ctr(es, nc, "LW2D")
    LX2D = [Ctr(es, nc, f"LX2D{i}") for i in range(2)]
    LCR = [Ctr(es, nc, f"LCR{i}") for i in range(2)]
    SACCD = [Ctr(es, nc, f"SACCD{i}") for i in range(NRT)]

    if "D" in phases:
        w13p = inp("w13p", [NE, D, 2816])
        w2p = inp("w2p", [NE, IS, D])
        accd = scratch("accd", [nt_real, NRT, 128, TT], F32)
        xn2d = scratch("xn2d", [nt_real, NRT, 128, TT], BF16)
        combd = scratch("combd", [NE, nt_real * TT], F32)
        phase_begin()
        pd_ = ExitStack()
        w13_sb = pd_.enter_context(nc.sbuf_tensor("w13_sb", [128, KT * 2816], BF16))
        w2_sb = pd_.enter_context(nc.sbuf_tensor("w2_sb", [128, IC * D], BF16))
        x2d = [pd_.enter_context(nc.sbuf_tensor(f"x2d{i}", [128, KT * TT], BF16))
               for i in range(2)]
        s_sb = pd_.enter_context(nc.sbuf_tensor("s_sb", [128, IC * TT], BF16))
        sg = [pd_.enter_context(nc.sbuf_tensor(f"sg{i}", [128, TT], F32))
              for i in range(2)]
        ob = [pd_.enter_context(nc.sbuf_tensor(f"ob{i}", [128, TT], F32))
              for i in range(2)]
        cps = pd_.enter_context(nc.sbuf_tensor("cps", [128, TT], F32))
        crow = [pd_.enter_context(nc.sbuf_tensor(f"crow{i}", [1, TT], F32))
                for i in range(2)]
        psG = [pd_.enter_context(nc.psum_tensor(f"psG{i}", [128, TT], F32))
               for i in range(2)]
        psU = [pd_.enter_context(nc.psum_tensor(f"psU{i}", [128, TT], F32))
               for i in range(2)]
        psC = pd_.enter_context(nc.psum_tensor("psC", [128, TT], F32))
        psO2 = [pd_.enter_context(nc.psum_tensor(f"psO2{i}", [128, TT], F32))
                for i in range(2)]

        LOCH = 6

        def load_w13(s, first=False, half=None):
            wv = w13_sb[:].rearrange("p (kt m) -> p kt m", kt=KT)
            sv = w13p[s].rearrange("(kt p) m -> p kt m", p=128)
            if half in (None, "lo"):
                trk.dma(qG, LW13, lambda e: e.dma_start(
                    out=wv[:, :, 0:LOCH * 128], in_=sv[:, :, 0:LOCH * 128]),
                    reads=[w13p], writes=[("w13L",)])
                trk.dma(qG, LW13, lambda e: e.dma_start(
                    out=wv[:, :, IS:IS + LOCH * 128],
                    in_=sv[:, :, IS:IS + LOCH * 128]),
                    reads=[w13p], writes=[("w13L",)], first=False)
            if half in (None, "hi"):
                trk.dma(qG, LW13H, lambda e: e.dma_start(
                    out=wv[:, :, LOCH * 128:IS], in_=sv[:, :, LOCH * 128:IS]),
                    reads=[w13p], writes=[("w13H",)])
                trk.dma(qG, LW13H, lambda e: e.dma_start(
                    out=wv[:, :, IS + LOCH * 128:2 * IS],
                    in_=sv[:, :, IS + LOCH * 128:2 * IS]),
                    reads=[w13p], writes=[("w13H",)], first=False)

        for s in range(NE):
            for t in range(nt_real):
                p2 = t % 2
                trk.dma(qY, LX2D[p2], lambda e, t=t, p2=p2: e.dma_start(
                    out=x2d[p2][:].rearrange("p (rt n) -> p rt n", rt=NRT),
                    in_=xn2d[t].rearrange("rt p n -> p rt n")),
                    reads=[("xn2d", t, r_) for r_ in range(NRT)],
                    writes=[x2d[p2]])
                if s < 16:
                    crdeps = [("combd", t, c4_) for c4_ in range(4)]
                else:
                    crdeps = [("combd", s)]
                trk.dma(qY, LCR[p2], lambda e, s=s, t=t, p2=p2: e.dma_start(
                    out=crow[p2][:], in_=combd[s:s + 1, t * TT:(t + 1) * TT]),
                    reads=crdeps, writes=[crow[p2]])
                if s == 0 and t == 0:
                    load_w13(s, first=True)
                    trk.dma(qG, LW2D, lambda e, s=s: e.dma_start(
                        out=w2_sb[:].rearrange("p (kt m) -> p kt m", kt=IC),
                        in_=w2p[s].rearrange("(kt p) m -> p kt m", p=128)),
                        reads=[w2p], writes=[w2_sb])
                trk.op(qT, lambda e, p2=p2: e.matmul(
                    psC[:], ones_row[:], crow[p2][:], start=True, stop=True),
                    reads=[ones_row, crow[p2]], writes=[psC])
                trk.op(qV, lambda e: e.tensor_copy(out=cps[:], in_=psC[:]),
                       reads=[psC], writes=[cps])
                for c in range(IC):
                    b = c % 2

                    def g1g(e, p2=p2, c=c, b=b):
                        mm = None
                        for kt in range(KT):
                            mm = e.matmul(
                                psG[b][:],
                                w13_sb[:, kt * 2816 + c * 128:kt * 2816 + c * 128 + 128],
                                x2d[p2][:, kt * TT:(kt + 1) * TT],
                                start=(kt == 0), stop=(kt == KT - 1))
                        return mm

                    def u1g(e, p2=p2, c=c, b=b):
                        mm = None
                        for kt in range(KT):
                            mm = e.matmul(
                                psU[b][:],
                                w13_sb[:, kt * 2816 + IS + c * 128:kt * 2816 + IS + c * 128 + 128],
                                x2d[p2][:, kt * TT:(kt + 1) * TT],
                                start=(kt == 0), stop=(kt == KT - 1))
                        return mm
                    wkey = ("w13L",) if c < LOCH else ("w13H",)
                    trk.op(qT, g1g, reads=[wkey, x2d[p2]], writes=[psG[b]])
                    trk.op(qT, u1g, reads=[wkey, x2d[p2]], writes=[psU[b]])
                    if (t == nt_real - 1 and c == LOCH - 1 and s + 1 < NE):
                        load_w13(s + 1, half="lo")
                    trk.op(qS, lambda e, b=b: e.activation(
                        out=sg[b][:], in_=psG[b][:], func=AF.Silu),
                        reads=[psG[b]], writes=[sg[b]])
                    trk.op(qV, lambda e, b=b, c=c: e.tensor_mul(
                        out=s_sb[:, c * TT:(c + 1) * TT], in0=sg[b][:],
                        in1=psU[b][:]), reads=[sg[b], psU[b]], writes=[s_sb])
                if t == nt_real - 1 and s + 1 < NE:
                    load_w13(s + 1, half="hi")
                for rt in range(NRT):
                    b = rt % 2

                    def o2g(e, rt=rt, b=b):
                        mm = None
                        for kt in range(IC):
                            mm = e.matmul(
                                psO2[b][:],
                                w2_sb[:, kt * D + rt * 128:kt * D + rt * 128 + 128],
                                s_sb[:, kt * TT:(kt + 1) * TT],
                                start=(kt == 0), stop=(kt == IC - 1))
                        return mm
                    trk.op(qT, o2g, reads=[w2_sb, s_sb], writes=[psO2[b]])
                    trk.op(qV, lambda e, b=b: e.tensor_mul(
                        out=ob[b][:], in0=cps[:], in1=psO2[b][:]),
                        reads=[cps, psO2[b]], writes=[ob[b]])
                    trk.dma(qG, SACCD[rt], lambda e, t=t, rt=rt, b=b: e.dma_start(
                        out=accd[t, rt], in_=ob[b][:], accum_op=ALU.add),
                        reads=[ob[b]], writes=[("accd", t, rt)])
                if t == nt_real - 1 and s + 1 < NE:
                    trk.dma(qG, LW2D, lambda e, s=s: e.dma_start(
                        out=w2_sb[:].rearrange("p (kt m) -> p kt m", kt=IC),
                        in_=w2p[s + 1].rearrange("(kt p) m -> p kt m", p=128)),
                        reads=[w2p], writes=[w2_sb])
        flush_phase()
        pd_.close()

    LAE = [Ctr(es, nc, f"LAE{i}") for i in range(2)]
    SOUT = [Ctr(es, nc, f"SOUT{i}") for i in range(2)]

    if "E" in phases:
        accd = scratch("accd", [nt_real, NRT, 128, TT], F32)
        out = nc.dram_tensor("out", [nt_real * TT, D], F32, kind="ExternalOutput").ap()
        phase_begin()
        pe_ = ExitStack()
        aee = [pe_.enter_context(nc.sbuf_tensor(f"aee{i}", [128, NRT * TT], F32))
               for i in range(2)]
        oy = [pe_.enter_context(nc.sbuf_tensor(f"oy{i}", [128, D], F32))
              for i in range(2)]
        psE = [pe_.enter_context(nc.psum_tensor(f"psE{i}", [128, TT], F32))
               for i in range(2)]

        for t in range(nt_real):
            p2 = t % 2
            trk.dma(qY, LAE[p2], lambda e, t=t, p2=p2: e.dma_start(
                out=aee[p2][:].rearrange("p (rt n) -> p rt n", rt=NRT),
                in_=accd[t].rearrange("rt p n -> p rt n")),
                reads=[("accd", t, r_) for r_ in range(NRT)], writes=[aee[p2]])
            for c4 in range(4):
                pc2 = c4 % 2
                for g4 in range(4):

                    def tpe(e, p2=p2, c4=c4, g4=g4, pc2=pc2):
                        mm = None
                        for j in range(4):
                            rt = g4 * 4 + j
                            mm = e.transpose(
                                psE[pc2][:, j * 128:(j + 1) * 128],
                                aee[p2][:, rt * TT + c4 * 128:rt * TT + c4 * 128 + 128],
                                ident[:])
                        return mm
                    trk.op(qT, tpe, reads=[aee[p2], ident], writes=[psE[pc2]])
                    trk.op(qS, lambda e, g4=g4, pc2=pc2: e.activation(
                        out=oy[pc2][:, g4 * TT:(g4 + 1) * TT], in_=psE[pc2][:],
                        func=AF.Identity), reads=[psE[pc2]], writes=[oy[pc2]])
                r0 = t * TT + c4 * 128
                trk.dma(qY, SOUT[pc2], lambda e, r0=r0, pc2=pc2: e.dma_start(
                    out=out[r0:r0 + 128, :], in_=oy[pc2][:]),
                    reads=[oy[pc2]], writes=[("out", r0)])
        flush_phase()
        pe_.close()

    for s in sorted(trk.dma_sems, key=lambda c: c.name):
        if s.v > 0:
            qG.wait(s, s.v)
    with nc.Block() as blk:
        @blk.gpsimd
        def _(e):
            qG.replay(e)

    es.close()
    return nc


def _pack_inputs(inputs, nt_real=8, n_cores=8, names=None):
    f32 = np.float32
    x = np.ascontiguousarray(np.asarray(inputs["L0_attn_x"]).reshape(32768, D))
    chunk = 4096
    warm = TT
    in_maps = []

    ppk = np.zeros((128, PC_N), f32)

    def col16(v):
        return np.ascontiguousarray(np.asarray(v).reshape(16, 128).T)

    ppk[:, PC_BY:PC_BY + 16] = col16(inputs["L0_attn_b_y"])
    ppk[:, PC_BX:PC_BX + 16] = col16(inputs["L0_attn_b_x"])
    ppk[:, PC_BIG:PC_BIG + 16] = col16(inputs["L0_attn_b_ig"])
    ppk[:, PC_BAG:PC_BAG + 16] = col16(inputs["L0_attn_b_ag"])
    ppk[:, PC_BOUT:PC_BOUT + 16] = col16(inputs["L0_attn_b_out"])
    ppk[:, PC_CB:PC_CB + 16] = col16(inputs["L0_attn_conv_b"])
    cw = np.asarray(inputs["L0_attn_conv_w"]).reshape(4, 16, 128).transpose(2, 1, 0)
    ppk[:, PC_CW:PC_CW + 64] = np.ascontiguousarray(cw.reshape(128, 64))
    ppk[:, PC_AP:PC_AP + 16] = col16(inputs["L0_attn_a_param"])

    gwT_ = np.ascontiguousarray(np.asarray(inputs["L0_ffn_gate_w"]).T)

    w13 = np.asarray(inputs["L0_ffn_w13"])
    sw13T = np.ascontiguousarray(np.asarray(inputs["L0_ffn_sw13"]).T)
    w2 = np.asarray(inputs["L0_ffn_w2"])
    sw2T = np.ascontiguousarray(np.asarray(inputs["L0_ffn_sw2"]).T)
    w13p_ = np.empty((NE, D, 2816), f32)
    w13p_[0:16] = w13.transpose(0, 2, 1)
    w13p_[16] = np.concatenate([sw13T[:, 0:IS], sw13T[:, 2 * IS:3 * IS]], axis=1)
    w13p_[17] = np.concatenate([sw13T[:, IS:2 * IS], sw13T[:, 3 * IS:4 * IS]], axis=1)
    w2p_ = np.empty((NE, IS, D), f32)
    w2p_[0:16] = w2.transpose(0, 2, 1)
    w2p_[16] = sw2T[0:IS]
    w2p_[17] = sw2T[IS:2 * IS]

    common = {
        "ppack": ppk, "gwT": gwT_, "w13p": w13p_, "w2p": w2p_,
        "wy": np.asarray(inputs["L0_attn_w_y"]),
        "wx": np.asarray(inputs["L0_attn_w_x"]),
        "wig": np.asarray(inputs["L0_attn_w_ig"]),
        "wag": np.asarray(inputs["L0_attn_w_ag"]),
        "wo": np.asarray(inputs["L0_attn_w_out"]),
    }
    if names is not None:
        common = {k: v for k, v in common.items() if k in names}
    for c in range(n_cores):
        lo = chunk * c
        if c % 4 == 0:
            sh = np.concatenate([np.zeros((warm, D), f32), x[lo:lo + nt_real * TT]],
                                axis=0)
            flag = 0.0
        else:
            sh = np.ascontiguousarray(x[lo - warm:lo + nt_real * TT])
            flag = 1.0
        m = dict(common)
        m["xs"] = sh
        m["flags"] = np.tile(np.array([[flag, 1.0 - flag]], f32), (128, 1))
        in_maps.append(m)
    return in_maps


def kernel(**inputs):
    n_cores = 8
    nc = build_program()
    in_maps = _pack_inputs(inputs, nt_real=8, n_cores=n_cores)
    res = run_bass_kernel_spmd(nc, in_maps, core_ids=list(range(n_cores)))
    out = np.concatenate([res.results[c]["out"] for c in range(n_cores)], axis=0)
    return out.reshape(2, 16384, D)
